# version 01
# baseline (speedup 1.0000x reference)
import contextlib
import math
from collections import defaultdict

import numpy as np
import concourse.bass as bass
import concourse.mybir as mybir
from concourse.bass_utils import run_bass_kernel_spmd

F32 = mybir.dt.float32
BF16 = mybir.dt.bfloat16
U32 = mybir.dt.uint32
I16 = mybir.dt.int16
AF = mybir.ActivationFunctionType
ALU = mybir.AluOpType
AX = mybir.AxisListType

NCORE = 8
T = 2048
D = 1024
DC = D // 128
NT4 = T // 512
MAGIC = 12582912.0

ENGS = ("sync", "gpsimd", "tensor", "vector", "scalar")
DEBUG_L0F = False
L3A_LIMIT = 99
DEBUG_L3A = False
DEBUG_L1F = False
L1F_LIMIT = 99
L2A_LIMIT = 99


def v(x):
    return x if isinstance(x, bass.AP) else x[:]


def ap_view(t, dims, off=0):
    a = v(t)
    return bass.AP(tensor=a.tensor, offset=a.offset + off, ap=[list(d) for d in dims])


class KB:

    def __init__(self, name="prog"):
        self.name = name
        self.nc = bass.Bass("TRN2", target_bir_lowering=False)
        self.es = contextlib.ExitStack()
        self.ops = {e: [] for e in ENGS}
        self.sems = {}
        self.cnt = defaultdict(int)
        self.last_wait = {e: defaultdict(int) for e in ENGS}
        self.inputs = {}
        self.outputs = []

    def dram_in(self, name, arr):
        arr = np.ascontiguousarray(arr)
        self.inputs[name] = arr
        dt = {np.dtype(np.float32): F32, np.dtype(np.int16): I16,
              np.dtype(np.uint32): U32}[np.dtype(arr.dtype)]
        return self.nc.dram_tensor(name, list(arr.shape), dt, kind="ExternalInput").ap()

    def dram_out(self, name, shape, dtype=F32):
        self.outputs.append(name)
        return self.nc.dram_tensor(name, list(shape), dtype, kind="ExternalOutput").ap()

    def dram_tmp(self, name, shape, dtype):
        return self.nc.dram_tensor(name, list(shape), dtype, kind="Internal").ap()

    def sbuf(self, name, shape, dtype):
        return self.es.enter_context(self.nc.sbuf_tensor(name, list(shape), dtype))

    def psum(self, name, shape, dtype=F32):
        return self.es.enter_context(self.nc.psum_tensor(name, list(shape), dtype))

    def sem(self, name):
        if name not in self.sems:
            self.sems[name] = self.es.enter_context(self.nc.semaphore(name))
        return self.sems[name]

    def emit(self, eng, fn, waits=(), sig=None, amt=1):
        if eng == "vector":
            if sig is None:
                sig = "ve"
            prev = self.cnt.get("ve", 0)
            waits = tuple(w for w in (waits if isinstance(waits, (list, tuple)) else (waits,))) + ((("ve", prev),) if prev else ())
        elif eng == "scalar":
            if sig is None:
                sig = "se"
            prev = self.cnt.get("se", 0)
            waits = tuple(w for w in (waits if isinstance(waits, (list, tuple)) else (waits,))) + ((("se", prev),) if prev else ())
        wl = []
        for tk in (waits if isinstance(waits, (list, tuple)) else (waits,)):
            if tk is None:
                continue
            s, val = tk
            if s.startswith("q_"):
                val = self.cnt[s]
            if val > self.last_wait[eng][s]:
                self.last_wait[eng][s] = val
                wl.append((s, val))
        tok = None
        if sig is not None:
            self.sem(sig)
            self.cnt[sig] += amt
            tok = (sig, self.cnt[sig])
        for s, _ in wl:
            self.sem(s)

        def closure(engobj, wl=wl, fn=fn, sig=sig, amt=amt):
            for s, valu in wl:
                engobj.wait_ge(self.sems[s], valu)
            inst = fn(engobj)
            if sig is not None:
                inst.then_inc(self.sems[sig], amt)
        self.ops[eng].append(closure)
        return tok

    def dma(self, eng, out, in_, waits=(), sig=None, **kw):
        if sig is None:
            sig = "q_" + eng
        return self.emit(eng, lambda e: e.dma_start(out=out, in_=in_, **kw),
                         waits=waits, sig=sig, amt=16)

    def wait_all_dma(self):
        for qe in ("sync", "gpsimd", "scalar", "vector"):
            s = "q_" + qe
            if self.cnt.get(s):
                cntv = self.cnt[s]
                self.emit(qe, lambda e: e.nop(), waits=((s, cntv),))

    def finalize(self):
        block = self.es.enter_context(self.nc.Block())
        for eng in ENGS:
            lst = self.ops[eng]
            if not lst:
                continue

            def make(lst):
                def body(engobj):
                    for f in lst:
                        f(engobj)
                return body
            getattr(block, eng)(make(lst))
        self.es.close()
        return self.nc

    def run(self, per_core_inputs, core_ids=None, trace=False):
        nc = self.finalize()
        in_maps = []
        for ci in per_core_inputs:
            m = dict(self.inputs)
            m.update(ci)
            in_maps.append(m)
        res = run_bass_kernel_spmd(nc, in_maps,
                                   core_ids=core_ids or list(range(len(in_maps))),
                                   trace=trace)
        return res


class Common:

    def __init__(self, kb: KB, xn_dtype=BF16, need_apl=False, psum_plan="f8"):
        self.kb = kb
        self.X = kb.sbuf("X", [128, DC * T], F32)
        self.XN = kb.sbuf("XN", [128, DC * T], xn_dtype)
        self.xn_dtype = xn_dtype
        self.SQ = kb.sbuf("SQ", [128, T], BF16)
        self.BC1 = kb.sbuf("BC1", [128, T], F32)
        self.BC2 = kb.sbuf("BC2", [128, T], F32)
        self.APL = kb.sbuf("APL", [128, T], F32) if need_apl else None
        self.IDENT = kb.sbuf("IDENT", [128, 128], BF16)
        self.ONESC = kb.sbuf("ONESC", [128, 1], BF16)
        self.ONESF = kb.sbuf("ONESF", [128, 128], F32)
        ident = kb.dram_in("c_ident", np.eye(128, dtype=np.float32))
        onesall = kb.dram_in("c_onesall", np.ones((128, 128), np.float32))
        onescol = kb.dram_in("c_onescol", np.ones((128, 1), np.float32))
        kb.dma("gpsimd", v(self.IDENT), ident)
        kb.dma("gpsimd", v(self.ONESC), onescol)
        self.const_tok = kb.dma("sync", v(self.ONESF), onesall)
        self.const_tok_gp = ("q_gpsimd", kb.cnt["q_gpsimd"])
        self.psA = kb.psum("psA", [128, T], F32)
        if psum_plan == "mixed":
            self.psB = kb.psum("psB", [128, 512], F32)
            self.psT = kb.psum("psT", [128, 2048], BF16)
        else:
            self.psB = kb.psum("psB", [128, T], F32)
            self.psT = None

    def xc(self, c, t0=0, tn=T):
        return ap_view(self.X, [[DC * T, 128], [1, tn]], off=c * T + t0)

    def xnc(self, c, t0=0, tn=T):
        return ap_view(self.XN, [[DC * T, 128], [1, tn]], off=c * T + t0)

    def row(self, seg, t0=0, tn=T):
        t = self.BC1 if seg == 1 else self.BC2
        return ap_view(t, [[T, 1], [1, tn]], off=t0)


def emit_load_x(kb, cm, x_dram):
    src = x_dram.rearrange("(c p) t -> p c t", p=128)
    dst = ap_view(cm.X, [[DC * T, 128], [T, DC], [1, T]])
    return kb.dma("sync", dst, src)


def emit_store_x(kb, cm, x_out_dram, waits):
    src = ap_view(cm.X, [[DC * T, 128], [T, DC], [1, T]])
    dst = x_out_dram.rearrange("(c p) t -> p c t", p=128)
    return kb.dma("sync", dst, src, waits=waits)


def emit_ln(kb, cm, x_tok, out_fn=None, gamma_beta=None, chunk_hook=None, entry=None):
    out_fn = out_fn or cm.xnc
    psA = cm.psA

    cast_toks = []
    for c in range(DC):
        tk = kb.emit("vector", lambda e, c=c: e.tensor_copy(cm.xnc(c), cm.xc(c)),
                     waits=(x_tok, cm.const_tok_gp), sig="ve")
        cast_toks.append(tk)
    last_mm = None
    for c in range(DC):
        for g in range(4):
            last_mm = kb.emit("tensor",
                              lambda e, c=c, g=g: e.matmul(
                                  ap_view(psA, [[T, 1], [1, 512]], off=g * 512),
                                  v(cm.ONESC), cm.xnc(c, g * 512, 512),
                                  start=(c == 0), stop=(c == DC - 1)),
                              waits=(cast_toks[c], entry) if (c == 0 and g == 0)
                              else (cast_toks[c],),
                              sig="te" if (c == DC - 1 and g == 3) else None)
    mean_r = cm.row(0)
    var_r = cm.row(1)
    psA_row = ap_view(psA, [[T, 1], [1, T]])
    mtk = kb.emit("vector", lambda e: e.tensor_scalar_mul(mean_r, psA_row, 1.0 / D),
                  waits=(last_mm,), sig="ve")
    sq_done = None
    for c in range(DC):
        sq_tok = kb.emit("scalar",
                         lambda e, c=c: e.activation(v(cm.SQ), cm.xnc(c), AF.Square),
                         waits=(cast_toks[c], sq_done), sig="se")
        for g in range(4):
            last_mm = kb.emit("tensor",
                              lambda e, c=c, g=g: e.matmul(
                                  ap_view(psA, [[T, 1], [1, 512]], off=g * 512),
                                  v(cm.ONESC),
                                  ap_view(cm.SQ, [[T, 128], [1, 512]], off=g * 512),
                                  start=(c == 0), stop=(c == DC - 1)),
                              waits=(sq_tok, mtk),
                              sig="te" if g == 3 else None)
        sq_done = last_mm
    vtk = kb.emit("vector", lambda e: e.tensor_scalar(var_r, psA_row, 1.0 / D, 1e-5,
                                                      ALU.mult, ALU.add),
                  waits=(last_mm,), sig="ve")
    sq2 = kb.emit("scalar", lambda e: e.activation(psA_row, mean_r, AF.Square),
                  waits=(mtk, vtk), sig="se")
    tk = kb.emit("vector", lambda e: e.tensor_sub(var_r, var_r, psA_row), waits=(sq2,),
                 sig="ve")
    tk = kb.emit("scalar", lambda e: e.activation(var_r, var_r, AF.Sqrt),
                 waits=(tk,), sig="se")
    tk = kb.emit("vector", lambda e: e.reciprocal(var_r, var_r), waits=(tk,), sig="ve")
    mr_tok = kb.emit("vector", lambda e: e.tensor_mul(mean_r, mean_r, var_r), sig="ve")
    for g in range(4):
        kb.emit("tensor", lambda e, g=g: e.matmul(
            ap_view(psA, [[T, 128], [1, 512]], off=g * 512),
            v(cm.ONESF)[0:1, :], cm.row(1, g * 512, 512), start=True, stop=True),
            waits=(tk, mr_tok, cm.const_tok), sig="te" if g == 3 else None)
    b1tok = ("te", kb.cnt["te"])
    e1 = kb.emit("scalar", lambda e: e.copy(v(cm.BC1), v(psA)), waits=(b1tok,), sig="se")
    for g in range(4):
        kb.emit("tensor", lambda e, g=g: e.matmul(
            ap_view(psA, [[T, 128], [1, 512]], off=g * 512),
            v(cm.ONESF)[0:1, :], cm.row(0, g * 512, 512), start=True, stop=True),
            waits=(e1,), sig="te" if g == 3 else None)
    b2tok = ("te", kb.cnt["te"])
    e2 = kb.emit("scalar", lambda e: e.copy(v(cm.BC2), v(psA)), waits=(b2tok,), sig="se")
    toks = []
    for c in range(DC):
        tmp = v(cm.APL) if cm.APL is not None else out_fn(c)
        kb.emit("vector", lambda e, c=c, tmp=tmp: e.tensor_mul(tmp, cm.xc(c), v(cm.BC1)),
                waits=(e1, e2))
        if gamma_beta is None:
            tk = kb.emit("vector", lambda e, c=c, tmp=tmp: e.tensor_sub(out_fn(c), tmp,
                                                                        v(cm.BC2)), sig="ve")
        else:
            ga, be = gamma_beta
            kb.emit("vector", lambda e, c=c, tmp=tmp: e.tensor_sub(tmp, tmp, v(cm.BC2)))
            tk = kb.emit("vector", lambda e, c=c, tmp=tmp, ga=ga, be=be: e.tensor_scalar(
                out_fn(c), tmp,
                ap_view(ga, [[v(ga).ap[0][0], 128], [1, 1]], off=c),
                ap_view(be, [[v(be).ap[0][0], 128], [1, 1]], off=c),
                ALU.mult, ALU.add), sig="ve")
        toks.append(tk)
        if chunk_hook is not None:
            chunk_hook(c, tk)
    return toks, e2, None



def build_l0a(weights):
    kb = KB("l0a")
    cm = Common(kb, need_apl=True)
    x_in = kb.nc.dram_tensor("x_in", [D, T], F32, kind="ExternalInput").ap()
    x_out = kb.dram_out("x_out", [D, T], F32)
    A = kb.dram_in("w_A", weights["L0_attn_A"].reshape(DC, 128).T)
    BC = kb.dram_in("w_BC", weights["L0_attn_BC"])

    XNF = kb.sbuf("XNF", [128, T], F32)
    HS = kb.sbuf("HS", [128, DC * T], BF16)
    A_sb = kb.sbuf("A_sb", [128, DC], F32)
    ABC = kb.sbuf("ABC", [128, T], F32)
    WB = [kb.sbuf("WB0", [128, DC * 128], BF16), kb.sbuf("WB1", [128, DC * 128], BF16)]

    x_tok = emit_load_x(kb, cm, x_in)
    a_tok = kb.dma("sync", v(A_sb), A)

    def xnf(c, t0=0, tn=T):
        return ap_view(XNF, [[T, 128], [1, tn]], off=t0)

    def hs(c, t0=0, tn=T):
        return ap_view(HS, [[DC * T, 128], [1, tn]], off=c * T + t0)

    scan_toks = []

    def scan_hook(c, ln_tok):
        kb.emit("vector", lambda e, c=c: e.tensor_copy(
            v(ABC), ap_view(A_sb, [[DC, 128], [0, T]], off=c)),
            waits=(a_tok,))
        tk = kb.emit("vector", lambda e, c=c: e.tensor_tensor_scan(
            hs(c), v(ABC), xnf(c), 0.0, ALU.mult, ALU.add), sig="ve")
        scan_toks.append(tk)

    ln_toks, psA_free0, psB_free0 = emit_ln(kb, cm, x_tok, out_fn=xnf,
                                            chunk_hook=scan_hook)

    wtok = {}
    buf_free = {0: None, 1: None}
    ps_free = {0: psA_free0, 1: psA_free0}
    res_tok = None
    for f in range(DC):
        buf = WB[f % 2]
        src = ap_view(BC, [[D, 128], [128 * D, DC], [1, 128]], off=f * 128)
        wtok[f] = kb.dma("gpsimd", ap_view(buf, [[DC * 128, 128], [128, DC], [1, 128]]),
                         src, waits=(buf_free[f % 2],))
        mm_tok = None
        for ts in range(NT4):
            sl = ts % 2
            for dc in range(DC):
                mm_tok = kb.emit("tensor", lambda e, ts=ts, dc=dc, buf=buf, sl=sl: e.matmul(
                    ap_view(cm.psA, [[T, 128], [1, 512]], off=sl * 1024),
                    ap_view(buf, [[DC * 128, 128], [1, 128]], off=dc * 128),
                    hs(dc, ts * 512, 512),
                    start=(dc == 0), stop=(dc == DC - 1)),
                    waits=(wtok[f], scan_toks[dc], ps_free[sl]),
                    sig="te" if dc == DC - 1 else None)
            res_tok = kb.emit("vector", lambda e, f=f, ts=ts, sl=sl: e.tensor_add(
                cm.xc(f, ts * 512, 512), cm.xc(f, ts * 512, 512),
                ap_view(cm.psA, [[T, 128], [1, 512]], off=sl * 1024)),
                waits=(mm_tok,), sig="ve")
            ps_free[sl] = res_tok
        buf_free[f % 2] = mm_tok

    st = emit_store_x(kb, cm, x_out, waits=(res_tok,))
    kb.wait_all_dma()
    return kb



def make_butterfly_arrays(w):
    Slow = np.zeros((7, 128, 32 * 128), np.float32)
    p = np.arange(128)
    for j, (sig, tr) in enumerate([(6, True), (5, False), (4, False), (3, False),
                                   (2, False), (1, False), (0, False)]):
        s2 = 1 << sig
        for c in range(32):
            f = c * 128 + p
            k = (f >> (sig + 1)) * s2 + (f & (s2 - 1))
            b = (f >> sig) & 1
            S = np.zeros((128, 128), np.float32)
            wd = np.where(b == 0, w[sig, k, 0, 0], w[sig, k, 1, 1])
            wo = np.where(b == 0, w[sig, k, 0, 1], w[sig, k, 1, 0])
            S[p, p] = wd
            S[p, p + np.where(b == 0, s2, -s2)] = wo
            if tr:
                S = np.ascontiguousarray(S.T)
            Slow[j, :, c * 128:(c + 1) * 128] = S
    HD = np.zeros((5, 128, 32 * 128), np.float32)
    HO = np.zeros((5, 128, 32 * 128), np.float32)
    for i, sig in enumerate(range(7, 12)):
        s2 = 1 << sig
        for c in range(32):
            f = c * 128 + p
            k = (f >> (sig + 1)) * s2 + (f & (s2 - 1))
            d = (c >> (sig - 7)) & 1
            HD[i, p, c * 128 + p] = w[sig, k, d, d]
            HO[i, p, c * 128 + p] = w[sig, k, d, 1 - d]
    return Slow, HD, HO


def _bf_build_M(kb, cm, Slow_dram, SS, M, psum, mtoks, free_tok):
    t0 = kb.dma("gpsimd", v(M), ap_view(Slow_dram, [[4096, 128], [1, 4096]]),
                waits=(free_tok,))
    prev = [t0]
    ev_tok = None
    for j in range(1, 7):
        stok = kb.dma("gpsimd", v(SS),
                      ap_view(Slow_dram, [[4096, 128], [1, 4096]], off=j * 128 * 4096),
                      waits=tuple(prev))
        for c in range(32):
            mm = kb.emit("tensor", lambda e, c=c: e.matmul(
                ap_view(psum, [[T, 128], [1, 128]], off=(c % 4) * 512),
                ap_view(SS, [[32 * 128, 128], [1, 128]], off=c * 128),
                ap_view(M, [[32 * 128, 128], [1, 128]], off=c * 128),
                start=True, stop=True),
                waits=(stok, (ev_tok if c >= 4 else None), mtoks), sig="te")
            ev_tok = kb.emit("vector", lambda e, c=c: e.tensor_copy(
                ap_view(M, [[32 * 128, 128], [1, 128]], off=c * 128),
                ap_view(psum, [[T, 128], [1, 128]], off=(c % 4) * 512)),
                waits=(mm,), sig="ve")
        prev = [ev_tok]
    return ev_tok


def build_l0f(weights):
    kb = KB("l0f")
    cm = Common(kb)
    x_in = kb.nc.dram_tensor("x_in", [D, T], F32, kind="ExternalInput").ap()
    x_out = kb.dram_out("x_out", [D, T], F32)
    Wexp = kb.dram_in("w_Wexp", weights["L0_ffn_W_exp"])
    Wcon = kb.dram_in("w_Wcon", weights["L0_ffn_W_con"])
    Slow1, HD1, HO1 = make_butterfly_arrays(np.asarray(weights["L0_ffn_up_weights"]))
    Slow2, HD2, HO2 = make_butterfly_arrays(np.asarray(weights["L0_ffn_down_weights"]))
    SL1 = kb.dram_in("w_sl1", Slow1)
    SL2 = kb.dram_in("w_sl2", Slow2)
    HD1d = kb.dram_in("w_hd1", HD1)
    HO1d = kb.dram_in("w_ho1", HO1)
    HD2d = kb.dram_in("w_hd2", HD2)
    HO2d = kb.dram_in("w_ho2", HO2)
    bexp = kb.dram_in("w_bexp", weights["L0_ffn_b_exp"].reshape(32, 128).T)
    upb = kb.dram_in("w_upb", weights["L0_ffn_up_bias"].reshape(32, 128).T)
    bcon = kb.dram_in("w_bcon", weights["L0_ffn_b_con"].reshape(8, 128).T)
    dnb = kb.dram_in("w_dnb", weights["L0_ffn_down_bias"].reshape(8, 128).T)

    H = kb.sbuf("H", [128, 32 * 512], BF16)
    M1 = kb.sbuf("M1", [128, 32 * 128], BF16)
    M2 = kb.sbuf("M2", [128, 32 * 128], BF16)
    SS = kb.sbuf("SS", [128, 32 * 128], BF16)
    DD = SS
    OO = kb.sbuf("OO", [128, 32 * 128], BF16)
    WB = [kb.sbuf("WB0", [128, DC * 128], BF16), kb.sbuf("WB1", [128, DC * 128], BF16)]
    BIA = kb.sbuf("BIA", [128, 32 * 4], F32)

    def hap(c, t0=0, tn=512):
        return ap_view(H, [[32 * 512, 128], [1, tn]], off=c * 512 + t0)

    x_tok = emit_load_x(kb, cm, x_in)
    b1 = kb.dma("sync", ap_view(BIA, [[128, 128], [1, 32]]), bexp)
    b2 = kb.dma("sync", ap_view(BIA, [[128, 128], [1, 32]], off=32), upb)
    b3 = kb.dma("sync", ap_view(BIA, [[128, 128], [1, 8]], off=64), bcon)
    b4 = kb.dma("sync", ap_view(BIA, [[128, 128], [1, 8]], off=72), dnb)
    bc_tok = kb.emit("vector", lambda e: e.tensor_add(
        ap_view(BIA, [[128, 128], [1, 8]], off=64),
        ap_view(BIA, [[128, 128], [1, 8]], off=64),
        ap_view(BIA, [[128, 128], [1, 8]], off=72)), waits=(b3, b4), sig="ve")

    ln_toks, psA_free, psB_free = emit_ln(kb, cm, x_tok)
    ln_last = ln_toks[-1]
    if DEBUG_L0F:
        dxn = kb.dram_out("dbg_xn", [128, DC * T], F32)
        ln_last = kb.dma("gpsimd", dxn, ap_view(cm.XN, [[DC * T, 128], [1, DC * T]]),
                         waits=(ln_last,), sig="ve")

    m1_tok = _bf_build_M(kb, cm, SL1, SS, M1, cm.psA, psA_free, None)
    m2_tok = _bf_build_M(kb, cm, SL2, SS, M2, cm.psA, m1_tok, None)

    wtok = {}
    buf_free = {0: None, 1: None}
    res_tok = None
    for tt in range(NT4):
        t0 = tt * 512
        h_toks = {}
        for f in range(32):
            buf = WB[f % 2]
            src = ap_view(Wexp, [[4096, 128], [128 * 4096, DC], [1, 128]], off=f * 128)
            wtok[f] = kb.dma("gpsimd", ap_view(buf, [[DC * 128, 128], [128, DC], [1, 128]]),
                             src, waits=(buf_free[f % 2],))
            mm = None
            for dc in range(DC):
                mm = kb.emit("tensor", lambda e, f=f, dc=dc, buf=buf, t0=t0: e.matmul(
                    ap_view(cm.psA, [[T, 128], [1, 512]], off=(f % 4) * 512),
                    ap_view(buf, [[DC * 128, 128], [1, 128]], off=dc * 128),
                    cm.xnc(dc, t0, 512),
                    start=(dc == 0), stop=(dc == DC - 1)),
                    waits=(wtok[f], ln_last, m2_tok,
                           h_toks.get(f - 4), psA_free),
                    sig="te" if dc == DC - 1 else None)
            buf_free[f % 2] = mm
            h_toks[f] = kb.emit("vector", lambda e, f=f: e.tensor_scalar_add(
                hap(f), ap_view(cm.psA, [[T, 128], [1, 512]], off=(f % 4) * 512),
                ap_view(BIA, [[128, 128], [1, 1]], off=f)),
                waits=(mm, b1), sig="ve")

        def butterfly(M, HDd, HOd, hin_tok):
            lows = {}
            for c in range(32):
                mm = kb.emit("tensor", lambda e, c=c: e.matmul(
                    ap_view(cm.psB, [[T, 128], [1, 512]], off=(c % 4) * 512),
                    ap_view(M, [[32 * 128, 128], [1, 128]], off=c * 128),
                    hap(c), start=True, stop=True),
                    waits=(hin_tok, lows.get(c - 4), psB_free), sig="te")
                lows[c] = kb.emit("vector", lambda e, c=c: e.tensor_copy(
                    hap(c), ap_view(cm.psB, [[T, 128], [1, 512]], off=(c % 4) * 512)),
                    waits=(mm,), sig="ve")
            last = lows[31]
            for sigma in range(7, 12):
                cd = 1 << (sigma - 7)
                si = sigma - 7
                dtoks = [
                    kb.dma("gpsimd", v(DD),
                           ap_view(HDd, [[4096, 128], [1, 4096]], off=si * 128 * 4096),
                           waits=(last,)),
                    kb.dma("gpsimd", v(OO),
                           ap_view(HOd, [[4096, 128], [1, 4096]], off=si * 128 * 4096),
                           waits=(last,))]
                pair_ev = {}
                done = []
                for c0 in range(0, 32, 2 * cd):
                    for cl in range(cd):
                        ca, cb = c0 + cl, c0 + cl + cd
                        for (cc, slot) in ((ca, 0), (cb, 1)):
                            mm = kb.emit("tensor", lambda e, cc=cc, ca=ca, cb=cb, slot=slot: (
                                e.matmul(ap_view(cm.psB, [[T, 128], [1, 512]], off=slot * 512),
                                         ap_view(DD, [[32 * 128, 128], [1, 128]], off=cc * 128),
                                         hap(cc), start=True, stop=False)),
                                waits=tuple(dtoks) + (last, pair_ev.get(slot),))
                            mm = kb.emit("tensor", lambda e, cc=cc, ca=ca, cb=cb, slot=slot: (
                                e.matmul(ap_view(cm.psB, [[T, 128], [1, 512]], off=slot * 512),
                                         ap_view(OO, [[32 * 128, 128], [1, 128]], off=cc * 128),
                                         hap(cb if cc == ca else ca), start=False, stop=True)),
                                sig="te")
                        for (cc, slot) in ((ca, 0), (cb, 1)):
                            tk = kb.emit("vector", lambda e, cc=cc, slot=slot: e.tensor_copy(
                                hap(cc), ap_view(cm.psB, [[T, 128], [1, 512]], off=slot * 512)),
                                waits=(mm,), sig="ve")
                            pair_ev[slot] = tk
                            done.append(tk)
                last = done[-1]
            return last

        if DEBUG_L0F and tt == 0:
            d0 = kb.dram_out("dbg_h0", [128, 32 * 512], F32)
            h_toks[31] = kb.dma("gpsimd", d0, ap_view(H, [[32 * 512, 128], [1, 32 * 512]]),
                                waits=(h_toks[31],), sig="ve")
        hb = butterfly(M1, HD1d, HO1d, h_toks[31])
        if DEBUG_L0F and tt == 0:
            d1 = kb.dram_out("dbg_h1", [128, 32 * 512], F32)
            hb = kb.dma("gpsimd", d1, ap_view(H, [[32 * 512, 128], [1, 32 * 512]]),
                        waits=(hb,), sig="ve")
        g_toks = []
        for c in range(32):
            tk = kb.emit("scalar", lambda e, c=c: e.activation(
                hap(c), hap(c), AF.Gelu,
                bias=ap_view(BIA, [[128, 128], [1, 1]], off=32 + c)),
                waits=(hb, b2), sig="se")
            g_toks.append(tk)
        if DEBUG_L0F and tt == 0:
            d2 = kb.dram_out("dbg_h2", [128, 32 * 512], F32)
            g_toks[-1] = kb.dma("gpsimd", d2, ap_view(H, [[32 * 512, 128], [1, 32 * 512]]),
                                waits=(g_toks[-1],), sig="ve")
        hb2 = butterfly(M2, HD2d, HO2d, g_toks[-1])
        if DEBUG_L0F and tt == 0:
            d3 = kb.dram_out("dbg_h3", [128, 32 * 512], F32)
            hb2 = kb.dma("gpsimd", d3, ap_view(H, [[32 * 512, 128], [1, 32 * 512]]),
                         waits=(hb2,), sig="ve")

        def conps(f):
            return (ap_view(cm.psA, [[T, 128], [1, 512]], off=f * 512) if f < 4
                    else ap_view(cm.psB, [[T, 128], [1, 512]], off=(f - 4) * 512))
        psA_free = h_toks[31]
        psB_free = hb2
        mm_last = None
        for c in range(32):
            buf = WB[c % 2]
            src = ap_view(Wcon, [[1024, 128], [1, 1024]], off=c * 128 * 1024)
            wtok[100 + c] = kb.dma("gpsimd",
                                   ap_view(buf, [[DC * 128, 128], [1, 1024]]),
                                   src, waits=(buf_free[c % 2],))
            for f in range(DC):
                mm_last = kb.emit("tensor", lambda e, f=f, c=c, buf=buf: e.matmul(
                    conps(f),
                    ap_view(buf, [[DC * 128, 128], [1, 128]], off=f * 128),
                    hap(c), start=(c == 0), stop=(c == 31)),
                    waits=(wtok[100 + c], hb2, psA_free, psB_free),
                    sig="te" if f == DC - 1 else None)
            buf_free[c % 2] = mm_last
        con_toks = {}
        for f in range(DC):
            con_toks[f] = kb.emit("vector", lambda e, f=f, t0=t0: e.scalar_tensor_tensor(
                cm.xc(f, t0, 512), conps(f),
                ap_view(BIA, [[128, 128], [1, 1]], off=64 + f),
                cm.xc(f, t0, 512), ALU.add, ALU.add),
                waits=(mm_last, bc_tok), sig="ve")
        res_tok = con_toks[DC - 1]
        psA_free = res_tok
        psB_free = res_tok

    st = emit_store_x(kb, cm, x_out, waits=(res_tok,))
    kb.wait_all_dma()
    return kb



def _l3a_perm():
    perm = []
    for g in range(4):
        for par in range(2):
            for hl in range(4):
                for i in range(32):
                    perm.append((4 * g + hl) * 64 + 2 * i + par)
    return np.array(perm)


def build_l3a(weights):
    kb = KB("l3a")
    cm = Common(kb)
    x_in = kb.nc.dram_tensor("x_in", [D, T], F32, kind="ExternalInput").ap()
    x_out = kb.dram_out("x_out", [D, T], F32)
    perm = _l3a_perm()
    Wq = kb.dram_in("w_Wq", np.asarray(weights["L3_attn_Wq"])[:, perm])
    Wk = kb.dram_in("w_Wk", np.asarray(weights["L3_attn_Wk"])[:, perm])
    Wv = kb.dram_in("w_Wv", weights["L3_attn_Wv"])
    W0 = kb.dram_in("w_W0", weights["L3_attn_W0"])
    bq = kb.dram_in("w_bq", np.asarray(weights["L3_attn_bq"])[perm].reshape(8, 128).T)
    bk = kb.dram_in("w_bk", np.asarray(weights["L3_attn_bk"])[perm].reshape(8, 128).T)
    bv = kb.dram_in("w_bv", np.asarray(weights["L3_attn_bv"]).reshape(1, 1024))
    b0 = kb.dram_in("w_b0", np.asarray(weights["L3_attn_b0"]).reshape(8, 128).T)
    invf = (1.0 / 10000.0 ** (np.arange(0, 64, 2, dtype=np.float64) / 64.0))
    ang = np.arange(T, dtype=np.float64)[None, :] * invf[:, None]
    cosT = np.cos(ang).astype(np.float32)
    sinT = np.sin(ang).astype(np.float32)
    Cd = kb.dram_in("c_cos", np.tile(cosT, (4, 1)))
    Sd = kb.dram_in("c_sin", np.tile(sinT, (4, 1)))

    VW = 16 * 65
    QR = kb.sbuf("QR", [128, 2 * T], BF16)
    KR = kb.sbuf("KR", [128, 2 * T], BF16)
    CT = kb.sbuf("CT", [128, T], BF16)
    ST = kb.sbuf("ST", [128, T], BF16)
    VT = kb.sbuf("VT", [128, 16 * VW], BF16)
    E4 = kb.sbuf("E4", [128, 2048], BF16)
    STG = kb.sbuf("STG", [128, 512], F32)
    OSTG = kb.sbuf("OSTG", [128, 2048], BF16)
    BVB = kb.sbuf("BVB", [128, 1024], BF16)
    BIA = kb.sbuf("BIA", [128, 3 * DC], F32)
    WB = [kb.sbuf("WB0", [128, DC * 128], BF16), kb.sbuf("WB1", [128, DC * 128], BF16)]
    if DEBUG_L3A:
        OTd = kb.dram_out("ot_scratch", [16, 64, T], BF16)
        dbg_qr = kb.dram_out("dbg_qr", [128, 2 * T], F32)
        dbg_kr = kb.dram_out("dbg_kr", [128, 2 * T], F32)
    else:
        OTd = kb.dram_tmp("ot_scratch", [16, 64, T], BF16)

    x_tok = emit_load_x(kb, cm, x_in)
    c1 = kb.dma("gpsimd", v(CT), Cd)
    c2 = kb.dma("gpsimd", v(ST), Sd)
    bq_t = kb.dma("sync", ap_view(BIA, [[3 * DC, 128], [1, DC]]), bq)
    bk_t = kb.dma("sync", ap_view(BIA, [[3 * DC, 128], [1, DC]], off=DC), bk)
    b0_t = kb.dma("sync", ap_view(BIA, [[3 * DC, 128], [1, DC]], off=2 * DC), b0)

    ln_toks, psA_free, psB_free = emit_ln(kb, cm, x_tok)
    ln_last = ln_toks[-1]
    bv_t = kb.dma("sync", ap_view(cm.BC1, [[T, 1], [1, 1024]]), bv,
                  waits=(ln_last, psA_free, psB_free))

    bb = []
    for g2 in range(2):
        bb.append(kb.emit("tensor", lambda e, g2=g2: e.matmul(
            ap_view(cm.psB, [[T, 128], [1, 512]], off=g2 * 512),
            v(cm.ONESF)[0:1, :], cm.row(1, g2 * 512, 512), start=True, stop=True),
            waits=(bv_t, psB_free, cm.const_tok), sig="te"))
    bvb_tok = kb.emit("scalar", lambda e: e.copy(v(BVB), ap_view(cm.psB, [[T, 128], [1, 1024]])),
                      waits=(bb[-1],), sig="se")

    on_tok = kb.emit("vector", lambda e: e.memset(
        ap_view(VT, [[16 * VW, 128], [VW, 16], [65, 16], [1, 1]], off=64), 1.0), sig="ve")
    on2_tok = on_tok
    wtok = {}
    buf_free = {0: None, 1: None}
    vt_toks = {}

    def vps(sl, nh):
        return (ap_view(cm.psA, [[T, 128], [1, 512]], off=sl * 1024 + nh * 512) if sl < 2
                else ap_view(cm.psB, [[T, 128], [1, 512]], off=(sl - 2) * 1024 + nh * 512))
    for tg in range(4):
        mm = None
        for c in range(DC):
            buf = WB[c % 2]
            wtok[(tg, c)] = kb.dma("gpsimd", ap_view(buf, [[DC * 128, 128], [1, 1024]]),
                                   ap_view(Wv, [[1024, 128], [1, 1024]], off=c * 128 * 1024),
                                   waits=(buf_free[c % 2],))
            for sl in range(4):
                tt = tg * 4 + sl
                for nh in range(2):
                    mm = kb.emit("tensor", lambda e, c=c, tt=tt, buf=buf, nh=nh, sl=sl: e.matmul(
                        vps(sl, nh),
                        ap_view(cm.XN, [[DC * T, 128], [1, 128]], off=c * T + tt * 128),
                        ap_view(buf, [[DC * 128, 128], [1, 512]], off=nh * 512),
                        start=(c == 0), stop=(c == DC - 1)),
                        waits=(wtok[(tg, c)], ln_last, psA_free, psB_free,
                               vt_toks.get(tg * 4 - 1)),
                        sig="te" if (nh == 1 and sl == 3) else None)
            buf_free[c % 2] = mm
        for sl in range(4):
            tt = tg * 4 + sl
            vt_toks[tt] = kb.emit("vector", lambda e, tt=tt, sl=sl: e.tensor_add(
                ap_view(VT, [[16 * VW, 128], [65, 16], [1, 64]], off=tt * VW),
                (ap_view(cm.psA, [[T, 128], [64, 16], [1, 64]], off=(sl % 2) * 1024) if sl < 2
                 else ap_view(cm.psB, [[T, 128], [64, 16], [1, 64]], off=(sl - 2) * 1024)),
                ap_view(BVB, [[1024, 128], [64, 16], [1, 64]])),
                waits=(mm, bvb_tok, on_tok, on2_tok), sig="ve")
    if DEBUG_L3A:
        dbg_vt = kb.dram_out("dbg_vt", [128, 2 * VW], F32)
        vt_toks[15] = kb.dma("gpsimd", dbg_vt, ap_view(VT, [[16 * VW, 128], [1, 2 * VW]]),
                             waits=(vt_toks[15],), sig="ve")
    vt_done = vt_toks[15]

    res_toks = {}
    stg_dma = {}
    last_norm = None
    group_guard = None
    if L3A_LIMIT < 2:
        st = emit_store_x(kb, cm, x_out, waits=(vt_done,))
        kb.wait_all_dma()
        return kb
    for g in range(4):
        def proj_rope(Wd, bia_off, OUT, name, entry_guard=None):
            toks = {}
            wts = []
            for par in range(2):
                col0 = g * 256 + par * 128
                buf = WB[par]
                wts.append(kb.dma("gpsimd",
                           ap_view(buf, [[DC * 128, 128], [128, DC], [1, 128]]),
                           ap_view(Wd, [[1024, 128], [128 * 1024, DC], [1, 128]], off=col0),
                           waits=(buf_free[par],)))
            rtoks = [entry_guard]
            for qs in range(NT4):
                for par in range(2):
                    buf = WB[par]
                    mm = None
                    for dc in range(DC):
                        mm = kb.emit("tensor", lambda e, dc=dc, qs=qs, buf=buf, par=par: e.matmul(
                            ap_view(cm.psB, [[T, 128], [1, 512]], off=par * 512),
                            ap_view(buf, [[DC * 128, 128], [1, 128]], off=dc * 128),
                            cm.xnc(dc, qs * 512, 512),
                            start=(dc == 0), stop=(dc == DC - 1)),
                            waits=(wts[par], ln_last, rtoks[-1], vt_done,
                                   group_guard),
                            sig="te" if dc == DC - 1 else None)
                    toks[(par, qs)] = mm
                pe = ap_view(cm.psB, [[T, 128], [1, 512]], off=0)
                po = ap_view(cm.psB, [[T, 128], [1, 512]], off=512)
                cslice = ap_view(CT, [[T, 128], [1, 512]], off=qs * 512)
                sslice = ap_view(ST, [[T, 128], [1, 512]], off=qs * 512)
                beven = ap_view(BIA, [[3 * DC, 128], [1, 1]], off=bia_off + 2 * g)
                bodd = ap_view(BIA, [[3 * DC, 128], [1, 1]], off=bia_off + 2 * g + 1)
                qoat = ap_view(OUT, [[2 * T, 128], [1, 512]], off=T + qs * 512)
                qeat = ap_view(OUT, [[2 * T, 128], [1, 512]], off=qs * 512)
                w0_ = (toks[(0, qs)], toks[(1, qs)], bq_t, bk_t, c1, c2)
                tb = ap_view(cm.BC2, [[T, 128], [1, 512]])
                ta = ap_view(cm.BC1, [[T, 128], [1, 512]])
                kb.emit("vector", lambda e, po=po, bodd=bodd, tb=tb: e.tensor_scalar_add(
                    tb, po, bodd), waits=w0_)
                kb.emit("vector", lambda e, pe=pe, beven=beven, cs=cslice, ta=ta:
                        e.scalar_tensor_tensor(ta, pe, beven, cs, ALU.add, ALU.mult))
                kb.emit("vector", lambda e, pe=pe, beven=beven, ss=sslice:
                        e.scalar_tensor_tensor(v(STG), pe, beven, ss, ALU.add, ALU.mult))
                kb.emit("vector", lambda e, cs=cslice, tb=tb: e.tensor_mul(
                    v(cm.SQ)[:, 0:512], tb, cs))
                kb.emit("vector", lambda e, qoat=qoat: e.tensor_add(
                    qoat, v(STG), v(cm.SQ)[:, 0:512]))
                kb.emit("vector", lambda e, ss=sslice, tb=tb: e.tensor_mul(
                    v(cm.SQ)[:, 0:512], tb, ss))
                tk = kb.emit("vector", lambda e, qeat=qeat, ta=ta: e.tensor_sub(
                    qeat, ta, v(cm.SQ)[:, 0:512]), sig="ve")
                rtoks.append(tk)
            return rtoks[-1]

        q_tok = proj_rope(Wq, 0, QR, "q")
        if DEBUG_L3A and g == 0:
            q_tok = kb.dma("gpsimd", dbg_qr, v(QR), waits=(q_tok,), sig="ve")
        if L3A_LIMIT >= 2.5:
            k_tok = proj_rope(Wk, DC, KR, "k", entry_guard=q_tok)
            if DEBUG_L3A and g == 0:
                k_tok = kb.dma("gpsimd", dbg_kr, v(KR), waits=(k_tok,), sig="ve")
        else:
            k_tok = q_tok
        buf_free[0] = ("te", kb.cnt["te"])
        buf_free[1] = ("te", kb.cnt["te"])
        if L3A_LIMIT < 3:
            group_guard = k_tok
            if L3A_LIMIT < 2.4:
                break
            continue

        for qs in range(NT4):
            av_done = {}
            ex_done = {}
            last_av = None
            for kvt in range(16):
                st_mms = []
                for hl in range(4):
                    b = hl * 32
                    for par in range(2):
                        st_mms.append(kb.emit("tensor",
                            lambda e, hl=hl, par=par, kvt=kvt, qs=qs, b=b: e.matmul(
                                ap_view(cm.psA, [[T, 128], [1, 512]], off=hl * 512),
                                ap_view(KR, [[2 * T, 32], [1, 128]],
                                        off=b * 2 * T + par * T + kvt * 128),
                                ap_view(QR, [[2 * T, 32], [1, 512]],
                                        off=b * 2 * T + par * T + qs * 512),
                                start=(par == 0), stop=(par == 1),
                                tile_position=(b, 0)),
                            waits=(q_tok, k_tok, ex_done.get(kvt - 1), last_norm),
                            sig="te" if (hl == 3 and par == 1) else None))
                ex_done[kvt] = kb.emit("scalar", lambda e: e.activation(
                    v(E4), v(cm.psA), AF.Exp, scale=0.125),
                    waits=(st_mms[-1], av_done.get(kvt - 1)), sig="se")
                if DEBUG_L3A and g == 0 and qs == 0 and kvt == 0:
                    dbg_e4 = kb.dram_out("dbg_e4", [128, 2048], F32)
                    ex_done[kvt] = kb.dma("gpsimd", dbg_e4, v(E4),
                                          waits=(ex_done[kvt],), sig="se")
                for hl in range(4):
                    h = 4 * g + hl
                    last_av = kb.emit("tensor", lambda e, hl=hl, h=h, kvt=kvt: e.matmul(
                        ap_view(cm.psB, [[T, 65], [1, 512]], off=hl * 512),
                        ap_view(VT, [[16 * VW, 128], [1, 65]],
                                off=kvt * VW + h * 65),
                        ap_view(E4, [[2048, 128], [1, 512]], off=hl * 512),
                        start=(kvt == 0), stop=(kvt == 15)),
                        waits=(vt_done, ex_done[kvt],
                               res_toks.get((g, hl, qs - 1)),
                               stg_dma.get((g - 1, hl))),
                        sig="te" if hl == 3 else None)
                av_done[kvt] = last_av
            if DEBUG_L3A and g == 0 and qs == 0:
                dbg_av = kb.dram_out("dbg_av", [128, 2048], F32)
                cpav = kb.emit("vector", lambda e: e.tensor_copy(
                    v(OSTG), ap_view(cm.psB, [[T, 128], [1, 2048]])),
                    waits=(last_av,), sig="ve")
                last_av = kb.dma("gpsimd", dbg_av, v(OSTG), waits=(cpav,), sig="se")
            prev_bc = None
            for hl in range(4):
                cp = kb.emit("vector", lambda e, hl=hl: e.tensor_copy(
                    v(STG)[0:96, :], ap_view(cm.psB, [[T, 96], [1, 512]], off=hl * 512)),
                    waits=(last_av, stg_dma.get((g, qs - 1)), prev_bc), sig="ve")
                rc = kb.emit("vector", lambda e: e.reciprocal(
                    v(STG)[64:65, :], v(STG)[64:65, :]), sig="ve")
                bc = kb.emit("tensor", lambda e: e.matmul(
                    ap_view(cm.psA, [[T, 64], [1, 512]], off=0),
                    v(cm.ONESF)[64:65, 0:64], v(STG)[64:65, :], start=True, stop=True),
                    waits=(rc, ex_done[15], res_toks.get((g, hl - 1, qs))), sig="te")
                prev_bc = bc
                res_toks[(g, hl, qs)] = kb.emit("vector",
                    lambda e, hl=hl: e.tensor_mul(
                        ap_view(OSTG, [[2048, 64], [1, 512]], off=hl * 512),
                        v(STG)[0:64, :],
                        ap_view(cm.psA, [[T, 64], [1, 512]])),
                    waits=(bc,), sig="ve")
            last_norm = res_toks[(g, 3, qs)]
            group_guard = last_norm
            stg_dma[(g, qs)] = kb.dma(
                "sync",
                ap_view(OTd, [[T, 64], [64 * T, 4], [1, 512]],
                        off=(4 * g) * 64 * T + qs * 512),
                ap_view(OSTG, [[2048, 64], [512, 4], [1, 512]]),
                waits=(res_toks[(g, 3, qs)],))

    if L3A_LIMIT < 4:
        st = emit_store_x(kb, cm, x_out, waits=(group_guard if L3A_LIMIT < 3 else last_norm,))
        kb.wait_all_dma()
        return kb
    final_dma = ("q_sync", kb.cnt["q_sync"])
    te_guard = ("te", kb.cnt["te"])
    ve_guard = ("ve", kb.cnt["ve"])
    otb = {}
    res2 = {}
    for ts in range(NT4):
        for oc in range(DC):
            n = ts * DC + oc
            buf = WB[oc % 2]
            wt = kb.dma("gpsimd", ap_view(buf, [[DC * 128, 128], [1, 1024]]),
                        ap_view(W0, [[1024, 128], [1, 1024]], off=oc * 128 * 1024),
                        waits=(buf_free[oc % 2],))
            ot_t = kb.dma("sync",
                          ap_view(E4, [[2048, 128], [1, 512]], off=(oc % 4) * 512),
                          ap_view(OTd, [[T, 128], [1, 512]],
                                  off=(2 * oc) * 64 * T + ts * 512),
                          waits=(final_dma, te_guard, otb.get(n - 4),))
            mmf = None
            for f in range(DC):
                mmf = kb.emit("tensor", lambda e, f=f, oc=oc, ts=ts, buf=buf: e.matmul(
                    (ap_view(cm.psA, [[T, 128], [1, 512]], off=f * 512) if f < 4 else
                     ap_view(cm.psB, [[T, 128], [1, 512]], off=(f - 4) * 512)),
                    ap_view(buf, [[DC * 128, 128], [1, 128]], off=f * 128),
                    ap_view(E4, [[2048, 128], [1, 512]], off=(oc % 4) * 512),
                    start=(oc == 0), stop=(oc == DC - 1)),
                    waits=(wt, ot_t, res2.get((ts - 1, DC - 1)), ve_guard),
                    sig="te" if f == DC - 1 else None)
            buf_free[oc % 2] = mmf
            otb[n] = mmf
        for f in range(DC):
            res2[(ts, f)] = kb.emit("vector", lambda e, f=f, ts=ts: e.scalar_tensor_tensor(
                cm.xc(f, ts * 512, 512),
                (ap_view(cm.psA, [[T, 128], [1, 512]], off=f * 512) if f < 4 else
                 ap_view(cm.psB, [[T, 128], [1, 512]], off=(f - 4) * 512)),
                ap_view(BIA, [[3 * DC, 128], [1, 1]], off=2 * DC + f),
                cm.xc(f, ts * 512, 512), ALU.add, ALU.add),
                waits=(mmf, b0_t), sig="ve")

    st = emit_store_x(kb, cm, x_out, waits=(res2[(NT4 - 1, DC - 1)],))
    kb.wait_all_dma()
    return kb



def _emit_quant_tokenmajor(kb, cm, XT, XD, SCL, Q1, Q2, wait, n=1024, dbg=False):
    nb = n // 32
    sw = v(SCL).ap[0][0]
    seg = lambda i: ap_view(SCL, [[sw, 128], [1, nb]], off=i * nb)
    segb = lambda i: ap_view(SCL, [[sw, 128], [1, nb], [0, 32]], off=i * nb)
    scl = seg(2)
    sclb = segb(2)
    rsc = seg(0)
    rscb = segb(0)
    q1 = ap_view(Q1, [[v(Q1).ap[0][0], 128], [1, n]])
    q13 = ap_view(Q1, [[v(Q1).ap[0][0], 128], [32, nb], [1, 32]])
    q2 = ap_view(Q2, [[v(Q2).ap[0][0], 128], [1, n]])
    xt = ap_view(XT, [[v(XT).ap[0][0], 128], [1, n]])
    xt3 = ap_view(XT, [[v(XT).ap[0][0], 128], [32, nb], [1, 32]])
    xd = ap_view(XD, [[v(XD).ap[0][0], 128], [1, n]])
    xd3 = ap_view(XD, [[v(XD).ap[0][0], 128], [32, nb], [1, 32]])
    rtk = kb.emit("vector", lambda e: e.tensor_reduce(seg(0), xt3, AX.X, ALU.max,
                                                      apply_absolute_value=True),
                  waits=wait, sig="ve")
    kb.emit("vector", lambda e: e.tensor_scalar(seg(2), seg(0), 1.0 / 7.5, 1e-30,
                                                ALU.mult, ALU.add), waits=(rtk,))
    rtk2 = kb.emit("vector", lambda e: e.reciprocal(rsc, scl), sig="ve")
    ytok = kb.emit("vector", lambda e: e.tensor_mul(q13, xt3, rscb), waits=(rtk2,), sig="ve")
    kb.emit("scalar", lambda e: e.activation(q2, q1, AF.Abs), waits=(ytok,), sig="se")
    kb.emit("vector", lambda e: e.tensor_single_scalar(v(cm.SQ)[:, 0:n], q2, 4.0, ALU.is_ge),
            waits=(("se", kb.cnt["se"]),))
    kb.emit("vector", lambda e: e.tensor_single_scalar(q2, q2, 2.0, ALU.is_ge))
    kb.emit("vector", lambda e: e.tensor_scalar(q2, q2, -4.0, 8.0, ALU.mult, ALU.add))
    kb.emit("vector", lambda e: e.scalar_tensor_tensor(q2, v(cm.SQ)[:, 0:n], -2.0, q2,
                                                       ALU.mult, ALU.add))
    if dbg:
        vetok = ("ve", kb.cnt["ve"])
        kb.emit("vector", lambda e: e.engine_nop(), sig="ve")
        d1 = kb.dram_out("dbg_y", [128, n], F32)
        kb.dma("gpsimd", d1, q1, waits=(vetok,))
        d2 = kb.dram_out("dbg_inv", [128, n], F32)
        kb.dma("gpsimd", d2, q2, waits=(vetok,))
        d3 = kb.dram_out("dbg_scl", [128, 32], F32)
        dmat = kb.dma("gpsimd", d3, scl, waits=(vetok,))
        kb.last_dbg = dmat
    w_ = (kb.last_dbg,) if dbg else ()
    kb.emit("vector", lambda e: e.tensor_mul(q1, q1, q2), waits=w_)
    kb.emit("vector", lambda e: e.tensor_single_scalar(q1, q1, MAGIC, ALU.add))
    kb.emit("vector", lambda e: e.tensor_single_scalar(q1, q1, MAGIC, ALU.subtract))
    rtk3 = kb.emit("vector", lambda e: e.reciprocal(q2, q2), sig="ve")
    kb.emit("vector", lambda e: e.tensor_mul(q1, q1, q2), waits=(rtk3,))
    return kb.emit("vector", lambda e: e.tensor_mul(xd3, q13, sclb), sig="ve")


def build_l1f(weights):
    kb = KB("l1f")
    cm = Common(kb, psum_plan="mixed")
    x_in = kb.nc.dram_tensor("x_in", [D, T], F32, kind="ExternalInput").ap()
    x_out = kb.dram_out("x_out", [D, T], F32)
    Wt = kb.dram_in("w_w", weights["L1_ffn_weight"])
    bias = kb.dram_in("w_b", np.asarray(weights["L1_ffn_bias"]).reshape(8, 128).T)

    XT = kb.sbuf("XT", [128, 1024], BF16)
    XD = kb.sbuf("XDt", [128, 1024], BF16)
    SCL = kb.sbuf("SCL", [128, 96], F32)
    Q1 = kb.sbuf("Q1", [128, 1024], F32)
    Q2 = kb.sbuf("Q2", [128, 1024], F32)
    WFP = kb.sbuf("WFP", [128, 1024], BF16)
    WDT = kb.sbuf("WDT", [128, DC * 1024], BF16)
    BIA = kb.sbuf("BIA", [128, DC], F32)

    x_tok = emit_load_x(kb, cm, x_in)
    b_tok = kb.dma("sync", v(BIA), bias)
    ln_toks, psA_free, _ = emit_ln(kb, cm, x_tok)
    ln_last = ln_toks[-1]

    wq_toks = {}
    prev_q = None
    tp_ev = {}
    for c in range(DC):
        wl = kb.dma("gpsimd", v(WFP),
                    ap_view(Wt, [[1024, 128], [1, 1024]], off=c * 128 * 1024),
                    waits=(prev_q,))
        qt = _emit_quant_tokenmajor(kb, cm, WFP, XD, SCL, Q1, Q2, (wl, ln_last),
                                    dbg=(DEBUG_L1F and c == 0))
        if DEBUG_L1F and c == 0:
            dq = kb.dram_out("dbg_wq", [128, 1024], F32)
            qt = kb.dma("gpsimd", dq, v(XD), waits=(qt,), sig="ve")
            dq2 = kb.dram_out("dbg_wraw", [128, 1024], F32)
            qt = kb.dma("gpsimd", dq2, v(WFP), waits=(qt,), sig="ve")
        prev_q = qt
        for b in range(DC):
            mmt = kb.emit("tensor", lambda e, c=c, b=b: e.transpose(
                ap_view(cm.psT, [[2048, 128], [1, 128]], off=(b % 2) * 1024),
                ap_view(XD, [[1024, 128], [1, 128]], off=b * 128),
                v(cm.IDENT)),
                waits=(qt, tp_ev.get((c, b - 2)), tp_ev.get((c - 1, b))), sig="te")
            tp_ev[(c, b)] = kb.emit("vector", lambda e, c=c, b=b: e.tensor_copy(
                ap_view(WDT, [[DC * 1024, 128], [1, 128]], off=b * 1024 + c * 128),
                ap_view(cm.psT, [[2048, 128], [1, 128]], off=(b % 2) * 1024)),
                waits=(mmt,), sig="ve")
    wq_done = tp_ev[(DC - 1, DC - 1)]

    if L1F_LIMIT < 2:
        st = emit_store_x(kb, cm, x_out, waits=(wq_done,))
        kb.wait_all_dma()
        return kb
    xq_done = {}
    ev = {}
    nseq = 0
    for tt in range(16):
        prev = xq_done.get(tt - 1, wq_done)
        for b in range(DC):
            mmt = kb.emit("tensor", lambda e, tt=tt, b=b, nseq=nseq: e.transpose(
                ap_view(cm.psT, [[2048, 128], [1, 128]], off=(nseq % 2) * 1024),
                ap_view(cm.XN, [[DC * T, 128], [1, 128]], off=b * T + tt * 128),
                v(cm.IDENT)),
                waits=(ln_last, prev, ev.get(nseq - 2)), sig="te")
            ev[nseq] = kb.emit("vector", lambda e, b=b, nseq=nseq: e.tensor_copy(
                ap_view(XT, [[1024, 128], [1, 128]], off=b * 128),
                ap_view(cm.psT, [[2048, 128], [1, 128]], off=(nseq % 2) * 1024)),
                waits=(mmt,), sig="ve")
            nseq += 1
        if L1F_LIMIT < 2.5:
            xq_done[tt] = ev[nseq - 1]
            if L1F_LIMIT < 2.25 and tt == 0:
                break
            continue
        qt = _emit_quant_tokenmajor(kb, cm, XT, XD, SCL, Q1, Q2,
                                    (("ve", kb.cnt["ve"]),))
        if L1F_LIMIT < 2.8:
            xq_done[tt] = qt
            continue
        last = None
        for b in range(DC):
            mmt = kb.emit("tensor", lambda e, tt=tt, b=b, nseq=nseq: e.transpose(
                ap_view(cm.psT, [[2048, 128], [1, 128]], off=(nseq % 2) * 1024),
                ap_view(XD, [[1024, 128], [1, 128]], off=b * 128),
                v(cm.IDENT)),
                waits=(qt, ev.get(nseq - 2)), sig="te")
            last = kb.emit("vector", lambda e, tt=tt, b=b, nseq=nseq: e.tensor_copy(
                ap_view(cm.XN, [[DC * T, 128], [1, 128]], off=b * T + tt * 128),
                ap_view(cm.psT, [[2048, 128], [1, 128]], off=(nseq % 2) * 1024)),
                waits=(mmt,), sig="ve")
            ev[nseq] = last
            nseq += 1
        xq_done[tt] = last
    xq_all = xq_done[max(xq_done)]

    if L1F_LIMIT < 3:
        st = emit_store_x(kb, cm, x_out, waits=(xq_all,))
        kb.wait_all_dma()
        return kb
    res_tok = None
    con = {}
    for ts in range(NT4):
        for f in range(DC):
            mm = None
            for c in range(DC):
                mm = kb.emit("tensor", lambda e, f=f, c=c, ts=ts: e.matmul(
                    ap_view(cm.psA, [[T, 128], [1, 512]], off=(f % 4) * 512),
                    ap_view(WDT, [[DC * 1024, 128], [1, 128]], off=c * 1024 + f * 128),
                    cm.xnc(c, ts * 512, 512),
                    start=(c == 0), stop=(c == DC - 1)),
                    waits=(xq_all, psA_free, con.get((ts, f - 4)), con.get((ts - 1, f + 4)),),
                    sig="te" if c == DC - 1 else None)
            con[(ts, f)] = kb.emit("vector", lambda e, f=f, ts=ts: e.scalar_tensor_tensor(
                cm.xc(f, ts * 512, 512),
                ap_view(cm.psA, [[T, 128], [1, 512]], off=(f % 4) * 512),
                ap_view(BIA, [[DC, 128], [1, 1]], off=f),
                cm.xc(f, ts * 512, 512), ALU.add, ALU.add),
                waits=(mm, b_tok), sig="ve")
            res_tok = con[(ts, f)]

    st = emit_store_x(kb, cm, x_out, waits=(res_tok,))
    kb.wait_all_dma()
    return kb



def _l1a_tile_runs(s):
    runs = []
    p = 128 * s
    end = 128 * (s + 1)
    while p < end:
        b = p // 46
        b_end = min(end, 46 * (b + 1), 2048)
        ln = b_end - p
        if b % 2 == 0:
            runs.append((p, p - 128 * s, ln, 1))
        else:
            runs.append((92 * b + 45 - p - (ln - 1), p - 128 * s, ln, -1))
        p = b_end
    return runs


def build_l1a(weights):
    kb = KB("l1a")
    cm = Common(kb)
    x_in = kb.nc.dram_tensor("x_in", [D, T], F32, kind="ExternalInput").ap()
    x_out = kb.dram_out("x_out", [D, T], F32)
    Wqkv = kb.dram_in("w_qkv", np.asarray(weights["L1_attn_W_qkv"]))
    Wout = kb.dram_in("w_out", np.asarray(weights["L1_attn_W_out"]))
    VONE = kb.dram_in("c_vones", np.ones((64, 16 * 65), np.float32))

    WQb = kb.sbuf("WQb", [128, DC * 1024], BF16)
    WKb = kb.sbuf("WKb", [128, DC * 1024], BF16)
    WVb = kb.sbuf("WVb", [128, DC * 1024], BF16)
    WOb = kb.sbuf("WOb", [128, DC * 1024], BF16)
    QS = kb.sbuf("QS", [128, 1024], BF16)
    KS = kb.sbuf("KS", [128, 512], BF16)
    VW = kb.sbuf("VW", [64, 16 * 65], BF16)
    EST = kb.sbuf("EST", [64, 128], BF16)
    STG = kb.sbuf("STG", [65, 128], F32)
    OT1 = kb.sbuf("OT1", [64, 128], BF16)
    OS = kb.sbuf("OS", [128, DC * 128], BF16)

    x_tok = emit_load_x(kb, cm, x_in)
    wq_t = kb.dma("gpsimd", ap_view(WQb, [[DC * 1024, 128], [1024, DC], [1, 1024]]),
                  ap_view(Wqkv, [[3072, 128], [128 * 3072, DC], [1, 1024]]))
    wk_t = kb.dma("gpsimd", ap_view(WKb, [[DC * 1024, 128], [1024, DC], [1, 1024]]),
                  ap_view(Wqkv, [[3072, 128], [128 * 3072, DC], [1, 1024]], off=1024))
    wv_t = kb.dma("gpsimd", ap_view(WVb, [[DC * 1024, 128], [1024, DC], [1, 1024]]),
                  ap_view(Wqkv, [[3072, 128], [128 * 3072, DC], [1, 1024]], off=2048))
    wo_t = kb.dma("gpsimd", ap_view(WOb, [[DC * 1024, 128], [1024, DC], [1, 1024]]),
                  Wout.rearrange("(c p) n -> p c n", p=128))
    vo_t = kb.dma("gpsimd", v(VW), VONE)

    ln_toks, psA_free, _ = emit_ln(kb, cm, x_tok)
    ln_last = ln_toks[-1]
    perm_tok = None
    for c in range(DC):
        t1 = kb.emit("vector", lambda e, c=c: e.tensor_copy(
            ap_view(cm.SQ, [[T, 128], [46, 22], [1, 46]]),
            ap_view(cm.XN, [[DC * T, 128], [92, 22], [-1, 46]], off=c * T + 91)),
            waits=(ln_last,), sig="ve")
        perm_tok = kb.emit("vector", lambda e, c=c: e.tensor_copy(
            ap_view(cm.XN, [[DC * T, 128], [92, 22], [1, 46]], off=c * T + 46),
            ap_view(cm.SQ, [[T, 128], [46, 22], [1, 46]])),
            waits=(t1,), sig="ve")

    st_tile = {}
    av_tile = {}
    vw_evs = {}
    xev_last = {-1: (perm_tok, psA_free)}
    ex_tok = {}
    av_seq = {}
    cp_tok = {}
    bc_tok = {}
    mul_tok = {}
    osd_tok = {}
    op_last = {}
    qs_ev = {}
    for s in range(16):
        mm = None
        for oc in range(DC):
            for c in range(DC):
                mm = kb.emit("tensor", lambda e, oc=oc, c=c, s=s: e.matmul(
                    ap_view(cm.psA, [[T, 128], [1, 128]], off=oc * 128),
                    ap_view(WQb, [[DC * 1024, 128], [1, 128]], off=c * 1024 + oc * 128),
                    ap_view(cm.XN, [[DC * T, 128], [1, 128]], off=c * T + s * 128),
                    start=(c == 0), stop=(c == DC - 1)),
                    waits=(wq_t, perm_tok) + tuple(xev_last[s - 1]) if c == 0 and oc == 0
                    else (wq_t,),
                    sig="te" if c == DC - 1 else None)
        q_mm = mm
        for oc in range(DC):
            for c in range(DC):
                mm = kb.emit("tensor", lambda e, oc=oc, c=c, s=s: e.matmul(
                    ap_view(cm.psA, [[T, 128], [1, 64]], off=1024 + oc * 64),
                    ap_view(WKb, [[DC * 1024, 128], [1, 128]], off=c * 1024 + oc * 128),
                    ap_view(cm.XN, [[DC * T, 128], [2, 64]], off=c * T + s * 128),
                    start=(c == 0), stop=(c == DC - 1)),
                    waits=(wk_t,), sig="te" if c == DC - 1 else None)
        k_mm = mm
        for half in range(2):
            for c in range(DC):
                mm = kb.emit("tensor", lambda e, half=half, c=c, s=s: e.matmul(
                    ap_view(cm.psB, [[T, 64], [1, 512]], off=half * 512),
                    ap_view(cm.XN, [[DC * T, 128], [2, 64]], off=c * T + s * 128),
                    ap_view(WVb, [[DC * 1024, 128], [1, 512]], off=c * 1024 + half * 512),
                    start=(c == 0), stop=(c == DC - 1)),
                    waits=(wv_t, vw_evs.get(s - 1)) if c == 0 else (wv_t,),
                    sig="te" if c == DC - 1 else None)
        v_mm = mm
        qs_ev[s] = kb.emit("vector", lambda e: e.tensor_copy(
            v(QS), ap_view(cm.psA, [[T, 128], [1, 1024]])),
            waits=(q_mm, st_tile.get(s - 1)), sig="ve")
        ks_ev = kb.emit("vector", lambda e: e.tensor_copy(
            v(KS), ap_view(cm.psA, [[T, 128], [1, 512]], off=1024)),
            waits=(k_mm,), sig="ve")
        vw_ev = kb.emit("vector", lambda e: e.tensor_copy(
            ap_view(VW, [[16 * 65, 64], [65, 16], [1, 64]]),
            ap_view(cm.psB, [[T, 64], [64, 16], [1, 64]])),
            waits=(v_mm, vo_t, av_tile.get(s - 1)), sig="ve")
        vw_evs[s] = vw_ev
        for h in range(16):
            hs = s * 16 + h
            pi = h % 2
            stm = kb.emit("tensor", lambda e, h=h, pi=pi: e.matmul(
                ap_view(cm.psB, [[T, 64], [1, 128]], off=1024 + pi * 512),
                ap_view(KS, [[512, 64], [1, 64]], off=(h % 2) * 64 * 512 + (h // 2) * 64),
                ap_view(QS, [[1024, 64], [1, 128]], off=(h % 2) * 64 * 1024 + (h // 2) * 128),
                start=True, stop=True),
                waits=(qs_ev[s], ks_ev, ex_tok.get(hs - 2), cp_tok.get(hs - 2)), sig="te")
            st_tile[s] = stm
            ex_tok[hs] = kb.emit("scalar", lambda e, pi=pi: e.activation(
                v(EST), ap_view(cm.psB, [[T, 64], [1, 128]], off=1024 + pi * 512),
                AF.Exp, scale=0.125),
                waits=(stm, av_seq.get(hs - 1)), sig="se")
            avm = kb.emit("tensor", lambda e, h=h, pi=pi: e.matmul(
                ap_view(cm.psB, [[T, 65], [1, 128]], off=1152 + pi * 512),
                ap_view(VW, [[16 * 65, 64], [1, 65]], off=h * 65),
                v(EST), start=True, stop=True),
                waits=(ex_tok[hs], vw_ev, cp_tok.get(hs - 2)), sig="te")
            av_tile[s] = avm
            av_seq[hs] = avm
            cp_tok[hs] = kb.emit("vector", lambda e, pi=pi: e.tensor_copy(
                v(STG), ap_view(cm.psB, [[T, 65], [1, 128]], off=1152 + pi * 512)),
                waits=(avm, bc_tok.get(hs - 1)), sig="ve")
            rc = kb.emit("vector", lambda e: e.reciprocal(
                v(STG)[64:65, :], v(STG)[64:65, :]), sig="ve")
            bc_tok[hs] = kb.emit("tensor", lambda e, pi=pi: e.matmul(
                ap_view(cm.psA, [[T, 64], [1, 128]], off=1536 + pi * 128),
                v(cm.ONESF)[64:65, 0:64], v(STG)[64:65, :], start=True, stop=True),
                waits=(rc, mul_tok.get(hs - 1), cm.const_tok), sig="te")
            mul_tok[hs] = kb.emit("vector", lambda e, pi=pi: e.tensor_mul(
                v(OT1), v(STG)[0:64, :],
                ap_view(cm.psA, [[T, 64], [1, 128]], off=1536 + pi * 128)),
                waits=(bc_tok[hs], osd_tok.get(hs - 1)), sig="ve")
            osd_tok[hs] = kb.dma("scalar",
                                 ap_view(OS, [[DC * 128, 64], [1, 128]],
                                         off=(h % 2) * 64 * DC * 128 + (h // 2) * 128),
                                 v(OT1), waits=(mul_tok[hs], op_last.get(s - 1)))
        opm = {}
        for oco in range(DC):
            mm = None
            for c2 in range(DC):
                mm = kb.emit("tensor", lambda e, oco=oco, c2=c2: e.matmul(
                    ap_view(cm.psA, [[T, 128], [1, 128]], off=oco * 128),
                    ap_view(WOb, [[DC * 1024, 128], [1, 128]], off=c2 * 1024 + oco * 128),
                    ap_view(OS, [[DC * 128, 128], [1, 128]], off=c2 * 128),
                    start=(c2 == 0), stop=(c2 == DC - 1)),
                    waits=(wo_t, osd_tok[s * 16 + 15], qs_ev[s], ks_ev),
                    sig="te" if c2 == DC - 1 else None)
            opm[oco] = mm
        op_last[s] = opm[DC - 1]
        xev = None
        for oco in range(DC):
            for (ost, toff, ln, dirn) in _l1a_tile_runs(s):
                if dirn > 0:
                    src = ap_view(cm.psA, [[T, 128], [1, ln]], off=oco * 128 + toff)
                else:
                    src = ap_view(cm.psA, [[T, 128], [-1, ln]],
                                  off=oco * 128 + toff + ln - 1)
                xev = kb.emit("vector", lambda e, oco=oco, ost=ost, ln=ln, src=src: e.tensor_add(
                    ap_view(cm.X, [[DC * T, 128], [1, ln]], off=oco * T + ost),
                    src,
                    ap_view(cm.X, [[DC * T, 128], [1, ln]], off=oco * T + ost)),
                    waits=(opm[DC - 1], x_tok), sig="ve")
        xev_last[s] = (xev,)

    st = emit_store_x(kb, cm, x_out, waits=(xev,))
    kb.wait_all_dma()
    return kb



def build_l2a(weights):
    kb = KB("l2a")
    cm = Common(kb)
    x_in = kb.nc.dram_tensor("x_in", [D, T], F32, kind="ExternalInput").ap()
    x_out = kb.dram_out("x_out", [D, T], F32)
    Wtk = kb.dram_in("w_tk", np.asarray(weights["L2_attn_w_tk"]))
    Wtv = kb.dram_in("w_tv", np.asarray(weights["L2_attn_w_tv"]))
    Wtr = kb.dram_in("w_tr", np.asarray(weights["L2_attn_w_tr"]))
    Wcr = kb.dram_in("w_cr", np.asarray(weights["L2_attn_w_cr"]))
    Wck = kb.dram_in("w_ck", np.asarray(weights["L2_attn_w_ck"]))
    Wcv = kb.dram_in("w_cv", np.asarray(weights["L2_attn_w_cv"]))
    pcols = []
    for key in ("time_decay", "time_first", "time_decay", "time_first",
                "mix_k", "mix_v", "mix_r", "cmix_k", "cmix_r",
                "ln1_g", "ln1_b", "ln2_g", "ln2_b"):
        pcols.append(np.asarray(weights["L2_attn_" + key]).reshape(8, 128).T)
    PRMd = kb.dram_in("w_prm", np.ascontiguousarray(np.concatenate(pcols, axis=1)))
    KKd = kb.dram_tmp("kk_scratch", [32 * 128, T], BF16)

    PRM = kb.sbuf("PRM", [128, 104], F32)
    MXA = kb.sbuf("MXA", [128, 16384], BF16)
    TMB = kb.sbuf("TMB", [128, 16384], BF16)
    SCA = kb.sbuf("SCA", [128, 513], F32)
    SCB = kb.sbuf("SCB", [128, 513], F32)
    SAb = kb.sbuf("SAb", [128, 513], F32)
    SBb = kb.sbuf("SBb", [128, 513], F32)
    GT2 = kb.sbuf("GT2", [128, 512], F32)
    CAR = kb.sbuf("CAR", [128, 40], F32)
    _WB0 = kb.sbuf("WBa", [128, 1024], BF16)
    WB = [_WB0, _WB0]

    prm = lambda off, n=1: ap_view(PRM, [[104, 128], [1, n]], off=off)
    tmb = lambda off, n=512: ap_view(TMB, [[16384, 128], [1, n]], off=off)
    mxa = lambda off, n=512: ap_view(MXA, [[16384, 128], [1, n]], off=off)
    car = lambda off: ap_view(CAR, [[40, 128], [1, 1]], off=off)
    psl = lambda oc, n=512: (ap_view(cm.psA, [[T, 128], [1, n]], off=(oc % 4) * 512)
                             if oc < 4 else
                             ap_view(cm.psB, [[T, 128], [1, n]], off=(oc % 4) * 512))

    x_tok = emit_load_x(kb, cm, x_in)
    prm_t = kb.dma("sync", v(PRM), PRMd)
    ew_t = kb.emit("scalar", lambda e: e.activation(prm(16, 16), prm(0, 16), AF.Exp),
                   waits=(prm_t,), sig="se")

    toks0, psA_f0, _ = emit_ln(kb, cm, x_tok, out_fn=cm.xc)
    toks1, psA_f1, _ = emit_ln(kb, cm, toks0[-1],
                               gamma_beta=(prm(72, 8), prm(80, 8)), entry=psA_f0)
    ln1_last = toks1[-1]

    if L2A_LIMIT < 2:
        st = emit_store_x(kb, cm, x_out, waits=(toks0[-1], ln1_last))
        kb.wait_all_dma()
        return kb
    wkv_done = {}
    proj_guard = psA_f1
    ev_toks = {}

    class _WS(dict):
        def __init__(self):
            super().__init__({0: None, 1: None})
            self.last = None

        def __getitem__(self, k):
            return self.last

        def __setitem__(self, k, v):
            self.last = v
    wstream = _WS()
    for tc in range(NT4):
        t0 = tc * 512
        rs_tok = None
        for pi, (mixoff, outbase, wd) in enumerate(
                [(32, 0, Wtk), (40, 4096, Wtv), (48, 8192, Wtr)]):
            bld = None
            for c in range(DC):
                base = c * T + t0
                nb = 511 if tc == 0 else 512
                ob = base + 1 if tc == 0 else base
                oo = 1 if tc == 0 else 0
                s1 = kb.emit("vector", lambda e, ob=ob, nb=nb, oo=oo: e.tensor_sub(
                    ap_view(GT2, [[512, 128], [1, nb]], off=oo),
                    ap_view(cm.XN, [[DC * T, 128], [1, nb]], off=ob),
                    ap_view(cm.XN, [[DC * T, 128], [1, nb]], off=ob - 1)),
                    waits=(ln1_last, ev_toks.get((tc, pi - 1), proj_guard)), sig="ve")
                bld = kb.emit("vector", lambda e, c=c, ob=ob, nb=nb, oo=oo, mixoff=mixoff:
                              e.scalar_tensor_tensor(
                    ap_view(MXA, [[16384, 128], [1, nb]], off=c * 512 + oo),
                    ap_view(GT2, [[512, 128], [1, nb]], off=oo), prm(mixoff + c),
                    ap_view(cm.XN, [[DC * T, 128], [1, nb]], off=ob - 1),
                    ALU.mult, ALU.add), sig="ve")
                if tc == 0:
                    bld = kb.emit("vector", lambda e, c=c, mixoff=mixoff:
                                  e.tensor_scalar_mul(
                        mxa(c * 512, 1),
                        ap_view(cm.XN, [[DC * T, 128], [1, 1]], off=c * T),
                        prm(mixoff + c)), sig="ve")
            for oc in range(DC):
                wt = kb.dma("gpsimd", v(WB[oc % 2]),
                            ap_view(wd, [[1024, 128], [128 * 1024, DC], [1, 128]],
                                    off=oc * 128),
                            waits=(wstream[oc % 2],))
                mm = None
                for c in range(DC):
                    mm = kb.emit("tensor", lambda e, oc=oc, c=c: e.matmul(
                        psl(oc),
                        ap_view(WB[oc % 2], [[1024, 128], [1, 128]], off=c * 128),
                        mxa(c * 512),
                        start=(c == 0), stop=(c == DC - 1)),
                        waits=(wt, bld, proj_guard if (tc == 0 and pi == 0) else None,
                               ev_toks.get((tc, pi - 1)) if c == 0 else None,
                               wkv_done.get(tc - 1) if c == 0 else None),
                        sig="te" if c == DC - 1 else None)
                wstream[oc % 2] = mm
                if pi == 2:
                    rs_tok = kb.emit("scalar", lambda e, oc=oc, outbase=outbase: e.activation(
                        tmb(outbase + oc * 512), psl(oc), AF.Sigmoid),
                        waits=(mm,), sig="se")
                    ev_toks[(tc, pi)] = rs_tok
                else:
                    ev_toks[(tc, pi)] = kb.emit("vector",
                        lambda e, oc=oc, outbase=outbase: e.tensor_copy(
                            tmb(outbase + oc * 512), psl(oc)),
                        waits=(mm,), sig="ve")
        proj_guard = ev_toks[(tc, 2)]
        last = None
        for cc in range(DC):
            kv = tmb(cc * 512)
            vv = tmb(4096 + cc * 512)
            mtk = kb.emit("vector", lambda e, kv=kv: e.tensor_reduce(
                car(32), kv, AX.X, ALU.max),
                waits=(ev_toks[(tc, 0)], ev_toks[(tc, 1)], rs_tok), sig="ve")
            if tc == 0:
                kb.emit("vector", lambda e, cc=cc: e.tensor_copy(car(16 + cc), car(32)))
            else:
                kb.emit("vector", lambda e, cc=cc: e.tensor_copy(car(24 + cc), car(16 + cc)))
                kb.emit("vector", lambda e, cc=cc: e.tensor_tensor(
                    car(16 + cc), car(32), car(16 + cc), ALU.max))
            sca1 = ap_view(SCA, [[513, 128], [1, 512]], off=1)
            scb1 = ap_view(SCB, [[513, 128], [1, 512]], off=1)
            sab0 = ap_view(SAb, [[513, 128], [1, 512]])
            sbb0 = ap_view(SBb, [[513, 128], [1, 512]])
            ngm = kb.emit("vector", lambda e, cc=cc: e.tensor_scalar(
                car(33), car(16 + cc), -1.0, 0.0, ALU.mult, ALU.add), sig="ve")
            ek = kb.emit("scalar", lambda e, kv=kv, cc=cc: e.activation(
                v(GT2), kv, AF.Exp, bias=car(33), scale=1.0),
                waits=(ngm,), sig="se")
            kb.emit("vector", lambda e, cc=cc, scb1=scb1: e.tensor_scalar_mul(
                scb1, v(GT2), prm(16 + cc)), waits=(ek, ew_t))
            rst = None
            if tc == 0:
                kb.emit("vector", lambda e: e.tensor_scalar(
                    ap_view(SCB, [[513, 128], [1, 1]]), car(32), 0.0, 0.0,
                    ALU.mult, ALU.add))
                kb.emit("vector", lambda e: e.tensor_scalar(
                    ap_view(SCA, [[513, 128], [1, 1]]), car(32), 0.0, 0.0,
                    ALU.mult, ALU.add))
            else:
                kb.emit("vector", lambda e, cc=cc: e.tensor_sub(
                    car(34), car(24 + cc), car(16 + cc)))
                rst = kb.emit("scalar", lambda e: e.activation(
                    car(35), car(34), AF.Exp), waits=(("ve", kb.cnt["ve"]),), sig="se")
                kb.emit("vector", lambda e, cc=cc: e.tensor_mul(
                    ap_view(SCB, [[513, 128], [1, 1]]), car(8 + cc), car(35)),
                    waits=(rst,))
            kb.emit("vector", lambda e, cc=cc: e.tensor_tensor_scan(
                v(SBb), ap_view(PRM, [[104, 128], [0, 513]], off=16 + cc), v(SCB),
                0.0, ALU.mult, ALU.add))
            kb.emit("vector", lambda e, cc=cc: e.tensor_copy(
                car(8 + cc), ap_view(SBb, [[513, 128], [1, 1]], off=512)))
            kb.emit("vector", lambda e, cc=cc, sca1=sca1, sbb0=sbb0: e.tensor_scalar_mul(
                sca1, sbb0, prm(16 + cc)))
            kb.emit("vector", lambda e, cc=cc, sca1=sca1: e.scalar_tensor_tensor(
                sca1, v(GT2), prm(16 + cc), sca1, ALU.mult, ALU.add))
            kb.emit("vector", lambda e, sab0=sab0, sbb0=sbb0: e.tensor_tensor(
                sab0, sbb0, v(GT2), ALU.max))
            kb.emit("vector", lambda e, sab0=sab0: e.reciprocal(sab0, sab0))
            kb.emit("vector", lambda e, sca1=sca1, sab0=sab0: e.tensor_mul(
                sca1, sca1, sab0))
            kb.emit("vector", lambda e, sca1=sca1, scb1=scb1: e.tensor_mul(
                scb1, sca1, v(GT2)))
            kb.emit("vector", lambda e, vv=vv, scb1=scb1: e.tensor_mul(
                scb1, scb1, vv))
            if tc == 0:
                kb.emit("vector", lambda e: e.tensor_scalar(
                    ap_view(SCB, [[513, 128], [1, 1]]), car(32), 0.0, 0.0,
                    ALU.mult, ALU.add))
            else:
                kb.emit("vector", lambda e, cc=cc: e.tensor_mul(
                    ap_view(SCB, [[513, 128], [1, 1]]), car(cc), car(35)))
            kb.emit("vector", lambda e: e.tensor_tensor_scan(
                v(SAb), v(SCA), v(SCB), 0.0, ALU.mult, ALU.add))
            kb.emit("vector", lambda e, cc=cc: e.tensor_copy(
                car(cc), ap_view(SAb, [[513, 128], [1, 1]], off=512)))
            kb.emit("vector", lambda e, cc=cc, sca1=sca1: e.tensor_scalar_mul(
                sca1, v(GT2), prm(24 + cc)))
            kb.emit("vector", lambda e, vv=vv, scb1=scb1, sca1=sca1: e.tensor_mul(
                scb1, sca1, vv))
            kb.emit("vector", lambda e, scb1=scb1, sab0=sab0: e.tensor_add(
                scb1, sab0, scb1))
            kb.emit("vector", lambda e, sca1=sca1, sbb0=sbb0: e.tensor_add(
                sca1, sbb0, sca1))
            kb.emit("vector", lambda e, sca1=sca1: e.reciprocal(sca1, sca1))
            kb.emit("vector", lambda e, cc=cc, scb1=scb1, sca1=sca1: e.tensor_mul(
                tmb(12288 + cc * 512), scb1, sca1))
            kb.emit("vector", lambda e, cc=cc: e.tensor_mul(
                v(GT2), tmb(8192 + cc * 512), tmb(12288 + cc * 512)))
            last = kb.emit("vector", lambda e, cc=cc, t0=t0: e.tensor_add(
                cm.xc(cc, t0, 512), v(GT2), cm.xc(cc, t0, 512)),
                waits=(x_tok,), sig="ve")
        wkv_done[tc] = last

    if L2A_LIMIT < 3:
        st = emit_store_x(kb, cm, x_out, waits=(wkv_done[NT4 - 1],))
        kb.wait_all_dma()
        return kb
    toks2, psA_f2, _ = emit_ln(kb, cm, wkv_done[NT4 - 1],
                               gamma_beta=(prm(88, 8), prm(96, 8)), entry=proj_guard)
    ln2_last = toks2[-1]
    ck_bld = None
    for c in range(DC):
        for q in range(4):
            base = c * T + q * 512
            nb = 511 if q == 0 else 512
            ob = base + 1 if q == 0 else base
            oo = 1 if q == 0 else 0
            kb.emit("vector", lambda e, ob=ob, nb=nb, oo=oo: e.tensor_sub(
                ap_view(GT2, [[512, 128], [1, nb]], off=oo),
                ap_view(cm.XN, [[DC * T, 128], [1, nb]], off=ob),
                ap_view(cm.XN, [[DC * T, 128], [1, nb]], off=ob - 1)),
                waits=(ln2_last,), sig="ve")
            ck_bld = kb.emit("vector", lambda e, c=c, q=q, ob=ob, nb=nb, oo=oo:
                             e.scalar_tensor_tensor(
                ap_view(MXA, [[16384, 128], [1, nb]], off=c * 2048 + q * 512 + oo),
                ap_view(GT2, [[512, 128], [1, nb]], off=oo), prm(56 + c),
                ap_view(cm.XN, [[DC * T, 128], [1, nb]], off=ob - 1),
                ALU.mult, ALU.add), sig="ve")
        ck_bld = kb.emit("vector", lambda e, c=c: e.tensor_scalar_mul(
            mxa(c * 2048, 1),
            ap_view(cm.XN, [[DC * T, 128], [1, 1]], off=c * T),
            prm(56 + c)), sig="ve")
    sq_toks = {}
    kk_free = {}
    for kc in range(32):
        wt = kb.dma("gpsimd", v(WB[kc % 2]),
                    ap_view(Wck, [[4096, 128], [128 * 4096, DC], [1, 128]], off=kc * 128),
                    waits=(wstream[kc % 2],))
        ps = cm.psA if kc % 2 == 0 else cm.psB
        mm = None
        for q in range(4):
            for c in range(DC):
                mm = kb.emit("tensor", lambda e, kc=kc, q=q, c=c, ps=ps: e.matmul(
                    ap_view(ps, [[T, 128], [1, 512]], off=q * 512),
                    ap_view(WB[kc % 2], [[1024, 128], [1, 128]], off=c * 128),
                    ap_view(MXA, [[16384, 128], [1, 512]], off=c * 2048 + q * 512),
                    start=(c == 0), stop=(c == DC - 1)),
                    waits=(wt, ck_bld, psA_f2 if kc <= 1 else None,
                           sq_toks.get(kc - 2) if (q == 0 and c == 0) else None),
                    sig="te" if (q == 3 and c == DC - 1) else None)
        wstream[kc % 2] = mm
        bnc = (kc % 2) * 2048
        rl = kb.emit("scalar", lambda e, ps=ps, bnc=bnc: e.activation(
            tmb(bnc, 2048), ap_view(ps, [[T, 128], [1, 2048]]), AF.Relu),
            waits=(mm, kk_free.get(kc - 2)), sig="se")
        sq_toks[kc] = kb.emit("vector", lambda e, bnc=bnc: e.tensor_mul(
            tmb(bnc, 2048), tmb(bnc, 2048), tmb(bnc, 2048)), waits=(rl,), sig="ve")
        kk_free[kc] = kb.dma("sync",
                             ap_view(KKd, [[T, 128], [1, T]], off=kc * 128 * T),
                             tmb(bnc, 2048), waits=(sq_toks[kc],))
    kk_last = kk_free[31]
    cr_bld = None
    for c in range(DC):
        for q in range(4):
            base = c * T + q * 512
            nb = 511 if q == 0 else 512
            ob = base + 1 if q == 0 else base
            oo = 1 if q == 0 else 0
            kb.emit("vector", lambda e, ob=ob, nb=nb, oo=oo: e.tensor_sub(
                ap_view(GT2, [[512, 128], [1, nb]], off=oo),
                ap_view(cm.XN, [[DC * T, 128], [1, nb]], off=ob),
                ap_view(cm.XN, [[DC * T, 128], [1, nb]], off=ob - 1)),
                waits=(sq_toks[31], wstream[0], wstream[1]), sig="ve")
            cr_bld = kb.emit("vector", lambda e, c=c, q=q, ob=ob, nb=nb, oo=oo:
                             e.scalar_tensor_tensor(
                ap_view(MXA, [[16384, 128], [1, nb]], off=c * 2048 + q * 512 + oo),
                ap_view(GT2, [[512, 128], [1, nb]], off=oo), prm(64 + c),
                ap_view(cm.XN, [[DC * T, 128], [1, nb]], off=ob - 1),
                ALU.mult, ALU.add), sig="ve")
        cr_bld = kb.emit("vector", lambda e, c=c: e.tensor_scalar_mul(
            mxa(c * 2048, 1),
            ap_view(cm.XN, [[DC * T, 128], [1, 1]], off=c * T),
            prm(64 + c)), sig="ve")
    crs_toks = {}
    for oc in range(DC):
        wt = kb.dma("gpsimd", v(WB[oc % 2]),
                    ap_view(Wcr, [[1024, 128], [128 * 1024, DC], [1, 128]], off=oc * 128),
                    waits=(wstream[oc % 2],))
        ps = cm.psA if oc % 2 == 0 else cm.psB
        mm = None
        for q in range(4):
            for c in range(DC):
                mm = kb.emit("tensor", lambda e, oc=oc, q=q, c=c, ps=ps: e.matmul(
                    ap_view(ps, [[T, 128], [1, 512]], off=q * 512),
                    ap_view(WB[oc % 2], [[1024, 128], [1, 128]], off=c * 128),
                    ap_view(MXA, [[16384, 128], [1, 512]], off=c * 2048 + q * 512),
                    start=(c == 0), stop=(c == DC - 1)),
                    waits=(wt, cr_bld, kk_last if oc <= 1 else None,
                           crs_toks.get(oc - 2) if (q == 0 and c == 0) else None),
                    sig="te" if (q == 3 and c == DC - 1) else None)
        wstream[oc % 2] = mm
        crs_toks[oc] = kb.emit("scalar", lambda e, oc=oc, ps=ps: e.activation(
            tmb(oc * 2048, 2048), ap_view(ps, [[T, 128], [1, 2048]]), AF.Sigmoid),
            waits=(mm,), sig="se")
    kknt_dma = None
    xcv = None
    ev_last = {}
    for nt in range(NT4):
        kknt_dma = kb.dma("sync", v(MXA),
                          ap_view(KKd, [[T, 128], [128 * T, 32], [1, 512]], off=nt * 512),
                          waits=(crs_toks[DC - 1] if nt == 0 else None,
                                 ev_last.get(nt - 1), kk_last))
        for oc in range(DC):
            mm = None
            for kg in range(4):
                wt = kb.dma("gpsimd", v(WB[kg % 2]),
                            ap_view(Wcv, [[1024, 128], [128 * 1024, 8], [1, 128]],
                                    off=(kg * 8) * 128 * 1024 + oc * 128),
                            waits=(wstream[kg % 2],))
                for kcl in range(8):
                    kc = kg * 8 + kcl
                    mm = kb.emit("tensor", lambda e, oc=oc, kc=kc, kcl=kcl: e.matmul(
                        psl(oc),
                        ap_view(WB[(kc // 8) % 2], [[1024, 128], [1, 128]], off=kcl * 128),
                        ap_view(MXA, [[16384, 128], [1, 512]], off=kc * 512),
                        start=(kc == 0), stop=(kc == 31)),
                        waits=(wt, kknt_dma,
                               ev_last.get(nt - 1) if kc == 0 else None),
                        sig="te" if kcl == 7 else None)
                wstream[kg % 2] = mm
            kb.emit("vector", lambda e, oc=oc, nt=nt: e.tensor_mul(
                v(GT2), tmb(oc * 2048 + nt * 512), psl(oc)),
                waits=(mm, crs_toks[oc]), sig="ve")
            xcv = kb.emit("vector", lambda e, oc=oc, nt=nt: e.tensor_add(
                cm.xc(oc, nt * 512, 512), v(GT2), cm.xc(oc, nt * 512, 512)), sig="ve")
            ev_last[nt] = xcv
    xfin = None
    for c in range(DC):
        for q in range(4):
            ld = kb.dma("sync",
                        ap_view(SAb if q % 2 == 0 else SBb, [[513, 128], [1, 512]]),
                        ap_view(x_in, [[T, 128], [1, 512]],
                                off=c * 128 * T + q * 512),
                        waits=(xfin,))
            xfin = kb.emit("vector", lambda e, c=c, q=q: e.tensor_add(
                cm.xc(c, q * 512, 512),
                ap_view(SAb if q % 2 == 0 else SBb, [[513, 128], [1, 512]]),
                cm.xc(c, q * 512, 512)),
                waits=(ld, xcv), sig="ve")
    st = emit_store_x(kb, cm, x_out, waits=(xfin,))
    kb.wait_all_dma()
    return kb



def build_l2f(weights):
    kb = KB("l2f")
    cm = Common(kb)
    x_in = kb.nc.dram_tensor("x_in", [D, T], F32, kind="ExternalInput").ap()
    x_out = kb.dram_out("x_out", [D, T], F32)
    WQT = kb.dram_in("w_wqt", np.ascontiguousarray(np.asarray(weights["L2_ffn_W_q"]).T))
    keys = np.asarray(weights["L2_ffn_keys"])
    ky = np.zeros((128, 512), np.float32)
    for h in range(4):
        for p in range(2):
            ky[(h % 2) * 64:(h % 2) * 64 + 64,
               (2 * p + h // 2) * 64:(2 * p + h // 2) * 64 + 64] = \
                keys[h, :, p, :].T
    KYd = kb.dram_in("w_ky", ky)
    DET = kb.dram_in("w_det", np.ascontiguousarray(np.asarray(weights["L2_ffn_down_embed"]).T))
    UEd = kb.dram_in("w_ue", np.asarray(weights["L2_ffn_up_embed"]))
    WUPT = kb.dram_in("w_wupt", np.ascontiguousarray(np.asarray(weights["L2_ffn_W_up"]).T))
    WDNT = kb.dram_in("w_wdnt", np.ascontiguousarray(np.asarray(weights["L2_ffn_W_down"]).T))
    ee = np.arange(4096)
    spx = (ee[None, :] // 64 == np.arange(64)[:, None]).astype(np.float32)
    spy = (ee[None, :] % 64 == np.arange(64)[:, None]).astype(np.float32)
    SPXd = kb.dram_in("c_spx", spx)
    SPYd = kb.dram_in("c_spy", spy)
    IDFd = kb.dram_in("c_idf2", np.eye(128, dtype=np.float32))
    Gd = kb.dram_tmp("g_scratch", [32 * 128, T], BF16)
    WDd = kb.dram_tmp("wd_scratch2", [32 * 128, T], BF16)
    DHd = kb.dram_tmp("dh_scratch", [32 * 128, T], BF16)
    RWd = kb.dram_tmp("rw_scratch", [8, T], F32)

    WQS = kb.sbuf("WQS", [128, DC * 512], BF16)
    Q = kb.sbuf("Q", [128, 4 * T], BF16)
    KY = kb.sbuf("KY", [128, 512], BF16)
    SIMS = kb.sbuf("SIMS", [64, 8 * T], BF16)
    GT = kb.sbuf("GT", [128, 264], F32)
    TMRC = kb.sbuf("TMRC", [128, 8], F32)
    IDF = kb.sbuf("IDF2", [128, 128], F32)
    ROWA = kb.sbuf("ROWA", [1, 512], F32)
    ROWB = kb.sbuf("ROWB", [1, 512], F32)
    WS = [kb.sbuf("WSa", [128, 1024], BF16), kb.sbuf("WSb", [128, 1024], BF16)]
    GB = kb.sbuf("GB", [128, T], BF16)
    WDC = kb.sbuf("WDC", [128, T], F32)
    EB = kb.sbuf("EB", [128, T], BF16)
    MB = kb.sbuf("MB", [128, T], BF16)
    PB = kb.sbuf("PB", [128, T], BF16)

    gt = lambda off, dims: ap_view(GT, [[264, 128]] + dims, off=off)
    psa = lambda slot, n=512: ap_view(cm.psA, [[T, 128], [1, n]], off=slot * 512)
    psb = lambda slot, n=512: ap_view(cm.psB, [[T, 128], [1, n]], off=slot * 512)
    psl_es = lambda oc, n=512: (psa(oc % 4, n) if oc < 4 else psb(oc % 4, n))

    x_tok = emit_load_x(kb, cm, x_in)
    wq_t = kb.dma("gpsimd", ap_view(WQS, [[DC * 512, 128], [512, DC], [1, 512]]),
                  WQT.rearrange("(c p) e -> p c e", p=128))
    ky_t = kb.dma("gpsimd", v(KY), KYd)
    idf_t = kb.dma("sync", v(IDF), IDFd)
    ln_toks, psA_free, _ = emit_ln(kb, cm, x_tok)
    ln_last = ln_toks[-1]

    qev = {}
    for qc in range(4):
        ps = psa if qc % 2 == 0 else psb
        mm = None
        for nt in range(NT4):
            for c in range(DC):
                mm = kb.emit("tensor", lambda e, qc=qc, nt=nt, c=c, ps=ps: e.matmul(
                    ps(nt),
                    ap_view(WQS, [[DC * 512, 128], [1, 128]], off=c * 512 + qc * 128),
                    cm.xnc(c, nt * 512, 512),
                    start=(c == 0), stop=(c == DC - 1)),
                    waits=(wq_t, ln_last, psA_free if qc == 0 else None,
                           qev.get(qc - 2) if (nt == 0 and c == 0) else None),
                    sig="te" if c == DC - 1 else None)
        qev[qc] = kb.emit("vector", lambda e, qc=qc, ps=ps: e.tensor_copy(
            ap_view(Q, [[4 * T, 128], [1, T]], off=qc * T),
            ps(0, T)), waits=(mm,), sig="ve")
    sev = {}
    sprev = qev[3]
    for h in range(4):
        for p in range(2):
            qc = 2 * p + h // 2
            base = (h % 2) * 64
            mm = None
            for nt in range(NT4):
                mm = kb.emit("tensor", lambda e, qc=qc, base=base, nt=nt: e.matmul(
                    ap_view(cm.psA, [[T, 64], [1, 512]], off=nt * 512),
                    ap_view(KY, [[512, 64], [1, 64]], off=base * 512 + qc * 64),
                    ap_view(Q, [[4 * T, 64], [1, 512]], off=base * 4 * T + qc * T + nt * 512),
                    start=True, stop=True),
                    waits=(ky_t, sprev) if nt == 0 else (),
                    sig="te" if nt == NT4 - 1 else None)
            sev[(h, p)] = mm
            sprev = kb.emit("vector", lambda e, h=h, p=p: e.tensor_copy(
                ap_view(SIMS, [[8 * T, 64], [1, T]], off=(h * 2 + p) * T),
                ap_view(cm.psA, [[T, 64], [1, T]], off=0)),
                waits=(mm,), sig="ve")
    sims_done = sprev

    rw_dma = {}
    tp_prev = {}
    tpr_prev = None
    cseq = 0
    for tt in range(16):
        for h in range(4):
            for p in range(2):
                slot = cseq % 2
                tp = kb.emit("tensor", lambda e, h=h, p=p, tt=tt, slot=slot: e.matmul(
                    ap_view(cm.psB, [[T, 128], [1, 64]], off=slot * 512),
                    ap_view(SIMS, [[8 * T, 64], [1, 128]], off=(h * 2 + p) * T + tt * 128),
                    ap_view(cm.IDENT, [[128, 64], [1, 64]]),
                    start=True, stop=True),
                    waits=(sims_done, idf_t, tp_prev.get(cseq - 2)), sig="te")
                cpt = kb.emit("vector", lambda e, slot=slot: e.tensor_copy(
                    gt(0, [[1, 64]]), ap_view(cm.psB, [[T, 128], [1, 64]], off=slot * 512)),
                    waits=(tp,), sig="ve")
                tp_prev[cseq] = cpt
                kb.emit("vector", lambda e: e.tensor_copy(gt(64, [[1, 64]]), gt(0, [[1, 64]])),
                        waits=(rw_dma.get(tt - 1) if cseq % 8 == 0 else None,))
                cseq += 1
                x8off = 128 + h * 16 + p * 8
                for r in range(8):
                    kb.emit("vector", lambda e, x8off=x8off, r=r: e.tensor_reduce(
                        gt(x8off + r, [[1, 1]]), gt(64, [[1, 64]]), AX.X, ALU.max))
                    if r < 7:
                        kb.emit("vector", lambda e, x8off=x8off, r=r: e.tensor_tensor(
                            gt(0, [[1, 64]]), gt(64, [[1, 64]]),
                            gt(x8off + r, [[0, 64]]), ALU.is_ge))
                        kb.emit("vector", lambda e: e.scalar_tensor_tensor(
                            gt(64, [[1, 64]]), gt(0, [[1, 64]]), -1e9, gt(64, [[1, 64]]),
                            ALU.mult, ALU.add))
            x8off = 128 + h * 16
            kb.emit("vector", lambda e, x8off=x8off: e.tensor_tensor(
                gt(192, [[8, 8], [1, 8]]),
                gt(x8off, [[1, 8], [0, 8]]),
                gt(x8off + 8, [[0, 8], [1, 8]]), ALU.add))
            kb.emit("vector", lambda e: e.tensor_copy(gt(64, [[1, 64]]), gt(192, [[1, 64]])))
            for r in range(8):
                kb.emit("vector", lambda e, r=r: e.tensor_reduce(
                    gt(256 if r < 7 else 259, [[1, 1]]), gt(64, [[1, 64]]), AX.X, ALU.max))
                if r == 0:
                    kb.emit("vector", lambda e: e.tensor_scalar(
                        gt(257, [[1, 1]]), gt(256, [[1, 1]]), -1.0, 0.0, ALU.mult, ALU.add))
                if r < 7:
                    kb.emit("vector", lambda e: e.tensor_tensor(
                        gt(0, [[1, 64]]), gt(64, [[1, 64]]), gt(256, [[0, 64]]), ALU.is_ge))
                    kb.emit("vector", lambda e: e.scalar_tensor_tensor(
                        gt(64, [[1, 64]]), gt(0, [[1, 64]]), -1e9, gt(64, [[1, 64]]),
                        ALU.mult, ALU.add))
            vtk = kb.emit("vector", lambda e: e.engine_nop(), sig="ve")
            ectk = kb.emit("scalar", lambda e: e.activation(
                gt(64, [[1, 64]]), gt(192, [[1, 64]]), AF.Exp, bias=gt(257, [[1, 1]]),
                scale=1.0), waits=(vtk,), sig="se")
            kb.emit("vector", lambda e: e.tensor_tensor(
                gt(0, [[1, 64]]), gt(192, [[1, 64]]), gt(259, [[0, 64]]), ALU.is_ge),
                waits=(ectk,))
            kb.emit("vector", lambda e: e.tensor_mul(
                gt(0, [[1, 64]]), gt(0, [[1, 64]]), gt(64, [[1, 64]])))
            sdn = kb.emit("vector", lambda e: e.tensor_reduce(
                gt(260, [[1, 1]]), gt(0, [[1, 64]]), AX.X, ALU.add), sig="ve")
            ln = kb.emit("scalar", lambda e: e.activation(
                gt(258, [[1, 1]]), gt(260, [[1, 1]]), AF.Ln), waits=(sdn,), sig="se")
            kb.emit("vector", lambda e, h=h: e.tensor_sub(
                ap_view(TMRC, [[8, 128], [1, 1]], off=h),
                gt(257, [[1, 1]]), gt(258, [[1, 1]])), waits=(ln, tpr_prev))
            kb.emit("vector", lambda e, h=h: e.tensor_scalar(
                ap_view(TMRC, [[8, 128], [1, 1]], off=4 + h),
                gt(259, [[1, 1]]), -1.0, 0.0, ALU.mult, ALU.add))
        ttok = ("ve", kb.cnt["ve"])
        tpr = kb.emit("tensor", lambda e: e.matmul(
            ap_view(cm.psB, [[T, 8], [1, 128]], off=1024),
            v(TMRC), v(IDF), start=True, stop=True),
            waits=(ttok, idf_t, rw_dma.get(tt - 1)), sig="te")
        rev = kb.emit("vector", lambda e: e.tensor_copy(
            ap_view(GT, [[264, 8], [1, 128]], off=64),
            ap_view(cm.psB, [[T, 8], [1, 128]], off=1024)),
            waits=(tpr,), sig="ve")
        rw_dma[tt] = kb.dma("sync", ap_view(RWd, [[T, 8], [1, 128]], off=tt * 128),
                            ap_view(GT, [[264, 8], [1, 128]], off=64), waits=(rev,))
        tpr_prev = tpr
    rw_last = rw_dma[15]

    g_dma = {}
    wprev = {0: None, 1: None}
    ev_prev = None
    for ec in range(32):
        wt = kb.dma("gpsimd", v(WS[ec % 2]),
                    ap_view(DET, [[4096, 128], [128 * 4096, DC], [1, 128]], off=ec * 128),
                    waits=(wprev[ec % 2],))
        mm = None
        for nt in range(NT4):
            for c in range(DC):
                mm = kb.emit("tensor", lambda e, ec=ec, nt=nt, c=c: e.matmul(
                    psa(nt),
                    ap_view(WS[ec % 2], [[1024, 128], [1, 128]], off=c * 128),
                    cm.xnc(c, nt * 512, 512),
                    start=(c == 0), stop=(c == DC - 1)),
                    waits=(wt, sims_done if ec == 0 else None,
                           g_dma.get(ec - 1) if (nt == 0 and c == 0) else None),
                    sig="te" if (nt == NT4 - 1 and c == DC - 1) else None)
        wprev[ec % 2] = mm
        ev = kb.emit("vector", lambda e: e.tensor_copy(v(GB), psa(0, T)),
                     waits=(mm, g_dma.get(ec - 1)), sig="ve")
        g_dma[ec] = kb.dma("gpsimd", ap_view(Gd, [[T, 128], [1, T]], off=ec * 128 * T),
                           v(GB), waits=(ev,))
    g_last = g_dma[31]

    spx_t = {}
    wd_dma = {}
    gload = {}
    for ec in range(32):
        s1 = kb.dma("gpsimd", ap_view(WS[0], [[1024, 64], [1, 128]]),
                    ap_view(SPXd, [[4096, 64], [1, 128]], off=ec * 128),
                    waits=(wprev[0] if ec == 0 else spx_t.get(ec - 1),))
        s2 = kb.dma("gpsimd", ap_view(WS[1], [[1024, 64], [1, 128]]),
                    ap_view(SPYd, [[4096, 64], [1, 128]], off=ec * 128),
                    waits=(wprev[1] if ec == 0 else spx_t.get(ec - 1),))
        gl = kb.dma("sync", v(GB), ap_view(Gd, [[T, 128], [1, T]], off=ec * 128 * T),
                    waits=(g_last, gload.get(ec - 1)))
        acc_tok = None
        for h in range(4):
            mm1 = None
            mm2 = None
            for nt in range(NT4):
                ra = kb.dma("sync", v(ROWA),
                            ap_view(RWd, [[T, 1], [1, 512]], off=h * T + nt * 512),
                            waits=(rw_last, ("te", kb.cnt["te"])))
                rb = kb.dma("sync", v(ROWB),
                            ap_view(RWd, [[T, 1], [1, 512]], off=(4 + h) * T + nt * 512),
                            waits=(rw_last, ("te", kb.cnt["te"])))
                mm1 = kb.emit("tensor", lambda e, ec=ec, h=h, nt=nt: e.matmul(
                    psa(nt),
                    ap_view(WS[0], [[1024, 64], [1, 128]]),
                    ap_view(SIMS, [[8 * T, 64], [1, 512]], off=(h * 2) * T + nt * 512),
                    start=True, stop=False),
                    waits=(s1, sev[(h, 0)], acc_tok if nt == 0 else None,
                           wd_dma.get(ec - 1) if (h == 0 and nt == 0) else None))
                kb.emit("tensor", lambda e, ec=ec, h=h, nt=nt: e.matmul(
                    psa(nt),
                    ap_view(WS[1], [[1024, 64], [1, 128]]),
                    ap_view(SIMS, [[8 * T, 64], [1, 512]], off=(h * 2 + 1) * T + nt * 512),
                    start=False, stop=False), waits=(s2,))
                mm1 = kb.emit("tensor", lambda e, nt=nt: e.matmul(
                    psa(nt),
                    ap_view(cm.ONESF, [[128, 1], [1, 128]]), v(ROWA),
                    start=False, stop=True),
                    waits=(ra, cm.const_tok), sig="te")
                mm2 = kb.emit("tensor", lambda e, ec=ec, h=h, nt=nt: e.matmul(
                    psb(nt),
                    ap_view(WS[0], [[1024, 64], [1, 128]]),
                    ap_view(SIMS, [[8 * T, 64], [1, 512]], off=(h * 2) * T + nt * 512),
                    start=True, stop=False), waits=())
                kb.emit("tensor", lambda e, ec=ec, h=h, nt=nt: e.matmul(
                    psb(nt),
                    ap_view(WS[1], [[1024, 64], [1, 128]]),
                    ap_view(SIMS, [[8 * T, 64], [1, 512]], off=(h * 2 + 1) * T + nt * 512),
                    start=False, stop=False), waits=())
                mm2 = kb.emit("tensor", lambda e, nt=nt: e.matmul(
                    psb(nt),
                    ap_view(cm.ONESF, [[128, 1], [1, 128]]), v(ROWB),
                    start=False, stop=True),
                    waits=(rb,), sig="te")
            exp_t = kb.emit("scalar", lambda e: e.activation(
                v(EB), ap_view(cm.psA, [[T, 128], [1, T]]), AF.Exp),
                waits=(mm1, acc_tok), sig="se")
            msk_t = kb.emit("vector", lambda e: e.tensor_single_scalar(
                v(MB), ap_view(cm.psB, [[T, 128], [1, T]]), 0.0, ALU.is_ge),
                waits=(mm2,), sig="ve")
            kb.emit("vector", lambda e: e.tensor_mul(v(PB), v(GB), v(EB)),
                    waits=(gl, exp_t))
            sil_t = kb.emit("scalar", lambda e: e.activation(
                v(EB), v(PB), AF.Silu), waits=(("ve", kb.cnt["ve"]),), sig="se")
            kb.emit("vector", lambda e: e.tensor_mul(v(PB), v(EB), v(MB)),
                    waits=(sil_t,))
            if h == 0:
                acc_tok = kb.emit("vector", lambda e: e.tensor_copy(v(WDC), v(PB)),
                                  sig="ve")
            else:
                acc_tok = kb.emit("vector", lambda e: e.tensor_add(v(WDC), v(PB), v(WDC)),
                                  sig="ve")
        spx_t[ec] = mm2
        gload[ec] = gl
        wd_dma[ec] = kb.dma("gpsimd", ap_view(WDd, [[T, 128], [1, T]], off=ec * 128 * T),
                            v(WDC), waits=(acc_tok,))
    wd_last = wd_dma[31]

    dh_dma = {}
    for hc in range(32):
        wt = kb.dma("gpsimd", v(WS[hc % 2]),
                    ap_view(WUPT, [[4096, 128], [128 * 4096, DC], [1, 128]], off=hc * 128),
                    waits=(spx_t.get(31) if hc < 2 else None, dh_dma.get(hc - 2),))
        mm = None
        for nt in range(NT4):
            for c in range(DC):
                mm = kb.emit("tensor", lambda e, hc=hc, nt=nt, c=c: e.matmul(
                    psa(nt) if hc % 2 == 0 else psb(nt),
                    ap_view(WS[hc % 2], [[1024, 128], [1, 128]], off=c * 128),
                    cm.xnc(c, nt * 512, 512),
                    start=(c == 0), stop=(c == DC - 1)),
                    waits=(wt, wd_dma.get(31) if hc <= 1 and nt == 0 and c == 0 else None,
                           dh_dma.get(hc - 2) if (nt == 0 and c == 0) else None),
                    sig="te" if (nt == NT4 - 1 and c == DC - 1) else None)
        wprev[hc % 2] = mm
        sl = kb.emit("scalar", lambda e, hc=hc: e.activation(
            v(GB) if hc % 2 == 0 else v(PB),
            ap_view(cm.psA if hc % 2 == 0 else cm.psB, [[T, 128], [1, T]]), AF.Silu),
            waits=(mm, dh_dma.get(hc - 2)), sig="se")
        dh_dma[hc] = kb.dma("sync", ap_view(DHd, [[T, 128], [1, T]], off=hc * 128 * T),
                            v(GB) if hc % 2 == 0 else v(PB), waits=(sl,))
    dh_last = dh_dma[31]

    def wdnt_slot(kc):
        if kc < 16:
            return Q, kc * 512
        if kc < 20:
            return GB, (kc - 16) * 512
        if kc < 24:
            return PB, (kc - 20) * 512
        if kc < 28:
            return EB, (kc - 24) * 512
        return MB, (kc - 28) * 512

    for (src_d, w_d, tag) in ((WDd, UEd, "es"), (DHd, WDNT, "dn")):
        ev_last = {}
        ld_toks = {}
        for nt in range(NT4):
            lds = []
            for tn, base_kc in ((Q, 0), (GB, 16), (PB, 20), (EB, 24), (MB, 28)):
                nkc = 16 if tn is Q else 4
                lds.append(kb.dma(
                    "sync",
                    ap_view(tn, [[v(tn).ap[0][0], 128], [512, nkc], [1, 512]]),
                    ap_view(src_d, [[T, 128], [128 * T, nkc], [1, 512]],
                            off=base_kc * 128 * T + nt * 512),
                    waits=(wd_last, dh_last,
                           ev_last.get(nt - 1),
                           sl if (tag == "dn" and nt == 0) else None)))
            ld_toks[nt] = lds[-1]
            for oc in range(DC):
                mm = None
                for kg in range(4):
                    wt = kb.dma("gpsimd", v(WS[kg % 2]),
                                ap_view(w_d, [[1024, 128], [131072, 8], [1, 128]],
                                        off=(kg * 8) * 131072 + oc * 128),
                                waits=(wprev[kg % 2],))
                    for kcl in range(8):
                        kc = kg * 8 + kcl
                        tn, coff = wdnt_slot(kc)
                        mm = kb.emit("tensor", lambda e, oc=oc, kc=kc, kcl=kcl, kg=kg,
                                     tn=tn, coff=coff, nt=nt: e.matmul(
                            psl_es(oc),
                            ap_view(WS[kg % 2], [[1024, 128], [1, 128]], off=kcl * 128),
                            ap_view(tn, [[v(tn).ap[0][0], 128], [1, 512]], off=coff),
                            start=(kc == 0), stop=(kc == 31)),
                            waits=(wt, lds[0], lds[1], lds[2], lds[3], lds[4],
                                   ev_last.get(nt - 1) if kc == 0 else None),
                            sig="te" if kcl == 7 else None)
                    wprev[kg % 2] = mm
                xcv = kb.emit("vector", lambda e, oc=oc, nt=nt: e.tensor_add(
                    cm.xc(oc, nt * 512, 512), psl_es(oc), cm.xc(oc, nt * 512, 512)),
                    waits=(mm, x_tok), sig="ve")
                ev_last[nt] = xcv
        sl = xcv

    st = emit_store_x(kb, cm, x_out, waits=(xcv,))
    kb.wait_all_dma()
    return kb





def build_l3f(weights):
    kb = KB("l3f")
    cm = Common(kb)
    x_in = kb.nc.dram_tensor("x_in", [D, T], F32, kind="ExternalInput").ap()
    x_out = kb.dram_out("x_out", [D, T], F32)
    GWd = kb.dram_in("w_gw", np.ascontiguousarray(np.asarray(weights["L3_ffn_gate_w"]).T))
    GBd = kb.dram_in("w_gb", np.asarray(weights["L3_ffn_gate_bias"]).reshape(1, 64))
    WEG = kb.dram_in("w_weg", np.asarray(weights["L3_ffn_We_gate"]).reshape(64 * 1024, 512))
    WEU = kb.dram_in("w_weu", np.asarray(weights["L3_ffn_We_up"]).reshape(64 * 1024, 512))
    WED = kb.dram_in("w_wed", np.asarray(weights["L3_ffn_We_down"]).reshape(64 * 512, 1024))
    WSG = kb.dram_in("w_wsg", np.asarray(weights["L3_ffn_Ws_gate"]))
    WSU = kb.dram_in("w_wsu", np.asarray(weights["L3_ffn_Ws_up"]))
    WSD = kb.dram_in("w_wsd", np.asarray(weights["L3_ffn_Ws_down"]))
    IDFd = kb.dram_in("c_idf", np.eye(128, dtype=np.float32))
    WDd = kb.dram_tmp("wd_scratch", [64, T], F32)

    GW = kb.sbuf("GW", [128, DC * 64], BF16)
    BR = kb.sbuf("BR", [1, 64], F32)
    BIASB = kb.sbuf("BIASB", [128, 64], F32)
    IDF = kb.sbuf("IDF", [128, 128], F32)
    SCO = kb.sbuf("SCO", [64, T], BF16)
    GT = kb.sbuf("GT", [128, 448], F32)
    WGT = [kb.sbuf("WGT0", [128, DC * 512], BF16), kb.sbuf("WGT1", [128, DC * 512], BF16)]
    WUP = [kb.sbuf("WUP0", [128, DC * 512], BF16), kb.sbuf("WUP1", [128, DC * 512], BF16)]
    WDN = [kb.sbuf("WDN0", [128, 4 * 1024], BF16), kb.sbuf("WDN1", [128, 4 * 1024], BF16)]
    BCW = [kb.sbuf("BCW0", [128, T], BF16), kb.sbuf("BCW1", [128, T], BF16)]
    S1 = kb.sbuf("S1", [128, 4 * 512], BF16)
    HM2 = kb.sbuf("HM2", [128, 4 * 512], BF16)
    WDS = kb.sbuf("WDS", [64, 256], F32)

    gt = lambda off, dims: ap_view(GT, [[448, 128]] + dims, off=off)
    psa = lambda slot, n=512: ap_view(cm.psA, [[T, 128], [1, n]], off=slot * 512)
    psb = lambda slot, n=512: ap_view(cm.psB, [[T, 128], [1, n]], off=slot * 512)

    x_tok = emit_load_x(kb, cm, x_in)
    gw_t = kb.dma("gpsimd", ap_view(GW, [[DC * 64, 128], [64, DC], [1, 64]]),
                  GWd.rearrange("(c p) j -> p c j", p=128))
    br_t = kb.dma("sync", v(BR), GBd)
    idf_t = kb.dma("sync", v(IDF), IDFd)
    ln_toks, psA_free, psB_free = emit_ln(kb, cm, x_tok)
    ln_last = ln_toks[-1]

    bb_mm = kb.emit("tensor", lambda e: e.matmul(
        ap_view(cm.psB, [[T, 128], [1, 64]]),
        ap_view(cm.ONESF, [[128, 1], [1, 128]]), v(BR),
        start=True, stop=True), waits=(br_t, psB_free), sig="te")
    bb_tok = kb.emit("vector", lambda e: e.tensor_copy(
        v(BIASB), ap_view(cm.psB, [[T, 128], [1, 64]])), waits=(bb_mm,), sig="ve")

    sg_tok = {}
    for nt in range(NT4):
        mm = None
        for c in range(DC):
            mm = kb.emit("tensor", lambda e, nt=nt, c=c: e.matmul(
                ap_view(cm.psA, [[T, 64], [1, 512]], off=nt * 512),
                ap_view(GW, [[DC * 64, 128], [1, 64]], off=c * 64),
                cm.xnc(c, nt * 512, 512),
                start=(c == 0), stop=(c == DC - 1)),
                waits=(gw_t, ln_last, psA_free), sig="te" if c == DC - 1 else None)
        sg_tok[nt] = kb.emit("scalar", lambda e, nt=nt: e.activation(
            ap_view(SCO, [[T, 64], [1, 512]], off=nt * 512),
            ap_view(cm.psA, [[T, 64], [1, 512]], off=nt * 512), AF.Sigmoid),
            waits=(mm,), sig="se")

    G3 = [[8, 8], [1, 8]]
    sbcp = {}
    wdd_dma = {}
    wtr = {}
    for tt in range(16):
        slot_f = (tt % 2)
        slot_w = 2 + (tt % 2)
        tp = kb.emit("tensor", lambda e, tt=tt, slot_f=slot_f: e.matmul(
            ap_view(cm.psB, [[T, 128], [1, 64]], off=slot_f * 512),
            ap_view(SCO, [[T, 64], [1, 128]], off=tt * 128),
            ap_view(cm.IDENT, [[128, 64], [1, 64]]),
            start=True, stop=True),
            waits=(idf_t, sg_tok[tt // 4], sbcp.get(tt - 2, bb_tok)), sig="te")
        sbcp[tt] = kb.emit("vector", lambda e, slot_f=slot_f: e.tensor_copy(
            gt(0, [[1, 64]]), psb(slot_f, 64)), waits=(tp,), sig="ve")
        kb.emit("vector", lambda e: e.tensor_add(
            gt(64, [[1, 64]]), gt(0, [[1, 64]]), v(BIASB)), waits=(bb_tok,))
        kb.emit("vector", lambda e: e.tensor_reduce(
            gt(384, [[1, 8]]), gt(64, G3), AX.X, ALU.max))
        kb.emit("vector", lambda e: e.tensor_tensor(
            gt(128, G3), gt(64, G3), gt(384, [[1, 8], [0, 8]]), ALU.is_ge))
        kb.emit("vector", lambda e: e.scalar_tensor_tensor(
            gt(192, G3), gt(128, G3), -1e9, gt(64, G3), ALU.mult, ALU.add))
        kb.emit("vector", lambda e: e.tensor_reduce(
            gt(392, [[1, 8]]), gt(192, G3), AX.X, ALU.max))
        kb.emit("vector", lambda e: e.tensor_add(
            gt(400, [[1, 8]]), gt(384, [[1, 8]]), gt(392, [[1, 8]])))
        kb.emit("vector", lambda e: e.tensor_copy(gt(408, [[1, 8]]), gt(400, [[1, 8]])))
        for r in range(4):
            kb.emit("vector", lambda e, r=r: e.tensor_reduce(
                gt(425 if r == 3 else 424, [[1, 1]]), gt(408, [[1, 8]]), AX.X, ALU.max))
            if r < 3:
                kb.emit("vector", lambda e: e.tensor_tensor(
                    gt(416, [[1, 8]]), gt(408, [[1, 8]]), gt(424, [[0, 8]]), ALU.is_ge))
                kb.emit("vector", lambda e: e.scalar_tensor_tensor(
                    gt(408, [[1, 8]]), gt(416, [[1, 8]]), -1e9, gt(408, [[1, 8]]),
                    ALU.mult, ALU.add))
        kb.emit("vector", lambda e: e.tensor_tensor(
            gt(416, [[1, 8]]), gt(400, [[1, 8]]), gt(425, [[0, 8]]), ALU.is_ge))
        kb.emit("vector", lambda e: e.tensor_tensor(
            gt(192, G3), gt(64, G3), gt(416, [[1, 8], [0, 8]]), ALU.mult))
        kb.emit("vector", lambda e: e.tensor_scalar(
            gt(432, [[1, 8]]), gt(416, [[1, 8]]), 1e9, -1e9, ALU.mult, ALU.add))
        kb.emit("vector", lambda e: e.tensor_tensor(
            gt(128, G3), gt(192, G3), gt(432, [[1, 8], [0, 8]]), ALU.add))
        kb.emit("vector", lambda e: e.tensor_copy(gt(192, [[1, 64]]), gt(128, [[1, 64]])))
        for r in range(8):
            kb.emit("vector", lambda e, r=r: e.tensor_reduce(
                gt(426 if r == 7 else 424, [[1, 1]]), gt(128, [[1, 64]]), AX.X, ALU.max))
            if r < 7:
                kb.emit("vector", lambda e: e.tensor_tensor(
                    gt(256, [[1, 64]]), gt(128, [[1, 64]]), gt(424, [[0, 64]]), ALU.is_ge))
                kb.emit("vector", lambda e: e.scalar_tensor_tensor(
                    gt(128, [[1, 64]]), gt(256, [[1, 64]]), -1e9, gt(128, [[1, 64]]),
                    ALU.mult, ALU.add))
        kb.emit("vector", lambda e: e.tensor_tensor(
            gt(256, [[1, 64]]), gt(192, [[1, 64]]), gt(426, [[0, 64]]), ALU.is_ge))
        kb.emit("vector", lambda e: e.tensor_tensor(
            gt(320, [[1, 64]]), gt(256, [[1, 64]]), gt(0, [[1, 64]]), ALU.mult))
        kb.emit("vector", lambda e: e.tensor_reduce(
            gt(427, [[1, 1]]), gt(320, [[1, 64]]), AX.X, ALU.add))
        kb.emit("vector", lambda e: e.tensor_scalar(
            gt(428, [[1, 1]]), gt(427, [[1, 1]]), 1.0, 1e-20, ALU.mult, ALU.add))
        rtk = kb.emit("vector", lambda e: e.reciprocal(
            gt(429, [[1, 1]]), gt(428, [[1, 1]])), sig="ve")
        kb.emit("vector", lambda e: e.tensor_scalar(
            gt(428, [[1, 1]]), gt(429, [[1, 1]]), 2.5, 0.0, ALU.mult, ALU.add),
            waits=(rtk,))
        wfin = kb.emit("vector", lambda e: e.tensor_tensor(
            gt(256, [[1, 64]]), gt(320, [[1, 64]]), gt(428, [[0, 64]]), ALU.mult),
            sig="ve")
        tp2 = kb.emit("tensor", lambda e, slot_w=slot_w: e.matmul(
            ap_view(cm.psB, [[T, 64], [1, 128]], off=slot_w * 512),
            gt(256, [[1, 64]]), v(IDF),
            start=True, stop=True),
            waits=(wfin, wtr.get(tt - 2)), sig="te")
        wtr[tt] = kb.emit("vector", lambda e, tt=tt, slot_w=slot_w: e.tensor_copy(
            ap_view(WDS, [[256, 64], [1, 128]], off=(tt % 2) * 128),
            ap_view(cm.psB, [[T, 64], [1, 128]], off=slot_w * 512)),
            waits=(tp2, wdd_dma.get(tt - 2)), sig="ve")
        wdd_dma[tt] = kb.dma("sync",
                             ap_view(WDd, [[T, 64], [1, 128]], off=tt * 128),
                             ap_view(WDS, [[256, 64], [1, 128]], off=(tt % 2) * 128),
                             waits=(wtr[tt],))
    wdd_last = wdd_dma[15]
    gate_ve_last = sbcp[15]

    free_g = {0: None, 1: None}
    free_u = {0: None, 1: None}
    free_d = {0: None, 1: None}
    free_b = {0: None, 1: None}
    xe_last = None
    sil_last = None
    dmm_last = None
    hm_prev = None
    for e in range(65):
        buf = e % 2
        shared = (e == 64)
        if shared:
            wg_t = kb.dma("gpsimd", ap_view(WGT[buf], [[DC * 512, 128], [512, DC], [1, 512]]),
                          ap_view(WSG, [[512, 128], [65536, DC], [1, 512]]),
                          waits=(free_g[buf],))
            wu_t = kb.dma("gpsimd", ap_view(WUP[buf], [[DC * 512, 128], [512, DC], [1, 512]]),
                          ap_view(WSU, [[512, 128], [65536, DC], [1, 512]]),
                          waits=(free_u[buf],))
            wd_t = kb.dma("gpsimd", ap_view(WDN[buf], [[4 * 1024, 128], [1024, 4], [1, 1024]]),
                          ap_view(WSD, [[1024, 128], [131072, 4], [1, 1024]]),
                          waits=(free_d[buf],))
            bc_t = None
        else:
            wg_t = kb.dma("gpsimd", ap_view(WGT[buf], [[DC * 512, 128], [512, DC], [1, 512]]),
                          ap_view(WEG, [[512, 128], [65536, DC], [1, 512]], off=e * 524288),
                          waits=(free_g[buf],))
            wu_t = kb.dma("gpsimd", ap_view(WUP[buf], [[DC * 512, 128], [512, DC], [1, 512]]),
                          ap_view(WEU, [[512, 128], [65536, DC], [1, 512]], off=e * 524288),
                          waits=(free_u[buf],))
            wd_t = kb.dma("gpsimd", ap_view(WDN[buf], [[4 * 1024, 128], [1024, 4], [1, 1024]]),
                          ap_view(WED, [[1024, 128], [131072, 4], [1, 1024]], off=e * 524288),
                          waits=(free_d[buf],))
            bc_t = kb.dma("gpsimd", v(BCW[buf]),
                          bass.AP(tensor=WDd.tensor, offset=e * T, ap=[[0, 128], [1, T]]),
                          waits=(wdd_last, free_b[buf]))
        for nt in range(NT4):
            first = (e == 0 and nt == 0)
            g_mm = {}
            u_mm = {}
            for fc in range(4):
                mm = None
                for c in range(DC):
                    mm = kb.emit("tensor", lambda e_, fc=fc, c=c, nt=nt, buf=buf: e_.matmul(
                        psa(fc),
                        ap_view(WGT[buf], [[DC * 512, 128], [1, 128]], off=c * 512 + fc * 128),
                        cm.xnc(c, nt * 512, 512),
                        start=(c == 0), stop=(c == DC - 1)),
                        waits=(wg_t, ln_last, gate_ve_last if first else xe_last),
                        sig="te" if c == DC - 1 else None)
                g_mm[fc] = mm
            for fc in range(4):
                mm = None
                for c in range(DC):
                    mm = kb.emit("tensor", lambda e_, fc=fc, c=c, nt=nt, buf=buf: e_.matmul(
                        psb(fc),
                        ap_view(WUP[buf], [[DC * 512, 128], [1, 128]], off=c * 512 + fc * 128),
                        cm.xnc(c, nt * 512, 512),
                        start=(c == 0), stop=(c == DC - 1)),
                        waits=(wu_t, wdd_last if first else None,),
                        sig="te" if c == DC - 1 else None)
                u_mm[fc] = mm
            hm_toks = []
            for fc in range(4):
                st = kb.emit("scalar", lambda e_, fc=fc: e_.activation(
                    ap_view(S1, [[4 * 512, 128], [1, 512]], off=fc * 512),
                    psa(fc), AF.Silu),
                    waits=(g_mm[fc], hm_prev, dmm_last), sig="se")
                sil_last = st
                if shared:
                    hmt = kb.emit("vector", lambda e_, fc=fc: e_.tensor_tensor(
                        ap_view(HM2, [[4 * 512, 128], [1, 512]], off=fc * 512),
                        ap_view(S1, [[4 * 512, 128], [1, 512]], off=fc * 512),
                        psb(fc), ALU.mult),
                        waits=(u_mm[fc], st, dmm_last), sig="ve")
                else:
                    kb.emit("vector", lambda e_, fc=fc: e_.tensor_tensor(
                        ap_view(HM2, [[4 * 512, 128], [1, 512]], off=fc * 512),
                        ap_view(S1, [[4 * 512, 128], [1, 512]], off=fc * 512),
                        psb(fc), ALU.mult),
                        waits=(u_mm[fc], st, dmm_last), sig="ve")
                    hmt = kb.emit("vector", lambda e_, fc=fc, nt=nt, buf=buf: e_.tensor_tensor(
                        ap_view(S1, [[4 * 512, 128], [1, 512]], off=fc * 512),
                        ap_view(HM2, [[4 * 512, 128], [1, 512]], off=fc * 512),
                        ap_view(BCW[buf], [[T, 128], [1, 512]], off=nt * 512),
                        ALU.mult),
                        waits=(bc_t,), sig="ve")
                hm_toks.append(hmt)
            hm_prev = hm_toks[-1]
            if not shared:
                free_b[buf] = hm_toks[-1]
            rhs_hm = HM2 if shared else S1
            d_mm = {}
            for oc in range(DC):
                mm = None
                for fc in range(4):
                    mm = kb.emit("tensor", lambda e_, oc=oc, fc=fc, buf=buf,
                                 rhs_hm=rhs_hm: e_.matmul(
                        psa(oc) if oc < 4 else psb(oc - 4),
                        ap_view(WDN[buf], [[4 * 1024, 128], [1, 128]], off=fc * 1024 + oc * 128),
                        ap_view(rhs_hm, [[4 * 512, 128], [1, 512]], off=fc * 512),
                        start=(fc == 0), stop=(fc == 3)),
                        waits=(wd_t, hm_toks[-1], sil_last),
                        sig="te" if fc == 3 else None)
                d_mm[oc] = mm
            dmm_last = d_mm[DC - 1]
            for oc in range(DC):
                xe_last = kb.emit("vector", lambda e_, oc=oc, nt=nt: e_.tensor_add(
                    cm.xc(oc, nt * 512, 512),
                    psa(oc) if oc < 4 else psb(oc - 4),
                    cm.xc(oc, nt * 512, 512)),
                    waits=(d_mm[oc],), sig="ve")
            if nt == NT4 - 1:
                free_g[buf] = g_mm[3]
                free_u[buf] = u_mm[3]
                free_d[buf] = d_mm[DC - 1]

    st = emit_store_x(kb, cm, x_out, waits=(xe_last,))
    kb.wait_all_dma()
    return kb


BLOCK_BUILDERS = (build_l0a, build_l0f, build_l1a, build_l1f,
                  build_l2a, build_l2f, build_l3a, build_l3f)


def kernel(**inputs):
    x = np.asarray(inputs["L0_attn_inputs"], dtype=np.float32)
    xs = [np.ascontiguousarray(x[b].T) for b in range(NCORE)]
    for build in BLOCK_BUILDERS:
        kb = build(inputs)
        res = kb.run([{"x_in": xs[b]} for b in range(NCORE)])
        xs = [np.ascontiguousarray(res.results[b]["x_out"]) for b in range(NCORE)]
    return np.stack([xb.T for xb in xs], axis=0).astype(np.float32)


# version 02 (final)
# speedup vs baseline: 1.0000x; 1.0000x over previous
import contextlib
import math
from collections import defaultdict

import numpy as np
import concourse.bass as bass
import concourse.mybir as mybir
from concourse.bass_utils import run_bass_kernel_spmd

F32 = mybir.dt.float32
BF16 = mybir.dt.bfloat16
U32 = mybir.dt.uint32
I16 = mybir.dt.int16
AF = mybir.ActivationFunctionType
ALU = mybir.AluOpType
AX = mybir.AxisListType

NCORE = 8
T = 2048
D = 1024
DC = D // 128
NT4 = T // 512
MAGIC = 12582912.0

ENGS = ("sync", "gpsimd", "tensor", "vector", "scalar")
DEBUG_L0F = False
L3A_LIMIT = 99
DEBUG_L3A = False
DEBUG_L1F = False
L1F_LIMIT = 99
L2A_LIMIT = 99


def v(x):
    return x if isinstance(x, bass.AP) else x[:]


def ap_view(t, dims, off=0):
    a = v(t)
    return bass.AP(tensor=a.tensor, offset=a.offset + off, ap=[list(d) for d in dims])


class KB:

    def __init__(self, name="prog"):
        self.name = name
        self.nc = bass.Bass("TRN2", target_bir_lowering=False)
        self.es = contextlib.ExitStack()
        self.tag = ""
        self.scopes = []
        self.bar = ()
        self.ops = {e: [] for e in ENGS}
        self.sems = {}
        self.cnt = defaultdict(int)
        self.last_wait = {e: defaultdict(int) for e in ENGS}
        self.inputs = {}
        self.outputs = []

    def push_scope(self):
        self.scopes.append(contextlib.ExitStack())

    def pop_scope(self):
        self.scopes.pop().close()

    def set_barrier(self):
        self.bar = (("ve", self.cnt["ve"]), ("se", self.cnt["se"]))

    def dram_in(self, name, arr):
        name = self.tag + name
        arr = np.ascontiguousarray(arr)
        self.inputs[name] = arr
        dt = {np.dtype(np.float32): F32, np.dtype(np.int16): I16,
              np.dtype(np.uint32): U32}[np.dtype(arr.dtype)]
        return self.nc.dram_tensor(name, list(arr.shape), dt, kind="ExternalInput").ap()

    def dram_out(self, name, shape, dtype=F32):
        name = self.tag + name
        self.outputs.append(name)
        return self.nc.dram_tensor(name, list(shape), dtype, kind="ExternalOutput").ap()

    def dram_tmp(self, name, shape, dtype):
        return self.nc.dram_tensor(self.tag + name, list(shape), dtype, kind="Internal").ap()

    def sbuf(self, name, shape, dtype):
        es = self.scopes[-1] if self.scopes else self.es
        return es.enter_context(self.nc.sbuf_tensor(self.tag + name, list(shape), dtype))

    def psum(self, name, shape, dtype=F32):
        es = self.scopes[-1] if self.scopes else self.es
        return es.enter_context(self.nc.psum_tensor(self.tag + name, list(shape), dtype))

    def sem(self, name):
        if name not in self.sems:
            self.sems[name] = self.es.enter_context(self.nc.semaphore(name))
        return self.sems[name]

    def emit(self, eng, fn, waits=(), sig=None, amt=1):
        if eng == "vector":
            if sig is None:
                sig = "ve"
            prev = self.cnt.get("ve", 0)
            waits = tuple(w for w in (waits if isinstance(waits, (list, tuple)) else (waits,))) + ((("ve", prev),) if prev else ())
        elif eng == "scalar":
            if sig is None:
                sig = "se"
            prev = self.cnt.get("se", 0)
            waits = tuple(w for w in (waits if isinstance(waits, (list, tuple)) else (waits,))) + ((("se", prev),) if prev else ())
        wl = []
        for tk in (waits if isinstance(waits, (list, tuple)) else (waits,)):
            if tk is None:
                continue
            s, val = tk
            if s.startswith("q_"):
                val = self.cnt[s]
            if val > self.last_wait[eng][s]:
                self.last_wait[eng][s] = val
                wl.append((s, val))
        tok = None
        if sig is not None:
            self.sem(sig)
            self.cnt[sig] += amt
            tok = (sig, self.cnt[sig])
        for s, _ in wl:
            self.sem(s)

        def closure(engobj, wl=wl, fn=fn, sig=sig, amt=amt):
            for s, valu in wl:
                engobj.wait_ge(self.sems[s], valu)
            inst = fn(engobj)
            if sig is not None:
                inst.then_inc(self.sems[sig], amt)
        self.ops[eng].append(closure)
        return tok

    def dma(self, eng, out, in_, waits=(), sig=None, **kw):
        waits = tuple(waits) + self.bar
        if sig is None:
            sig = "q_" + eng
        return self.emit(eng, lambda e: e.dma_start(out=out, in_=in_, **kw),
                         waits=waits, sig=sig, amt=16)

    def wait_all_dma(self):
        for qe in ("sync", "gpsimd", "scalar", "vector"):
            s = "q_" + qe
            if self.cnt.get(s):
                cntv = self.cnt[s]
                self.emit(qe, lambda e: e.nop(), waits=((s, cntv),))

    def finalize(self):
        block = self.es.enter_context(self.nc.Block())
        for eng in ENGS:
            lst = self.ops[eng]
            if not lst:
                continue

            def make(lst):
                def body(engobj):
                    for f in lst:
                        f(engobj)
                return body
            getattr(block, eng)(make(lst))
        self.es.close()
        return self.nc

    def run(self, per_core_inputs, core_ids=None, trace=False):
        nc = self.finalize()
        in_maps = []
        for ci in per_core_inputs:
            m = dict(self.inputs)
            m.update(ci)
            in_maps.append(m)
        res = run_bass_kernel_spmd(nc, in_maps,
                                   core_ids=core_ids or list(range(len(in_maps))),
                                   trace=trace)
        return res


class Common:

    def __init__(self, kb: KB, xn_dtype=BF16, need_apl=False, psum_plan="f8",
                 base=None):
        self.kb = kb
        if base is not None:
            self.X = base.X
            self.XN = base.XN
            self.xn_dtype = base.xn_dtype
            self.SQ = base.SQ
            self.BC1 = base.BC1
            self.BC2 = base.BC2
            self.IDENT = base.IDENT
            self.ONESC = base.ONESC
            self.ONESF = base.ONESF
            self.const_tok = base.const_tok
            self.const_tok_gp = base.const_tok_gp
        else:
            self.X = kb.sbuf("X", [128, DC * T], F32)
            self.XN = kb.sbuf("XN", [128, DC * T], xn_dtype)
            self.xn_dtype = xn_dtype
            self.SQ = kb.sbuf("SQ", [128, T], BF16)
            self.BC1 = kb.sbuf("BC1", [128, T], F32)
            self.BC2 = kb.sbuf("BC2", [128, T], F32)
            self.IDENT = kb.sbuf("IDENT", [128, 128], BF16)
            self.ONESC = kb.sbuf("ONESC", [128, 1], BF16)
            self.ONESF = kb.sbuf("ONESF", [128, 128], F32)
            ident = kb.dram_in("c_ident", np.eye(128, dtype=np.float32))
            onesall = kb.dram_in("c_onesall", np.ones((128, 128), np.float32))
            onescol = kb.dram_in("c_onescol", np.ones((128, 1), np.float32))
            kb.dma("gpsimd", v(self.IDENT), ident)
            kb.dma("gpsimd", v(self.ONESC), onescol)
            self.const_tok = kb.dma("sync", v(self.ONESF), onesall)
            self.const_tok_gp = ("q_gpsimd", kb.cnt["q_gpsimd"])
        self.APL = kb.sbuf("APL", [128, T], F32) if need_apl else None
        if psum_plan is None:
            self.psA = self.psB = self.psT = None
        elif psum_plan == "mixed":
            self.psA = kb.psum("psA", [128, T], F32)
            self.psB = kb.psum("psB", [128, 512], F32)
            self.psT = kb.psum("psT", [128, 2048], BF16)
        else:
            self.psA = kb.psum("psA", [128, T], F32)
            self.psB = kb.psum("psB", [128, T], F32)
            self.psT = None

    def xc(self, c, t0=0, tn=T):
        return ap_view(self.X, [[DC * T, 128], [1, tn]], off=c * T + t0)

    def xnc(self, c, t0=0, tn=T):
        return ap_view(self.XN, [[DC * T, 128], [1, tn]], off=c * T + t0)

    def row(self, seg, t0=0, tn=T):
        t = self.BC1 if seg == 1 else self.BC2
        return ap_view(t, [[T, 1], [1, tn]], off=t0)


def _blk_head(fz, tag, psum_plan="f8", need_apl=False):
    if fz is None:
        kb = KB(tag)
        cm = Common(kb, psum_plan=psum_plan, need_apl=need_apl)
        x_in = kb.nc.dram_tensor("x_in", [D, T], F32, kind="ExternalInput").ap()
        x_out = kb.dram_out("x_out", [D, T], F32)
        return kb, cm, x_in, x_out
    kb = fz["kb"]
    kb.tag = tag + "_"
    kb.push_scope()
    kb.set_barrier()
    cm = Common(kb, psum_plan=psum_plan, need_apl=need_apl, base=fz["base"])
    return kb, cm, None, None


def _blk_tail(kb, cm, fz, x_out, tok):
    if fz is None:
        emit_store_x(kb, cm, x_out, waits=(tok,))
        kb.wait_all_dma()
        return kb
    kb.pop_scope()
    return tok


def emit_load_x(kb, cm, x_dram):
    src = x_dram.rearrange("(c p) t -> p c t", p=128)
    dst = ap_view(cm.X, [[DC * T, 128], [T, DC], [1, T]])
    return kb.dma("sync", dst, src)


def emit_store_x(kb, cm, x_out_dram, waits):
    src = ap_view(cm.X, [[DC * T, 128], [T, DC], [1, T]])
    dst = x_out_dram.rearrange("(c p) t -> p c t", p=128)
    return kb.dma("sync", dst, src, waits=waits)


def emit_ln(kb, cm, x_tok, out_fn=None, gamma_beta=None, chunk_hook=None, entry=None,
            apply_extra=None):
    out_fn = out_fn or cm.xnc
    psA = cm.psA

    cast_toks = []
    for c in range(DC):
        tk = kb.emit("vector", lambda e, c=c: e.tensor_copy(cm.xnc(c), cm.xc(c)),
                     waits=(x_tok, cm.const_tok_gp), sig="ve")
        cast_toks.append(tk)
    last_mm = None
    for c in range(DC):
        for g in range(4):
            last_mm = kb.emit("tensor",
                              lambda e, c=c, g=g: e.matmul(
                                  ap_view(psA, [[T, 1], [1, 512]], off=g * 512),
                                  v(cm.ONESC), cm.xnc(c, g * 512, 512),
                                  start=(c == 0), stop=(c == DC - 1)),
                              waits=(cast_toks[c], entry) if (c == 0 and g == 0)
                              else (cast_toks[c],),
                              sig="te" if (c == DC - 1 and g == 3) else None)
    mean_r = cm.row(0)
    var_r = cm.row(1)
    psA_row = ap_view(psA, [[T, 1], [1, T]])
    mtk = kb.emit("vector", lambda e: e.tensor_scalar_mul(mean_r, psA_row, 1.0 / D),
                  waits=(last_mm,), sig="ve")
    sq_done = None
    for c in range(DC):
        sq_tok = kb.emit("scalar",
                         lambda e, c=c: e.activation(v(cm.SQ), cm.xnc(c), AF.Square),
                         waits=(cast_toks[c], sq_done), sig="se")
        for g in range(4):
            last_mm = kb.emit("tensor",
                              lambda e, c=c, g=g: e.matmul(
                                  ap_view(psA, [[T, 1], [1, 512]], off=g * 512),
                                  v(cm.ONESC),
                                  ap_view(cm.SQ, [[T, 128], [1, 512]], off=g * 512),
                                  start=(c == 0), stop=(c == DC - 1)),
                              waits=(sq_tok, mtk),
                              sig="te" if g == 3 else None)
        sq_done = last_mm
    vtk = kb.emit("vector", lambda e: e.tensor_scalar(var_r, psA_row, 1.0 / D, 1e-5,
                                                      ALU.mult, ALU.add),
                  waits=(last_mm,), sig="ve")
    sq2 = kb.emit("scalar", lambda e: e.activation(psA_row, mean_r, AF.Square),
                  waits=(mtk, vtk), sig="se")
    tk = kb.emit("vector", lambda e: e.tensor_sub(var_r, var_r, psA_row), waits=(sq2,),
                 sig="ve")
    tk = kb.emit("scalar", lambda e: e.activation(var_r, var_r, AF.Sqrt),
                 waits=(tk,), sig="se")
    tk = kb.emit("vector", lambda e: e.reciprocal(var_r, var_r), waits=(tk,), sig="ve")
    mr_tok = kb.emit("vector", lambda e: e.tensor_mul(mean_r, mean_r, var_r), sig="ve")
    for g in range(4):
        kb.emit("tensor", lambda e, g=g: e.matmul(
            ap_view(psA, [[T, 128], [1, 512]], off=g * 512),
            v(cm.ONESF)[0:1, :], cm.row(1, g * 512, 512), start=True, stop=True),
            waits=(tk, mr_tok, cm.const_tok), sig="te" if g == 3 else None)
    b1tok = ("te", kb.cnt["te"])
    e1 = kb.emit("scalar", lambda e: e.copy(v(cm.BC1), v(psA)), waits=(b1tok,), sig="se")
    for g in range(4):
        kb.emit("tensor", lambda e, g=g: e.matmul(
            ap_view(psA, [[T, 128], [1, 512]], off=g * 512),
            v(cm.ONESF)[0:1, :], cm.row(0, g * 512, 512), start=True, stop=True),
            waits=(e1,), sig="te" if g == 3 else None)
    b2tok = ("te", kb.cnt["te"])
    e2 = kb.emit("scalar", lambda e: e.copy(v(cm.BC2), v(psA)), waits=(b2tok,), sig="se")
    toks = []
    for c in range(DC):
        tmp = v(cm.APL) if cm.APL is not None else out_fn(c)
        kb.emit("vector", lambda e, c=c, tmp=tmp: e.tensor_mul(tmp, cm.xc(c), v(cm.BC1)),
                waits=(e1, e2, apply_extra))
        if gamma_beta is None:
            tk = kb.emit("vector", lambda e, c=c, tmp=tmp: e.tensor_sub(out_fn(c), tmp,
                                                                        v(cm.BC2)), sig="ve")
        else:
            ga, be = gamma_beta
            kb.emit("vector", lambda e, c=c, tmp=tmp: e.tensor_sub(tmp, tmp, v(cm.BC2)))
            tk = kb.emit("vector", lambda e, c=c, tmp=tmp, ga=ga, be=be: e.tensor_scalar(
                out_fn(c), tmp,
                ap_view(ga, [[v(ga).ap[0][0], 128], [1, 1]], off=c),
                ap_view(be, [[v(be).ap[0][0], 128], [1, 1]], off=c),
                ALU.mult, ALU.add), sig="ve")
        toks.append(tk)
        if chunk_hook is not None:
            chunk_hook(c, tk)
    return toks, e2, None



def build_l0a(weights, fz=None):
    kb, cm, x_in, x_out = _blk_head(fz, "l0a", need_apl=True)
    A = kb.dram_in("w_A", weights["L0_attn_A"].reshape(DC, 128).T)
    BC = kb.dram_in("w_BC", weights["L0_attn_BC"])

    XNF = kb.sbuf("XNF", [128, T], F32)
    HS = kb.sbuf("HS", [128, DC * T], BF16)
    A_sb = kb.sbuf("A_sb", [128, DC], F32)
    ABC = kb.sbuf("ABC", [128, T], F32)
    WB = [kb.sbuf("WB0", [128, DC * 128], BF16), kb.sbuf("WB1", [128, DC * 128], BF16)]

    x_tok = emit_load_x(kb, cm, x_in) if fz is None else fz["x_tok"]
    a_tok = kb.dma("sync", v(A_sb), A)

    def xnf(c, t0=0, tn=T):
        return ap_view(XNF, [[T, 128], [1, tn]], off=t0)

    def hs(c, t0=0, tn=T):
        return ap_view(HS, [[DC * T, 128], [1, tn]], off=c * T + t0)

    scan_toks = []

    def scan_hook(c, ln_tok):
        kb.emit("vector", lambda e, c=c: e.tensor_copy(
            v(ABC), ap_view(A_sb, [[DC, 128], [0, T]], off=c)),
            waits=(a_tok,))
        tk = kb.emit("vector", lambda e, c=c: e.tensor_tensor_scan(
            hs(c), v(ABC), xnf(c), 0.0, ALU.mult, ALU.add), sig="ve")
        scan_toks.append(tk)

    ln_toks, psA_free0, psB_free0 = emit_ln(kb, cm, x_tok, out_fn=xnf,
                                            chunk_hook=scan_hook)

    wtok = {}
    buf_free = {0: None, 1: None}
    ps_free = {0: psA_free0, 1: psA_free0}
    res_tok = None
    for f in range(DC):
        buf = WB[f % 2]
        src = ap_view(BC, [[D, 128], [128 * D, DC], [1, 128]], off=f * 128)
        wtok[f] = kb.dma("gpsimd", ap_view(buf, [[DC * 128, 128], [128, DC], [1, 128]]),
                         src, waits=(buf_free[f % 2],))
        mm_tok = None
        for ts in range(NT4):
            sl = ts % 2
            for dc in range(DC):
                mm_tok = kb.emit("tensor", lambda e, ts=ts, dc=dc, buf=buf, sl=sl: e.matmul(
                    ap_view(cm.psA, [[T, 128], [1, 512]], off=sl * 1024),
                    ap_view(buf, [[DC * 128, 128], [1, 128]], off=dc * 128),
                    hs(dc, ts * 512, 512),
                    start=(dc == 0), stop=(dc == DC - 1)),
                    waits=(wtok[f], scan_toks[dc], ps_free[sl]),
                    sig="te" if dc == DC - 1 else None)
            res_tok = kb.emit("vector", lambda e, f=f, ts=ts, sl=sl: e.tensor_add(
                cm.xc(f, ts * 512, 512), cm.xc(f, ts * 512, 512),
                ap_view(cm.psA, [[T, 128], [1, 512]], off=sl * 1024)),
                waits=(mm_tok,), sig="ve")
            ps_free[sl] = res_tok
        buf_free[f % 2] = mm_tok

    return _blk_tail(kb, cm, fz, x_out, res_tok)



def make_butterfly_arrays(w):
    Slow = np.zeros((7, 128, 32 * 128), np.float32)
    p = np.arange(128)
    for j, (sig, tr) in enumerate([(6, True), (5, False), (4, False), (3, False),
                                   (2, False), (1, False), (0, False)]):
        s2 = 1 << sig
        for c in range(32):
            f = c * 128 + p
            k = (f >> (sig + 1)) * s2 + (f & (s2 - 1))
            b = (f >> sig) & 1
            S = np.zeros((128, 128), np.float32)
            wd = np.where(b == 0, w[sig, k, 0, 0], w[sig, k, 1, 1])
            wo = np.where(b == 0, w[sig, k, 0, 1], w[sig, k, 1, 0])
            S[p, p] = wd
            S[p, p + np.where(b == 0, s2, -s2)] = wo
            if tr:
                S = np.ascontiguousarray(S.T)
            Slow[j, :, c * 128:(c + 1) * 128] = S
    HD = np.zeros((5, 128, 32 * 128), np.float32)
    HO = np.zeros((5, 128, 32 * 128), np.float32)
    for i, sig in enumerate(range(7, 12)):
        s2 = 1 << sig
        for c in range(32):
            f = c * 128 + p
            k = (f >> (sig + 1)) * s2 + (f & (s2 - 1))
            d = (c >> (sig - 7)) & 1
            HD[i, p, c * 128 + p] = w[sig, k, d, d]
            HO[i, p, c * 128 + p] = w[sig, k, d, 1 - d]
    return Slow, HD, HO


def _bf_build_M(kb, cm, Slow_dram, SS, M, psum, mtoks, free_tok):
    t0 = kb.dma("gpsimd", v(M), ap_view(Slow_dram, [[4096, 128], [1, 4096]]),
                waits=(free_tok,))
    prev = [t0]
    ev_tok = None
    for j in range(1, 7):
        stok = kb.dma("gpsimd", v(SS),
                      ap_view(Slow_dram, [[4096, 128], [1, 4096]], off=j * 128 * 4096),
                      waits=tuple(prev))
        for c in range(32):
            mm = kb.emit("tensor", lambda e, c=c: e.matmul(
                ap_view(psum, [[T, 128], [1, 128]], off=(c % 4) * 512),
                ap_view(SS, [[32 * 128, 128], [1, 128]], off=c * 128),
                ap_view(M, [[32 * 128, 128], [1, 128]], off=c * 128),
                start=True, stop=True),
                waits=(stok, (ev_tok if c >= 4 else None), mtoks), sig="te")
            ev_tok = kb.emit("vector", lambda e, c=c: e.tensor_copy(
                ap_view(M, [[32 * 128, 128], [1, 128]], off=c * 128),
                ap_view(psum, [[T, 128], [1, 128]], off=(c % 4) * 512)),
                waits=(mm,), sig="ve")
        prev = [ev_tok]
    return ev_tok


def build_l0f(weights, fz=None):
    kb, cm, x_in, x_out = _blk_head(fz, "l0f")
    Wexp = kb.dram_in("w_Wexp", weights["L0_ffn_W_exp"])
    Wcon = kb.dram_in("w_Wcon", weights["L0_ffn_W_con"])
    Slow1, HD1, HO1 = make_butterfly_arrays(np.asarray(weights["L0_ffn_up_weights"]))
    Slow2, HD2, HO2 = make_butterfly_arrays(np.asarray(weights["L0_ffn_down_weights"]))
    SL1 = kb.dram_in("w_sl1", Slow1)
    SL2 = kb.dram_in("w_sl2", Slow2)
    HD1d = kb.dram_in("w_hd1", HD1)
    HO1d = kb.dram_in("w_ho1", HO1)
    HD2d = kb.dram_in("w_hd2", HD2)
    HO2d = kb.dram_in("w_ho2", HO2)
    bexp = kb.dram_in("w_bexp", weights["L0_ffn_b_exp"].reshape(32, 128).T)
    upb = kb.dram_in("w_upb", weights["L0_ffn_up_bias"].reshape(32, 128).T)
    bcon = kb.dram_in("w_bcon", weights["L0_ffn_b_con"].reshape(8, 128).T)
    dnb = kb.dram_in("w_dnb", weights["L0_ffn_down_bias"].reshape(8, 128).T)

    H = kb.sbuf("H", [128, 32 * 512], BF16)
    M1 = kb.sbuf("M1", [128, 32 * 128], BF16)
    M2 = kb.sbuf("M2", [128, 32 * 128], BF16)
    SS = kb.sbuf("SS", [128, 32 * 128], BF16)
    DD = SS
    OO = kb.sbuf("OO", [128, 32 * 128], BF16)
    WB = [kb.sbuf("WB0", [128, DC * 128], BF16), kb.sbuf("WB1", [128, DC * 128], BF16)]
    BIA = kb.sbuf("BIA", [128, 32 * 4], F32)

    def hap(c, t0=0, tn=512):
        return ap_view(H, [[32 * 512, 128], [1, tn]], off=c * 512 + t0)

    x_tok = emit_load_x(kb, cm, x_in) if fz is None else fz["x_tok"]
    b1 = kb.dma("sync", ap_view(BIA, [[128, 128], [1, 32]]), bexp)
    b2 = kb.dma("sync", ap_view(BIA, [[128, 128], [1, 32]], off=32), upb)
    b3 = kb.dma("sync", ap_view(BIA, [[128, 128], [1, 8]], off=64), bcon)
    b4 = kb.dma("sync", ap_view(BIA, [[128, 128], [1, 8]], off=72), dnb)
    bc_tok = kb.emit("vector", lambda e: e.tensor_add(
        ap_view(BIA, [[128, 128], [1, 8]], off=64),
        ap_view(BIA, [[128, 128], [1, 8]], off=64),
        ap_view(BIA, [[128, 128], [1, 8]], off=72)), waits=(b3, b4), sig="ve")

    ln_toks, psA_free, psB_free = emit_ln(kb, cm, x_tok)
    ln_last = ln_toks[-1]
    if DEBUG_L0F:
        dxn = kb.dram_out("dbg_xn", [128, DC * T], F32)
        ln_last = kb.dma("gpsimd", dxn, ap_view(cm.XN, [[DC * T, 128], [1, DC * T]]),
                         waits=(ln_last,), sig="ve")

    m1_tok = _bf_build_M(kb, cm, SL1, SS, M1, cm.psA, psA_free, None)
    m2_tok = _bf_build_M(kb, cm, SL2, SS, M2, cm.psA, m1_tok, None)

    wtok = {}
    buf_free = {0: None, 1: None}
    res_tok = None
    for tt in range(NT4):
        t0 = tt * 512
        h_toks = {}
        for f in range(32):
            buf = WB[f % 2]
            src = ap_view(Wexp, [[4096, 128], [128 * 4096, DC], [1, 128]], off=f * 128)
            wtok[f] = kb.dma("gpsimd", ap_view(buf, [[DC * 128, 128], [128, DC], [1, 128]]),
                             src, waits=(buf_free[f % 2],))
            mm = None
            for dc in range(DC):
                mm = kb.emit("tensor", lambda e, f=f, dc=dc, buf=buf, t0=t0: e.matmul(
                    ap_view(cm.psA, [[T, 128], [1, 512]], off=(f % 4) * 512),
                    ap_view(buf, [[DC * 128, 128], [1, 128]], off=dc * 128),
                    cm.xnc(dc, t0, 512),
                    start=(dc == 0), stop=(dc == DC - 1)),
                    waits=(wtok[f], ln_last, m2_tok,
                           h_toks.get(f - 4), psA_free),
                    sig="te" if dc == DC - 1 else None)
            buf_free[f % 2] = mm
            h_toks[f] = kb.emit("vector", lambda e, f=f: e.tensor_scalar_add(
                hap(f), ap_view(cm.psA, [[T, 128], [1, 512]], off=(f % 4) * 512),
                ap_view(BIA, [[128, 128], [1, 1]], off=f)),
                waits=(mm, b1), sig="ve")

        def butterfly(M, HDd, HOd, hin_tok):
            lows = {}
            for c in range(32):
                mm = kb.emit("tensor", lambda e, c=c: e.matmul(
                    ap_view(cm.psB, [[T, 128], [1, 512]], off=(c % 4) * 512),
                    ap_view(M, [[32 * 128, 128], [1, 128]], off=c * 128),
                    hap(c), start=True, stop=True),
                    waits=(hin_tok, lows.get(c - 4), psB_free), sig="te")
                lows[c] = kb.emit("vector", lambda e, c=c: e.tensor_copy(
                    hap(c), ap_view(cm.psB, [[T, 128], [1, 512]], off=(c % 4) * 512)),
                    waits=(mm,), sig="ve")
            last = lows[31]
            for sigma in range(7, 12):
                cd = 1 << (sigma - 7)
                si = sigma - 7
                dtoks = [
                    kb.dma("gpsimd", v(DD),
                           ap_view(HDd, [[4096, 128], [1, 4096]], off=si * 128 * 4096),
                           waits=(last,)),
                    kb.dma("gpsimd", v(OO),
                           ap_view(HOd, [[4096, 128], [1, 4096]], off=si * 128 * 4096),
                           waits=(last,))]
                pair_ev = {}
                done = []
                for c0 in range(0, 32, 2 * cd):
                    for cl in range(cd):
                        ca, cb = c0 + cl, c0 + cl + cd
                        for (cc, slot) in ((ca, 0), (cb, 1)):
                            mm = kb.emit("tensor", lambda e, cc=cc, ca=ca, cb=cb, slot=slot: (
                                e.matmul(ap_view(cm.psB, [[T, 128], [1, 512]], off=slot * 512),
                                         ap_view(DD, [[32 * 128, 128], [1, 128]], off=cc * 128),
                                         hap(cc), start=True, stop=False)),
                                waits=tuple(dtoks) + (last, pair_ev.get(slot),))
                            mm = kb.emit("tensor", lambda e, cc=cc, ca=ca, cb=cb, slot=slot: (
                                e.matmul(ap_view(cm.psB, [[T, 128], [1, 512]], off=slot * 512),
                                         ap_view(OO, [[32 * 128, 128], [1, 128]], off=cc * 128),
                                         hap(cb if cc == ca else ca), start=False, stop=True)),
                                sig="te")
                        for (cc, slot) in ((ca, 0), (cb, 1)):
                            tk = kb.emit("vector", lambda e, cc=cc, slot=slot: e.tensor_copy(
                                hap(cc), ap_view(cm.psB, [[T, 128], [1, 512]], off=slot * 512)),
                                waits=(mm,), sig="ve")
                            pair_ev[slot] = tk
                            done.append(tk)
                last = done[-1]
            return last

        if DEBUG_L0F and tt == 0:
            d0 = kb.dram_out("dbg_h0", [128, 32 * 512], F32)
            h_toks[31] = kb.dma("gpsimd", d0, ap_view(H, [[32 * 512, 128], [1, 32 * 512]]),
                                waits=(h_toks[31],), sig="ve")
        hb = butterfly(M1, HD1d, HO1d, h_toks[31])
        if DEBUG_L0F and tt == 0:
            d1 = kb.dram_out("dbg_h1", [128, 32 * 512], F32)
            hb = kb.dma("gpsimd", d1, ap_view(H, [[32 * 512, 128], [1, 32 * 512]]),
                        waits=(hb,), sig="ve")
        g_toks = []
        for c in range(32):
            tk = kb.emit("scalar", lambda e, c=c: e.activation(
                hap(c), hap(c), AF.Gelu,
                bias=ap_view(BIA, [[128, 128], [1, 1]], off=32 + c)),
                waits=(hb, b2), sig="se")
            g_toks.append(tk)
        if DEBUG_L0F and tt == 0:
            d2 = kb.dram_out("dbg_h2", [128, 32 * 512], F32)
            g_toks[-1] = kb.dma("gpsimd", d2, ap_view(H, [[32 * 512, 128], [1, 32 * 512]]),
                                waits=(g_toks[-1],), sig="ve")
        hb2 = butterfly(M2, HD2d, HO2d, g_toks[-1])
        if DEBUG_L0F and tt == 0:
            d3 = kb.dram_out("dbg_h3", [128, 32 * 512], F32)
            hb2 = kb.dma("gpsimd", d3, ap_view(H, [[32 * 512, 128], [1, 32 * 512]]),
                         waits=(hb2,), sig="ve")

        def conps(f):
            return (ap_view(cm.psA, [[T, 128], [1, 512]], off=f * 512) if f < 4
                    else ap_view(cm.psB, [[T, 128], [1, 512]], off=(f - 4) * 512))
        psA_free = h_toks[31]
        psB_free = hb2
        mm_last = None
        for c in range(32):
            buf = WB[c % 2]
            src = ap_view(Wcon, [[1024, 128], [1, 1024]], off=c * 128 * 1024)
            wtok[100 + c] = kb.dma("gpsimd",
                                   ap_view(buf, [[DC * 128, 128], [1, 1024]]),
                                   src, waits=(buf_free[c % 2],))
            for f in range(DC):
                mm_last = kb.emit("tensor", lambda e, f=f, c=c, buf=buf: e.matmul(
                    conps(f),
                    ap_view(buf, [[DC * 128, 128], [1, 128]], off=f * 128),
                    hap(c), start=(c == 0), stop=(c == 31)),
                    waits=(wtok[100 + c], hb2, psA_free, psB_free),
                    sig="te" if f == DC - 1 else None)
            buf_free[c % 2] = mm_last
        con_toks = {}
        for f in range(DC):
            con_toks[f] = kb.emit("vector", lambda e, f=f, t0=t0: e.scalar_tensor_tensor(
                cm.xc(f, t0, 512), conps(f),
                ap_view(BIA, [[128, 128], [1, 1]], off=64 + f),
                cm.xc(f, t0, 512), ALU.add, ALU.add),
                waits=(mm_last, bc_tok), sig="ve")
        res_tok = con_toks[DC - 1]
        psA_free = res_tok
        psB_free = res_tok

    return _blk_tail(kb, cm, fz, x_out, res_tok)



def _l3a_perm():
    perm = []
    for g in range(4):
        for par in range(2):
            for hl in range(4):
                for i in range(32):
                    perm.append((4 * g + hl) * 64 + 2 * i + par)
    return np.array(perm)


def build_l3a(weights, fz=None):
    kb, cm, x_in, x_out = _blk_head(fz, "l3a")
    perm = _l3a_perm()
    Wq = kb.dram_in("w_Wq", np.asarray(weights["L3_attn_Wq"])[:, perm])
    Wk = kb.dram_in("w_Wk", np.asarray(weights["L3_attn_Wk"])[:, perm])
    Wv = kb.dram_in("w_Wv", weights["L3_attn_Wv"])
    W0 = kb.dram_in("w_W0", weights["L3_attn_W0"])
    bq = kb.dram_in("w_bq", np.asarray(weights["L3_attn_bq"])[perm].reshape(8, 128).T)
    bk = kb.dram_in("w_bk", np.asarray(weights["L3_attn_bk"])[perm].reshape(8, 128).T)
    bv = kb.dram_in("w_bv", np.asarray(weights["L3_attn_bv"]).reshape(1, 1024))
    b0 = kb.dram_in("w_b0", np.asarray(weights["L3_attn_b0"]).reshape(8, 128).T)
    invf = (1.0 / 10000.0 ** (np.arange(0, 64, 2, dtype=np.float64) / 64.0))
    ang = np.arange(T, dtype=np.float64)[None, :] * invf[:, None]
    cosT = np.cos(ang).astype(np.float32)
    sinT = np.sin(ang).astype(np.float32)
    Cd = kb.dram_in("c_cos", np.tile(cosT, (4, 1)))
    Sd = kb.dram_in("c_sin", np.tile(sinT, (4, 1)))

    VW = 16 * 65
    QR = kb.sbuf("QR", [128, 2 * T], BF16)
    KR = kb.sbuf("KR", [128, 2 * T], BF16)
    CT = kb.sbuf("CT", [128, T], BF16)
    ST = kb.sbuf("ST", [128, T], BF16)
    VT = kb.sbuf("VT", [128, 16 * VW], BF16)
    E4 = kb.sbuf("E4", [128, 2048], BF16)
    STG = kb.sbuf("STG", [128, 512], F32)
    OSTG = kb.sbuf("OSTG", [128, 2048], BF16)
    BVB = kb.sbuf("BVB", [128, 1024], BF16)
    BIA = kb.sbuf("BIA", [128, 3 * DC], F32)
    WB = [kb.sbuf("WB0", [128, DC * 128], BF16), kb.sbuf("WB1", [128, DC * 128], BF16)]
    if DEBUG_L3A:
        OTd = kb.dram_out("ot_scratch", [16, 64, T], BF16)
        dbg_qr = kb.dram_out("dbg_qr", [128, 2 * T], F32)
        dbg_kr = kb.dram_out("dbg_kr", [128, 2 * T], F32)
    else:
        OTd = kb.dram_tmp("ot_scratch", [16, 64, T], BF16)

    x_tok = emit_load_x(kb, cm, x_in) if fz is None else fz["x_tok"]
    c1 = kb.dma("gpsimd", v(CT), Cd)
    c2 = kb.dma("gpsimd", v(ST), Sd)
    bq_t = kb.dma("sync", ap_view(BIA, [[3 * DC, 128], [1, DC]]), bq)
    bk_t = kb.dma("sync", ap_view(BIA, [[3 * DC, 128], [1, DC]], off=DC), bk)
    b0_t = kb.dma("sync", ap_view(BIA, [[3 * DC, 128], [1, DC]], off=2 * DC), b0)

    ln_toks, psA_free, psB_free = emit_ln(kb, cm, x_tok)
    ln_last = ln_toks[-1]
    bv_t = kb.dma("sync", ap_view(cm.BC1, [[T, 1], [1, 1024]]), bv,
                  waits=(ln_last, psA_free, psB_free))

    bb = []
    for g2 in range(2):
        bb.append(kb.emit("tensor", lambda e, g2=g2: e.matmul(
            ap_view(cm.psB, [[T, 128], [1, 512]], off=g2 * 512),
            v(cm.ONESF)[0:1, :], cm.row(1, g2 * 512, 512), start=True, stop=True),
            waits=(bv_t, psB_free, cm.const_tok), sig="te"))
    bvb_tok = kb.emit("scalar", lambda e: e.copy(v(BVB), ap_view(cm.psB, [[T, 128], [1, 1024]])),
                      waits=(bb[-1],), sig="se")

    on_tok = kb.emit("vector", lambda e: e.memset(
        ap_view(VT, [[16 * VW, 128], [VW, 16], [65, 16], [1, 1]], off=64), 1.0), sig="ve")
    on2_tok = on_tok
    wtok = {}
    buf_free = {0: None, 1: None}
    vt_toks = {}

    def vps(sl, nh):
        return (ap_view(cm.psA, [[T, 128], [1, 512]], off=sl * 1024 + nh * 512) if sl < 2
                else ap_view(cm.psB, [[T, 128], [1, 512]], off=(sl - 2) * 1024 + nh * 512))
    for tg in range(4):
        mm = None
        for c in range(DC):
            buf = WB[c % 2]
            wtok[(tg, c)] = kb.dma("gpsimd", ap_view(buf, [[DC * 128, 128], [1, 1024]]),
                                   ap_view(Wv, [[1024, 128], [1, 1024]], off=c * 128 * 1024),
                                   waits=(buf_free[c % 2],))
            for sl in range(4):
                tt = tg * 4 + sl
                for nh in range(2):
                    mm = kb.emit("tensor", lambda e, c=c, tt=tt, buf=buf, nh=nh, sl=sl: e.matmul(
                        vps(sl, nh),
                        ap_view(cm.XN, [[DC * T, 128], [1, 128]], off=c * T + tt * 128),
                        ap_view(buf, [[DC * 128, 128], [1, 512]], off=nh * 512),
                        start=(c == 0), stop=(c == DC - 1)),
                        waits=(wtok[(tg, c)], ln_last, psA_free, psB_free,
                               vt_toks.get(tg * 4 - 1)),
                        sig="te" if (nh == 1 and sl == 3) else None)
            buf_free[c % 2] = mm
        for sl in range(4):
            tt = tg * 4 + sl
            vt_toks[tt] = kb.emit("vector", lambda e, tt=tt, sl=sl: e.tensor_add(
                ap_view(VT, [[16 * VW, 128], [65, 16], [1, 64]], off=tt * VW),
                (ap_view(cm.psA, [[T, 128], [64, 16], [1, 64]], off=(sl % 2) * 1024) if sl < 2
                 else ap_view(cm.psB, [[T, 128], [64, 16], [1, 64]], off=(sl - 2) * 1024)),
                ap_view(BVB, [[1024, 128], [64, 16], [1, 64]])),
                waits=(mm, bvb_tok, on_tok, on2_tok), sig="ve")
    if DEBUG_L3A:
        dbg_vt = kb.dram_out("dbg_vt", [128, 2 * VW], F32)
        vt_toks[15] = kb.dma("gpsimd", dbg_vt, ap_view(VT, [[16 * VW, 128], [1, 2 * VW]]),
                             waits=(vt_toks[15],), sig="ve")
    vt_done = vt_toks[15]

    res_toks = {}
    stg_dma = {}
    last_norm = None
    group_guard = None
    if L3A_LIMIT < 2:
        st = emit_store_x(kb, cm, x_out, waits=(vt_done,))
        kb.wait_all_dma()
        return kb
    for g in range(4):
        def proj_rope(Wd, bia_off, OUT, name, entry_guard=None):
            toks = {}
            wts = []
            for par in range(2):
                col0 = g * 256 + par * 128
                buf = WB[par]
                wts.append(kb.dma("gpsimd",
                           ap_view(buf, [[DC * 128, 128], [128, DC], [1, 128]]),
                           ap_view(Wd, [[1024, 128], [128 * 1024, DC], [1, 128]], off=col0),
                           waits=(buf_free[par],)))
            rtoks = [entry_guard]
            for qs in range(NT4):
                for par in range(2):
                    buf = WB[par]
                    mm = None
                    for dc in range(DC):
                        mm = kb.emit("tensor", lambda e, dc=dc, qs=qs, buf=buf, par=par: e.matmul(
                            ap_view(cm.psB, [[T, 128], [1, 512]], off=par * 512),
                            ap_view(buf, [[DC * 128, 128], [1, 128]], off=dc * 128),
                            cm.xnc(dc, qs * 512, 512),
                            start=(dc == 0), stop=(dc == DC - 1)),
                            waits=(wts[par], ln_last, rtoks[-1], vt_done,
                                   group_guard),
                            sig="te" if dc == DC - 1 else None)
                    toks[(par, qs)] = mm
                pe = ap_view(cm.psB, [[T, 128], [1, 512]], off=0)
                po = ap_view(cm.psB, [[T, 128], [1, 512]], off=512)
                cslice = ap_view(CT, [[T, 128], [1, 512]], off=qs * 512)
                sslice = ap_view(ST, [[T, 128], [1, 512]], off=qs * 512)
                beven = ap_view(BIA, [[3 * DC, 128], [1, 1]], off=bia_off + 2 * g)
                bodd = ap_view(BIA, [[3 * DC, 128], [1, 1]], off=bia_off + 2 * g + 1)
                qoat = ap_view(OUT, [[2 * T, 128], [1, 512]], off=T + qs * 512)
                qeat = ap_view(OUT, [[2 * T, 128], [1, 512]], off=qs * 512)
                w0_ = (toks[(0, qs)], toks[(1, qs)], bq_t, bk_t, c1, c2)
                tb = ap_view(cm.BC2, [[T, 128], [1, 512]])
                ta = ap_view(cm.BC1, [[T, 128], [1, 512]])
                kb.emit("vector", lambda e, po=po, bodd=bodd, tb=tb: e.tensor_scalar_add(
                    tb, po, bodd), waits=w0_)
                kb.emit("vector", lambda e, pe=pe, beven=beven, cs=cslice, ta=ta:
                        e.scalar_tensor_tensor(ta, pe, beven, cs, ALU.add, ALU.mult))
                kb.emit("vector", lambda e, pe=pe, beven=beven, ss=sslice:
                        e.scalar_tensor_tensor(v(STG), pe, beven, ss, ALU.add, ALU.mult))
                kb.emit("vector", lambda e, cs=cslice, tb=tb: e.tensor_mul(
                    v(cm.SQ)[:, 0:512], tb, cs))
                kb.emit("vector", lambda e, qoat=qoat: e.tensor_add(
                    qoat, v(STG), v(cm.SQ)[:, 0:512]))
                kb.emit("vector", lambda e, ss=sslice, tb=tb: e.tensor_mul(
                    v(cm.SQ)[:, 0:512], tb, ss))
                tk = kb.emit("vector", lambda e, qeat=qeat, ta=ta: e.tensor_sub(
                    qeat, ta, v(cm.SQ)[:, 0:512]), sig="ve")
                rtoks.append(tk)
            return rtoks[-1]

        q_tok = proj_rope(Wq, 0, QR, "q")
        if DEBUG_L3A and g == 0:
            q_tok = kb.dma("gpsimd", dbg_qr, v(QR), waits=(q_tok,), sig="ve")
        if L3A_LIMIT >= 2.5:
            k_tok = proj_rope(Wk, DC, KR, "k", entry_guard=q_tok)
            if DEBUG_L3A and g == 0:
                k_tok = kb.dma("gpsimd", dbg_kr, v(KR), waits=(k_tok,), sig="ve")
        else:
            k_tok = q_tok
        buf_free[0] = ("te", kb.cnt["te"])
        buf_free[1] = ("te", kb.cnt["te"])
        if L3A_LIMIT < 3:
            group_guard = k_tok
            if L3A_LIMIT < 2.4:
                break
            continue

        for qs in range(NT4):
            av_done = {}
            ex_done = {}
            last_av = None
            for kvt in range(16):
                st_mms = []
                for hl in range(4):
                    b = hl * 32
                    for par in range(2):
                        st_mms.append(kb.emit("tensor",
                            lambda e, hl=hl, par=par, kvt=kvt, qs=qs, b=b: e.matmul(
                                ap_view(cm.psA, [[T, 128], [1, 512]], off=hl * 512),
                                ap_view(KR, [[2 * T, 32], [1, 128]],
                                        off=b * 2 * T + par * T + kvt * 128),
                                ap_view(QR, [[2 * T, 32], [1, 512]],
                                        off=b * 2 * T + par * T + qs * 512),
                                start=(par == 0), stop=(par == 1),
                                tile_position=(b, 0)),
                            waits=(q_tok, k_tok, ex_done.get(kvt - 1), last_norm),
                            sig="te" if (hl == 3 and par == 1) else None))
                ex_done[kvt] = kb.emit("scalar", lambda e: e.activation(
                    v(E4), v(cm.psA), AF.Exp, scale=0.125),
                    waits=(st_mms[-1], av_done.get(kvt - 1)), sig="se")
                if DEBUG_L3A and g == 0 and qs == 0 and kvt == 0:
                    dbg_e4 = kb.dram_out("dbg_e4", [128, 2048], F32)
                    ex_done[kvt] = kb.dma("gpsimd", dbg_e4, v(E4),
                                          waits=(ex_done[kvt],), sig="se")
                for hl in range(4):
                    h = 4 * g + hl
                    last_av = kb.emit("tensor", lambda e, hl=hl, h=h, kvt=kvt: e.matmul(
                        ap_view(cm.psB, [[T, 65], [1, 512]], off=hl * 512),
                        ap_view(VT, [[16 * VW, 128], [1, 65]],
                                off=kvt * VW + h * 65),
                        ap_view(E4, [[2048, 128], [1, 512]], off=hl * 512),
                        start=(kvt == 0), stop=(kvt == 15)),
                        waits=(vt_done, ex_done[kvt],
                               res_toks.get((g, hl, qs - 1)),
                               stg_dma.get((g - 1, hl))),
                        sig="te" if hl == 3 else None)
                av_done[kvt] = last_av
            if DEBUG_L3A and g == 0 and qs == 0:
                dbg_av = kb.dram_out("dbg_av", [128, 2048], F32)
                cpav = kb.emit("vector", lambda e: e.tensor_copy(
                    v(OSTG), ap_view(cm.psB, [[T, 128], [1, 2048]])),
                    waits=(last_av,), sig="ve")
                last_av = kb.dma("gpsimd", dbg_av, v(OSTG), waits=(cpav,), sig="se")
            prev_bc = None
            for hl in range(4):
                cp = kb.emit("vector", lambda e, hl=hl: e.tensor_copy(
                    v(STG)[0:96, :], ap_view(cm.psB, [[T, 96], [1, 512]], off=hl * 512)),
                    waits=(last_av, stg_dma.get((g, qs - 1)), prev_bc), sig="ve")
                rc = kb.emit("vector", lambda e: e.reciprocal(
                    v(STG)[64:65, :], v(STG)[64:65, :]), sig="ve")
                bc = kb.emit("tensor", lambda e: e.matmul(
                    ap_view(cm.psA, [[T, 64], [1, 512]], off=0),
                    v(cm.ONESF)[64:65, 0:64], v(STG)[64:65, :], start=True, stop=True),
                    waits=(rc, ex_done[15], res_toks.get((g, hl - 1, qs))), sig="te")
                prev_bc = bc
                res_toks[(g, hl, qs)] = kb.emit("vector",
                    lambda e, hl=hl: e.tensor_mul(
                        ap_view(OSTG, [[2048, 64], [1, 512]], off=hl * 512),
                        v(STG)[0:64, :],
                        ap_view(cm.psA, [[T, 64], [1, 512]])),
                    waits=(bc,), sig="ve")
            last_norm = res_toks[(g, 3, qs)]
            group_guard = last_norm
            stg_dma[(g, qs)] = kb.dma(
                "sync",
                ap_view(OTd, [[T, 64], [64 * T, 4], [1, 512]],
                        off=(4 * g) * 64 * T + qs * 512),
                ap_view(OSTG, [[2048, 64], [512, 4], [1, 512]]),
                waits=(res_toks[(g, 3, qs)],))

    if L3A_LIMIT < 4:
        st = emit_store_x(kb, cm, x_out, waits=(group_guard if L3A_LIMIT < 3 else last_norm,))
        kb.wait_all_dma()
        return kb
    final_dma = ("q_sync", kb.cnt["q_sync"])
    te_guard = ("te", kb.cnt["te"])
    ve_guard = ("ve", kb.cnt["ve"])
    otb = {}
    res2 = {}
    for ts in range(NT4):
        for oc in range(DC):
            n = ts * DC + oc
            buf = WB[oc % 2]
            wt = kb.dma("gpsimd", ap_view(buf, [[DC * 128, 128], [1, 1024]]),
                        ap_view(W0, [[1024, 128], [1, 1024]], off=oc * 128 * 1024),
                        waits=(buf_free[oc % 2],))
            ot_t = kb.dma("sync",
                          ap_view(E4, [[2048, 128], [1, 512]], off=(oc % 4) * 512),
                          ap_view(OTd, [[T, 128], [1, 512]],
                                  off=(2 * oc) * 64 * T + ts * 512),
                          waits=(final_dma, te_guard, otb.get(n - 4),))
            mmf = None
            for f in range(DC):
                mmf = kb.emit("tensor", lambda e, f=f, oc=oc, ts=ts, buf=buf: e.matmul(
                    (ap_view(cm.psA, [[T, 128], [1, 512]], off=f * 512) if f < 4 else
                     ap_view(cm.psB, [[T, 128], [1, 512]], off=(f - 4) * 512)),
                    ap_view(buf, [[DC * 128, 128], [1, 128]], off=f * 128),
                    ap_view(E4, [[2048, 128], [1, 512]], off=(oc % 4) * 512),
                    start=(oc == 0), stop=(oc == DC - 1)),
                    waits=(wt, ot_t, res2.get((ts - 1, DC - 1)), ve_guard),
                    sig="te" if f == DC - 1 else None)
            buf_free[oc % 2] = mmf
            otb[n] = mmf
        for f in range(DC):
            res2[(ts, f)] = kb.emit("vector", lambda e, f=f, ts=ts: e.scalar_tensor_tensor(
                cm.xc(f, ts * 512, 512),
                (ap_view(cm.psA, [[T, 128], [1, 512]], off=f * 512) if f < 4 else
                 ap_view(cm.psB, [[T, 128], [1, 512]], off=(f - 4) * 512)),
                ap_view(BIA, [[3 * DC, 128], [1, 1]], off=2 * DC + f),
                cm.xc(f, ts * 512, 512), ALU.add, ALU.add),
                waits=(mmf, b0_t), sig="ve")

    return _blk_tail(kb, cm, fz, x_out, res2[(NT4 - 1, DC - 1)])



def _emit_quant_tokenmajor(kb, cm, XT, XD, SCL, Q1, Q2, wait, n=1024, dbg=False):
    nb = n // 32
    sw = v(SCL).ap[0][0]
    seg = lambda i: ap_view(SCL, [[sw, 128], [1, nb]], off=i * nb)
    segb = lambda i: ap_view(SCL, [[sw, 128], [1, nb], [0, 32]], off=i * nb)
    scl = seg(2)
    sclb = segb(2)
    rsc = seg(0)
    rscb = segb(0)
    q1 = ap_view(Q1, [[v(Q1).ap[0][0], 128], [1, n]])
    q13 = ap_view(Q1, [[v(Q1).ap[0][0], 128], [32, nb], [1, 32]])
    q2 = ap_view(Q2, [[v(Q2).ap[0][0], 128], [1, n]])
    xt = ap_view(XT, [[v(XT).ap[0][0], 128], [1, n]])
    xt3 = ap_view(XT, [[v(XT).ap[0][0], 128], [32, nb], [1, 32]])
    xd = ap_view(XD, [[v(XD).ap[0][0], 128], [1, n]])
    xd3 = ap_view(XD, [[v(XD).ap[0][0], 128], [32, nb], [1, 32]])
    rtk = kb.emit("vector", lambda e: e.tensor_reduce(seg(0), xt3, AX.X, ALU.max,
                                                      apply_absolute_value=True),
                  waits=wait, sig="ve")
    kb.emit("vector", lambda e: e.tensor_scalar(seg(2), seg(0), 1.0 / 7.5, 1e-30,
                                                ALU.mult, ALU.add), waits=(rtk,))
    rtk2 = kb.emit("vector", lambda e: e.reciprocal(rsc, scl), sig="ve")
    ytok = kb.emit("vector", lambda e: e.tensor_mul(q13, xt3, rscb), waits=(rtk2,), sig="ve")
    kb.emit("scalar", lambda e: e.activation(q2, q1, AF.Abs), waits=(ytok,), sig="se")
    kb.emit("vector", lambda e: e.tensor_single_scalar(v(cm.SQ)[:, 0:n], q2, 4.0, ALU.is_ge),
            waits=(("se", kb.cnt["se"]),))
    kb.emit("vector", lambda e: e.tensor_single_scalar(q2, q2, 2.0, ALU.is_ge))
    kb.emit("vector", lambda e: e.tensor_scalar(q2, q2, -4.0, 8.0, ALU.mult, ALU.add))
    kb.emit("vector", lambda e: e.scalar_tensor_tensor(q2, v(cm.SQ)[:, 0:n], -2.0, q2,
                                                       ALU.mult, ALU.add))
    if dbg:
        vetok = ("ve", kb.cnt["ve"])
        kb.emit("vector", lambda e: e.engine_nop(), sig="ve")
        d1 = kb.dram_out("dbg_y", [128, n], F32)
        kb.dma("gpsimd", d1, q1, waits=(vetok,))
        d2 = kb.dram_out("dbg_inv", [128, n], F32)
        kb.dma("gpsimd", d2, q2, waits=(vetok,))
        d3 = kb.dram_out("dbg_scl", [128, 32], F32)
        dmat = kb.dma("gpsimd", d3, scl, waits=(vetok,))
        kb.last_dbg = dmat
    w_ = (kb.last_dbg,) if dbg else ()
    kb.emit("vector", lambda e: e.tensor_mul(q1, q1, q2), waits=w_)
    kb.emit("vector", lambda e: e.tensor_single_scalar(q1, q1, MAGIC, ALU.add))
    kb.emit("vector", lambda e: e.tensor_single_scalar(q1, q1, MAGIC, ALU.subtract))
    rtk3 = kb.emit("vector", lambda e: e.reciprocal(q2, q2), sig="ve")
    kb.emit("vector", lambda e: e.tensor_mul(q1, q1, q2), waits=(rtk3,))
    return kb.emit("vector", lambda e: e.tensor_mul(xd3, q13, sclb), sig="ve")


def build_l1f(weights, fz=None):
    kb, cm, x_in, x_out = _blk_head(fz, "l1f", psum_plan="mixed")
    Wt = kb.dram_in("w_w", weights["L1_ffn_weight"])
    bias = kb.dram_in("w_b", np.asarray(weights["L1_ffn_bias"]).reshape(8, 128).T)

    XT = kb.sbuf("XT", [128, 1024], BF16)
    XD = kb.sbuf("XDt", [128, 1024], BF16)
    SCL = kb.sbuf("SCL", [128, 96], F32)
    Q1 = kb.sbuf("Q1", [128, 1024], F32)
    Q2 = kb.sbuf("Q2", [128, 1024], F32)
    WFP = kb.sbuf("WFP", [128, 1024], BF16)
    WDT = kb.sbuf("WDT", [128, DC * 1024], BF16)
    BIA = kb.sbuf("BIA", [128, DC], F32)

    x_tok = emit_load_x(kb, cm, x_in) if fz is None else fz["x_tok"]
    b_tok = kb.dma("sync", v(BIA), bias)
    ln_toks, psA_free, _ = emit_ln(kb, cm, x_tok)
    ln_last = ln_toks[-1]

    wq_toks = {}
    prev_q = None
    tp_ev = {}
    for c in range(DC):
        wl = kb.dma("gpsimd", v(WFP),
                    ap_view(Wt, [[1024, 128], [1, 1024]], off=c * 128 * 1024),
                    waits=(prev_q,))
        qt = _emit_quant_tokenmajor(kb, cm, WFP, XD, SCL, Q1, Q2, (wl, ln_last),
                                    dbg=(DEBUG_L1F and c == 0))
        if DEBUG_L1F and c == 0:
            dq = kb.dram_out("dbg_wq", [128, 1024], F32)
            qt = kb.dma("gpsimd", dq, v(XD), waits=(qt,), sig="ve")
            dq2 = kb.dram_out("dbg_wraw", [128, 1024], F32)
            qt = kb.dma("gpsimd", dq2, v(WFP), waits=(qt,), sig="ve")
        prev_q = qt
        for b in range(DC):
            mmt = kb.emit("tensor", lambda e, c=c, b=b: e.transpose(
                ap_view(cm.psT, [[2048, 128], [1, 128]], off=(b % 2) * 1024),
                ap_view(XD, [[1024, 128], [1, 128]], off=b * 128),
                v(cm.IDENT)),
                waits=(qt, tp_ev.get((c, b - 2)), tp_ev.get((c - 1, b))), sig="te")
            tp_ev[(c, b)] = kb.emit("vector", lambda e, c=c, b=b: e.tensor_copy(
                ap_view(WDT, [[DC * 1024, 128], [1, 128]], off=b * 1024 + c * 128),
                ap_view(cm.psT, [[2048, 128], [1, 128]], off=(b % 2) * 1024)),
                waits=(mmt,), sig="ve")
    wq_done = tp_ev[(DC - 1, DC - 1)]

    if L1F_LIMIT < 2:
        st = emit_store_x(kb, cm, x_out, waits=(wq_done,))
        kb.wait_all_dma()
        return kb
    xq_done = {}
    ev = {}
    nseq = 0
    for tt in range(16):
        prev = xq_done.get(tt - 1, wq_done)
        for b in range(DC):
            mmt = kb.emit("tensor", lambda e, tt=tt, b=b, nseq=nseq: e.transpose(
                ap_view(cm.psT, [[2048, 128], [1, 128]], off=(nseq % 2) * 1024),
                ap_view(cm.XN, [[DC * T, 128], [1, 128]], off=b * T + tt * 128),
                v(cm.IDENT)),
                waits=(ln_last, prev, ev.get(nseq - 2)), sig="te")
            ev[nseq] = kb.emit("vector", lambda e, b=b, nseq=nseq: e.tensor_copy(
                ap_view(XT, [[1024, 128], [1, 128]], off=b * 128),
                ap_view(cm.psT, [[2048, 128], [1, 128]], off=(nseq % 2) * 1024)),
                waits=(mmt,), sig="ve")
            nseq += 1
        if L1F_LIMIT < 2.5:
            xq_done[tt] = ev[nseq - 1]
            if L1F_LIMIT < 2.25 and tt == 0:
                break
            continue
        qt = _emit_quant_tokenmajor(kb, cm, XT, XD, SCL, Q1, Q2,
                                    (("ve", kb.cnt["ve"]),))
        if L1F_LIMIT < 2.8:
            xq_done[tt] = qt
            continue
        last = None
        for b in range(DC):
            mmt = kb.emit("tensor", lambda e, tt=tt, b=b, nseq=nseq: e.transpose(
                ap_view(cm.psT, [[2048, 128], [1, 128]], off=(nseq % 2) * 1024),
                ap_view(XD, [[1024, 128], [1, 128]], off=b * 128),
                v(cm.IDENT)),
                waits=(qt, ev.get(nseq - 2)), sig="te")
            last = kb.emit("vector", lambda e, tt=tt, b=b, nseq=nseq: e.tensor_copy(
                ap_view(cm.XN, [[DC * T, 128], [1, 128]], off=b * T + tt * 128),
                ap_view(cm.psT, [[2048, 128], [1, 128]], off=(nseq % 2) * 1024)),
                waits=(mmt,), sig="ve")
            ev[nseq] = last
            nseq += 1
        xq_done[tt] = last
    xq_all = xq_done[max(xq_done)]

    if L1F_LIMIT < 3:
        st = emit_store_x(kb, cm, x_out, waits=(xq_all,))
        kb.wait_all_dma()
        return kb
    res_tok = None
    con = {}
    for ts in range(NT4):
        for f in range(DC):
            mm = None
            for c in range(DC):
                mm = kb.emit("tensor", lambda e, f=f, c=c, ts=ts: e.matmul(
                    ap_view(cm.psA, [[T, 128], [1, 512]], off=(f % 4) * 512),
                    ap_view(WDT, [[DC * 1024, 128], [1, 128]], off=c * 1024 + f * 128),
                    cm.xnc(c, ts * 512, 512),
                    start=(c == 0), stop=(c == DC - 1)),
                    waits=(xq_all, psA_free, con.get((ts, f - 4)), con.get((ts - 1, f + 4)),),
                    sig="te" if c == DC - 1 else None)
            con[(ts, f)] = kb.emit("vector", lambda e, f=f, ts=ts: e.scalar_tensor_tensor(
                cm.xc(f, ts * 512, 512),
                ap_view(cm.psA, [[T, 128], [1, 512]], off=(f % 4) * 512),
                ap_view(BIA, [[DC, 128], [1, 1]], off=f),
                cm.xc(f, ts * 512, 512), ALU.add, ALU.add),
                waits=(mm, b_tok), sig="ve")
            res_tok = con[(ts, f)]

    return _blk_tail(kb, cm, fz, x_out, res_tok)



def _l1a_tile_runs(s):
    runs = []
    p = 128 * s
    end = 128 * (s + 1)
    while p < end:
        b = p // 46
        b_end = min(end, 46 * (b + 1), 2048)
        ln = b_end - p
        if b % 2 == 0:
            runs.append((p, p - 128 * s, ln, 1))
        else:
            runs.append((92 * b + 45 - p - (ln - 1), p - 128 * s, ln, -1))
        p = b_end
    return runs


def build_l1a(weights, fz=None):
    kb, cm, x_in, x_out = _blk_head(fz, "l1a")
    Wqkv = kb.dram_in("w_qkv", np.asarray(weights["L1_attn_W_qkv"]))
    Wout = kb.dram_in("w_out", np.asarray(weights["L1_attn_W_out"]))
    VONE = kb.dram_in("c_vones", np.ones((64, 16 * 65), np.float32))

    WQb = kb.sbuf("WQb", [128, DC * 1024], BF16)
    WKb = kb.sbuf("WKb", [128, DC * 1024], BF16)
    WVb = kb.sbuf("WVb", [128, DC * 1024], BF16)
    WOb = kb.sbuf("WOb", [128, DC * 1024], BF16)
    QS = kb.sbuf("QS", [128, 1024], BF16)
    KS = kb.sbuf("KS", [128, 512], BF16)
    VW = kb.sbuf("VW", [64, 16 * 65], BF16)
    EST = kb.sbuf("EST", [64, 128], BF16)
    STG = kb.sbuf("STG", [65, 128], F32)
    OT1 = kb.sbuf("OT1", [64, 128], BF16)
    OS = kb.sbuf("OS", [128, DC * 128], BF16)

    x_tok = emit_load_x(kb, cm, x_in) if fz is None else fz["x_tok"]
    wq_t = kb.dma("gpsimd", ap_view(WQb, [[DC * 1024, 128], [1024, DC], [1, 1024]]),
                  ap_view(Wqkv, [[3072, 128], [128 * 3072, DC], [1, 1024]]))
    wk_t = kb.dma("gpsimd", ap_view(WKb, [[DC * 1024, 128], [1024, DC], [1, 1024]]),
                  ap_view(Wqkv, [[3072, 128], [128 * 3072, DC], [1, 1024]], off=1024))
    wv_t = kb.dma("gpsimd", ap_view(WVb, [[DC * 1024, 128], [1024, DC], [1, 1024]]),
                  ap_view(Wqkv, [[3072, 128], [128 * 3072, DC], [1, 1024]], off=2048))
    wo_t = kb.dma("gpsimd", ap_view(WOb, [[DC * 1024, 128], [1024, DC], [1, 1024]]),
                  Wout.rearrange("(c p) n -> p c n", p=128))
    vo_t = kb.dma("gpsimd", v(VW), VONE)

    ln_toks, psA_free, _ = emit_ln(kb, cm, x_tok)
    ln_last = ln_toks[-1]
    perm_tok = None
    for c in range(DC):
        t1 = kb.emit("vector", lambda e, c=c: e.tensor_copy(
            ap_view(cm.SQ, [[T, 128], [46, 22], [1, 46]]),
            ap_view(cm.XN, [[DC * T, 128], [92, 22], [-1, 46]], off=c * T + 91)),
            waits=(ln_last,), sig="ve")
        perm_tok = kb.emit("vector", lambda e, c=c: e.tensor_copy(
            ap_view(cm.XN, [[DC * T, 128], [92, 22], [1, 46]], off=c * T + 46),
            ap_view(cm.SQ, [[T, 128], [46, 22], [1, 46]])),
            waits=(t1,), sig="ve")

    st_tile = {}
    av_tile = {}
    vw_evs = {}
    xev_last = {-1: (perm_tok, psA_free)}
    ex_tok = {}
    av_seq = {}
    cp_tok = {}
    bc_tok = {}
    mul_tok = {}
    osd_tok = {}
    op_last = {}
    qs_ev = {}
    for s in range(16):
        mm = None
        for oc in range(DC):
            for c in range(DC):
                mm = kb.emit("tensor", lambda e, oc=oc, c=c, s=s: e.matmul(
                    ap_view(cm.psA, [[T, 128], [1, 128]], off=oc * 128),
                    ap_view(WQb, [[DC * 1024, 128], [1, 128]], off=c * 1024 + oc * 128),
                    ap_view(cm.XN, [[DC * T, 128], [1, 128]], off=c * T + s * 128),
                    start=(c == 0), stop=(c == DC - 1)),
                    waits=(wq_t, perm_tok) + tuple(xev_last[s - 1]) if c == 0 and oc == 0
                    else (wq_t,),
                    sig="te" if c == DC - 1 else None)
        q_mm = mm
        for oc in range(DC):
            for c in range(DC):
                mm = kb.emit("tensor", lambda e, oc=oc, c=c, s=s: e.matmul(
                    ap_view(cm.psA, [[T, 128], [1, 64]], off=1024 + oc * 64),
                    ap_view(WKb, [[DC * 1024, 128], [1, 128]], off=c * 1024 + oc * 128),
                    ap_view(cm.XN, [[DC * T, 128], [2, 64]], off=c * T + s * 128),
                    start=(c == 0), stop=(c == DC - 1)),
                    waits=(wk_t,), sig="te" if c == DC - 1 else None)
        k_mm = mm
        for half in range(2):
            for c in range(DC):
                mm = kb.emit("tensor", lambda e, half=half, c=c, s=s: e.matmul(
                    ap_view(cm.psB, [[T, 64], [1, 512]], off=half * 512),
                    ap_view(cm.XN, [[DC * T, 128], [2, 64]], off=c * T + s * 128),
                    ap_view(WVb, [[DC * 1024, 128], [1, 512]], off=c * 1024 + half * 512),
                    start=(c == 0), stop=(c == DC - 1)),
                    waits=(wv_t, vw_evs.get(s - 1)) if c == 0 else (wv_t,),
                    sig="te" if c == DC - 1 else None)
        v_mm = mm
        qs_ev[s] = kb.emit("vector", lambda e: e.tensor_copy(
            v(QS), ap_view(cm.psA, [[T, 128], [1, 1024]])),
            waits=(q_mm, st_tile.get(s - 1)), sig="ve")
        ks_ev = kb.emit("vector", lambda e: e.tensor_copy(
            v(KS), ap_view(cm.psA, [[T, 128], [1, 512]], off=1024)),
            waits=(k_mm,), sig="ve")
        vw_ev = kb.emit("vector", lambda e: e.tensor_copy(
            ap_view(VW, [[16 * 65, 64], [65, 16], [1, 64]]),
            ap_view(cm.psB, [[T, 64], [64, 16], [1, 64]])),
            waits=(v_mm, vo_t, av_tile.get(s - 1)), sig="ve")
        vw_evs[s] = vw_ev
        for h in range(16):
            hs = s * 16 + h
            pi = h % 2
            stm = kb.emit("tensor", lambda e, h=h, pi=pi: e.matmul(
                ap_view(cm.psB, [[T, 64], [1, 128]], off=1024 + pi * 512),
                ap_view(KS, [[512, 64], [1, 64]], off=(h % 2) * 64 * 512 + (h // 2) * 64),
                ap_view(QS, [[1024, 64], [1, 128]], off=(h % 2) * 64 * 1024 + (h // 2) * 128),
                start=True, stop=True),
                waits=(qs_ev[s], ks_ev, ex_tok.get(hs - 2), cp_tok.get(hs - 2)), sig="te")
            st_tile[s] = stm
            ex_tok[hs] = kb.emit("scalar", lambda e, pi=pi: e.activation(
                v(EST), ap_view(cm.psB, [[T, 64], [1, 128]], off=1024 + pi * 512),
                AF.Exp, scale=0.125),
                waits=(stm, av_seq.get(hs - 1)), sig="se")
            avm = kb.emit("tensor", lambda e, h=h, pi=pi: e.matmul(
                ap_view(cm.psB, [[T, 65], [1, 128]], off=1152 + pi * 512),
                ap_view(VW, [[16 * 65, 64], [1, 65]], off=h * 65),
                v(EST), start=True, stop=True),
                waits=(ex_tok[hs], vw_ev, cp_tok.get(hs - 2)), sig="te")
            av_tile[s] = avm
            av_seq[hs] = avm
            cp_tok[hs] = kb.emit("vector", lambda e, pi=pi: e.tensor_copy(
                v(STG), ap_view(cm.psB, [[T, 65], [1, 128]], off=1152 + pi * 512)),
                waits=(avm, bc_tok.get(hs - 1)), sig="ve")
            rc = kb.emit("vector", lambda e: e.reciprocal(
                v(STG)[64:65, :], v(STG)[64:65, :]), sig="ve")
            bc_tok[hs] = kb.emit("tensor", lambda e, pi=pi: e.matmul(
                ap_view(cm.psA, [[T, 64], [1, 128]], off=1536 + pi * 128),
                v(cm.ONESF)[64:65, 0:64], v(STG)[64:65, :], start=True, stop=True),
                waits=(rc, mul_tok.get(hs - 1), cm.const_tok), sig="te")
            mul_tok[hs] = kb.emit("vector", lambda e, pi=pi: e.tensor_mul(
                v(OT1), v(STG)[0:64, :],
                ap_view(cm.psA, [[T, 64], [1, 128]], off=1536 + pi * 128)),
                waits=(bc_tok[hs], osd_tok.get(hs - 1)), sig="ve")
            osd_tok[hs] = kb.dma("scalar",
                                 ap_view(OS, [[DC * 128, 64], [1, 128]],
                                         off=(h % 2) * 64 * DC * 128 + (h // 2) * 128),
                                 v(OT1), waits=(mul_tok[hs], op_last.get(s - 1)))
        opm = {}
        for oco in range(DC):
            mm = None
            for c2 in range(DC):
                mm = kb.emit("tensor", lambda e, oco=oco, c2=c2: e.matmul(
                    ap_view(cm.psA, [[T, 128], [1, 128]], off=oco * 128),
                    ap_view(WOb, [[DC * 1024, 128], [1, 128]], off=c2 * 1024 + oco * 128),
                    ap_view(OS, [[DC * 128, 128], [1, 128]], off=c2 * 128),
                    start=(c2 == 0), stop=(c2 == DC - 1)),
                    waits=(wo_t, osd_tok[s * 16 + 15], qs_ev[s], ks_ev),
                    sig="te" if c2 == DC - 1 else None)
            opm[oco] = mm
        op_last[s] = opm[DC - 1]
        xev = None
        for oco in range(DC):
            for (ost, toff, ln, dirn) in _l1a_tile_runs(s):
                if dirn > 0:
                    src = ap_view(cm.psA, [[T, 128], [1, ln]], off=oco * 128 + toff)
                else:
                    src = ap_view(cm.psA, [[T, 128], [-1, ln]],
                                  off=oco * 128 + toff + ln - 1)
                xev = kb.emit("vector", lambda e, oco=oco, ost=ost, ln=ln, src=src: e.tensor_add(
                    ap_view(cm.X, [[DC * T, 128], [1, ln]], off=oco * T + ost),
                    src,
                    ap_view(cm.X, [[DC * T, 128], [1, ln]], off=oco * T + ost)),
                    waits=(opm[DC - 1], x_tok), sig="ve")
        xev_last[s] = (xev,)

    return _blk_tail(kb, cm, fz, x_out, xev)



def build_l2a(weights, fz=None):
    kb, cm, x_in, x_out = _blk_head(fz, "l2a")
    Wtk = kb.dram_in("w_tk", np.asarray(weights["L2_attn_w_tk"]))
    Wtv = kb.dram_in("w_tv", np.asarray(weights["L2_attn_w_tv"]))
    Wtr = kb.dram_in("w_tr", np.asarray(weights["L2_attn_w_tr"]))
    Wcr = kb.dram_in("w_cr", np.asarray(weights["L2_attn_w_cr"]))
    Wck = kb.dram_in("w_ck", np.asarray(weights["L2_attn_w_ck"]))
    Wcv = kb.dram_in("w_cv", np.asarray(weights["L2_attn_w_cv"]))
    pcols = []
    for key in ("time_decay", "time_first", "time_decay", "time_first",
                "mix_k", "mix_v", "mix_r", "cmix_k", "cmix_r",
                "ln1_g", "ln1_b", "ln2_g", "ln2_b"):
        pcols.append(np.asarray(weights["L2_attn_" + key]).reshape(8, 128).T)
    PRMd = kb.dram_in("w_prm", np.ascontiguousarray(np.concatenate(pcols, axis=1)))
    KKd = kb.dram_tmp("kk_scratch", [32 * 128, T], BF16)

    PRM = kb.sbuf("PRM", [128, 104], F32)
    MXA = kb.sbuf("MXA", [128, 16384], BF16)
    TMB = kb.sbuf("TMB", [128, 16384], BF16)
    SCA = kb.sbuf("SCA", [128, 513], F32)
    SCB = kb.sbuf("SCB", [128, 513], F32)
    SAb = kb.sbuf("SAb", [128, 513], F32)
    SBb = kb.sbuf("SBb", [128, 513], F32)
    GT2 = kb.sbuf("GT2", [128, 512], F32)
    CAR = kb.sbuf("CAR", [128, 40], F32)
    _WB0 = kb.sbuf("WBa", [128, 1024], BF16)
    WB = [_WB0, _WB0]

    prm = lambda off, n=1: ap_view(PRM, [[104, 128], [1, n]], off=off)
    tmb = lambda off, n=512: ap_view(TMB, [[16384, 128], [1, n]], off=off)
    mxa = lambda off, n=512: ap_view(MXA, [[16384, 128], [1, n]], off=off)
    car = lambda off: ap_view(CAR, [[40, 128], [1, 1]], off=off)
    psl = lambda oc, n=512: (ap_view(cm.psA, [[T, 128], [1, n]], off=(oc % 4) * 512)
                             if oc < 4 else
                             ap_view(cm.psB, [[T, 128], [1, n]], off=(oc % 4) * 512))

    x_tok = emit_load_x(kb, cm, x_in) if fz is None else fz["x_tok"]
    prm_t = kb.dma("sync", v(PRM), PRMd)
    ew_t = kb.emit("scalar", lambda e: e.activation(prm(16, 16), prm(0, 16), AF.Exp),
                   waits=(prm_t,), sig="se")

    xs_t = None
    XSAVE = None
    if fz is not None:
        XSAVE = kb.dram_tmp("xsave", [128, DC * T], F32)
        xs_t = kb.dma("sync", XSAVE, v(cm.X), waits=(x_tok,))
    toks0, psA_f0, _ = emit_ln(kb, cm, x_tok, out_fn=cm.xc, apply_extra=xs_t)
    toks1, psA_f1, _ = emit_ln(kb, cm, toks0[-1],
                               gamma_beta=(prm(72, 8), prm(80, 8)), entry=psA_f0)
    ln1_last = toks1[-1]

    if L2A_LIMIT < 2:
        st = emit_store_x(kb, cm, x_out, waits=(toks0[-1], ln1_last))
        kb.wait_all_dma()
        return kb
    wkv_done = {}
    proj_guard = psA_f1
    ev_toks = {}

    class _WS(dict):
        def __init__(self):
            super().__init__({0: None, 1: None})
            self.last = None

        def __getitem__(self, k):
            return self.last

        def __setitem__(self, k, v):
            self.last = v
    wstream = _WS()
    for tc in range(NT4):
        t0 = tc * 512
        rs_tok = None
        for pi, (mixoff, outbase, wd) in enumerate(
                [(32, 0, Wtk), (40, 4096, Wtv), (48, 8192, Wtr)]):
            bld = None
            for c in range(DC):
                base = c * T + t0
                nb = 511 if tc == 0 else 512
                ob = base + 1 if tc == 0 else base
                oo = 1 if tc == 0 else 0
                s1 = kb.emit("vector", lambda e, ob=ob, nb=nb, oo=oo: e.tensor_sub(
                    ap_view(GT2, [[512, 128], [1, nb]], off=oo),
                    ap_view(cm.XN, [[DC * T, 128], [1, nb]], off=ob),
                    ap_view(cm.XN, [[DC * T, 128], [1, nb]], off=ob - 1)),
                    waits=(ln1_last, ev_toks.get((tc, pi - 1), proj_guard)), sig="ve")
                bld = kb.emit("vector", lambda e, c=c, ob=ob, nb=nb, oo=oo, mixoff=mixoff:
                              e.scalar_tensor_tensor(
                    ap_view(MXA, [[16384, 128], [1, nb]], off=c * 512 + oo),
                    ap_view(GT2, [[512, 128], [1, nb]], off=oo), prm(mixoff + c),
                    ap_view(cm.XN, [[DC * T, 128], [1, nb]], off=ob - 1),
                    ALU.mult, ALU.add), sig="ve")
                if tc == 0:
                    bld = kb.emit("vector", lambda e, c=c, mixoff=mixoff:
                                  e.tensor_scalar_mul(
                        mxa(c * 512, 1),
                        ap_view(cm.XN, [[DC * T, 128], [1, 1]], off=c * T),
                        prm(mixoff + c)), sig="ve")
            for oc in range(DC):
                wt = kb.dma("gpsimd", v(WB[oc % 2]),
                            ap_view(wd, [[1024, 128], [128 * 1024, DC], [1, 128]],
                                    off=oc * 128),
                            waits=(wstream[oc % 2],))
                mm = None
                for c in range(DC):
                    mm = kb.emit("tensor", lambda e, oc=oc, c=c: e.matmul(
                        psl(oc),
                        ap_view(WB[oc % 2], [[1024, 128], [1, 128]], off=c * 128),
                        mxa(c * 512),
                        start=(c == 0), stop=(c == DC - 1)),
                        waits=(wt, bld, proj_guard if (tc == 0 and pi == 0) else None,
                               ev_toks.get((tc, pi - 1)) if c == 0 else None,
                               wkv_done.get(tc - 1) if c == 0 else None),
                        sig="te" if c == DC - 1 else None)
                wstream[oc % 2] = mm
                if pi == 2:
                    rs_tok = kb.emit("scalar", lambda e, oc=oc, outbase=outbase: e.activation(
                        tmb(outbase + oc * 512), psl(oc), AF.Sigmoid),
                        waits=(mm,), sig="se")
                    ev_toks[(tc, pi)] = rs_tok
                else:
                    ev_toks[(tc, pi)] = kb.emit("vector",
                        lambda e, oc=oc, outbase=outbase: e.tensor_copy(
                            tmb(outbase + oc * 512), psl(oc)),
                        waits=(mm,), sig="ve")
        proj_guard = ev_toks[(tc, 2)]
        last = None
        for cc in range(DC):
            kv = tmb(cc * 512)
            vv = tmb(4096 + cc * 512)
            mtk = kb.emit("vector", lambda e, kv=kv: e.tensor_reduce(
                car(32), kv, AX.X, ALU.max),
                waits=(ev_toks[(tc, 0)], ev_toks[(tc, 1)], rs_tok), sig="ve")
            if tc == 0:
                kb.emit("vector", lambda e, cc=cc: e.tensor_copy(car(16 + cc), car(32)))
            else:
                kb.emit("vector", lambda e, cc=cc: e.tensor_copy(car(24 + cc), car(16 + cc)))
                kb.emit("vector", lambda e, cc=cc: e.tensor_tensor(
                    car(16 + cc), car(32), car(16 + cc), ALU.max))
            sca1 = ap_view(SCA, [[513, 128], [1, 512]], off=1)
            scb1 = ap_view(SCB, [[513, 128], [1, 512]], off=1)
            sab0 = ap_view(SAb, [[513, 128], [1, 512]])
            sbb0 = ap_view(SBb, [[513, 128], [1, 512]])
            ngm = kb.emit("vector", lambda e, cc=cc: e.tensor_scalar(
                car(33), car(16 + cc), -1.0, 0.0, ALU.mult, ALU.add), sig="ve")
            ek = kb.emit("scalar", lambda e, kv=kv, cc=cc: e.activation(
                v(GT2), kv, AF.Exp, bias=car(33), scale=1.0),
                waits=(ngm,), sig="se")
            kb.emit("vector", lambda e, cc=cc, scb1=scb1: e.tensor_scalar_mul(
                scb1, v(GT2), prm(16 + cc)), waits=(ek, ew_t))
            rst = None
            if tc == 0:
                kb.emit("vector", lambda e: e.tensor_scalar(
                    ap_view(SCB, [[513, 128], [1, 1]]), car(32), 0.0, 0.0,
                    ALU.mult, ALU.add))
                kb.emit("vector", lambda e: e.tensor_scalar(
                    ap_view(SCA, [[513, 128], [1, 1]]), car(32), 0.0, 0.0,
                    ALU.mult, ALU.add))
            else:
                kb.emit("vector", lambda e, cc=cc: e.tensor_sub(
                    car(34), car(24 + cc), car(16 + cc)))
                rst = kb.emit("scalar", lambda e: e.activation(
                    car(35), car(34), AF.Exp), waits=(("ve", kb.cnt["ve"]),), sig="se")
                kb.emit("vector", lambda e, cc=cc: e.tensor_mul(
                    ap_view(SCB, [[513, 128], [1, 1]]), car(8 + cc), car(35)),
                    waits=(rst,))
            kb.emit("vector", lambda e, cc=cc: e.tensor_tensor_scan(
                v(SBb), ap_view(PRM, [[104, 128], [0, 513]], off=16 + cc), v(SCB),
                0.0, ALU.mult, ALU.add))
            kb.emit("vector", lambda e, cc=cc: e.tensor_copy(
                car(8 + cc), ap_view(SBb, [[513, 128], [1, 1]], off=512)))
            kb.emit("vector", lambda e, cc=cc, sca1=sca1, sbb0=sbb0: e.tensor_scalar_mul(
                sca1, sbb0, prm(16 + cc)))
            kb.emit("vector", lambda e, cc=cc, sca1=sca1: e.scalar_tensor_tensor(
                sca1, v(GT2), prm(16 + cc), sca1, ALU.mult, ALU.add))
            kb.emit("vector", lambda e, sab0=sab0, sbb0=sbb0: e.tensor_tensor(
                sab0, sbb0, v(GT2), ALU.max))
            kb.emit("vector", lambda e, sab0=sab0: e.reciprocal(sab0, sab0))
            kb.emit("vector", lambda e, sca1=sca1, sab0=sab0: e.tensor_mul(
                sca1, sca1, sab0))
            kb.emit("vector", lambda e, sca1=sca1, scb1=scb1: e.tensor_mul(
                scb1, sca1, v(GT2)))
            kb.emit("vector", lambda e, vv=vv, scb1=scb1: e.tensor_mul(
                scb1, scb1, vv))
            if tc == 0:
                kb.emit("vector", lambda e: e.tensor_scalar(
                    ap_view(SCB, [[513, 128], [1, 1]]), car(32), 0.0, 0.0,
                    ALU.mult, ALU.add))
            else:
                kb.emit("vector", lambda e, cc=cc: e.tensor_mul(
                    ap_view(SCB, [[513, 128], [1, 1]]), car(cc), car(35)))
            kb.emit("vector", lambda e: e.tensor_tensor_scan(
                v(SAb), v(SCA), v(SCB), 0.0, ALU.mult, ALU.add))
            kb.emit("vector", lambda e, cc=cc: e.tensor_copy(
                car(cc), ap_view(SAb, [[513, 128], [1, 1]], off=512)))
            kb.emit("vector", lambda e, cc=cc, sca1=sca1: e.tensor_scalar_mul(
                sca1, v(GT2), prm(24 + cc)))
            kb.emit("vector", lambda e, vv=vv, scb1=scb1, sca1=sca1: e.tensor_mul(
                scb1, sca1, vv))
            kb.emit("vector", lambda e, scb1=scb1, sab0=sab0: e.tensor_add(
                scb1, sab0, scb1))
            kb.emit("vector", lambda e, sca1=sca1, sbb0=sbb0: e.tensor_add(
                sca1, sbb0, sca1))
            kb.emit("vector", lambda e, sca1=sca1: e.reciprocal(sca1, sca1))
            kb.emit("vector", lambda e, cc=cc, scb1=scb1, sca1=sca1: e.tensor_mul(
                tmb(12288 + cc * 512), scb1, sca1))
            kb.emit("vector", lambda e, cc=cc: e.tensor_mul(
                v(GT2), tmb(8192 + cc * 512), tmb(12288 + cc * 512)))
            last = kb.emit("vector", lambda e, cc=cc, t0=t0: e.tensor_add(
                cm.xc(cc, t0, 512), v(GT2), cm.xc(cc, t0, 512)),
                waits=(x_tok,), sig="ve")
        wkv_done[tc] = last

    if L2A_LIMIT < 3:
        st = emit_store_x(kb, cm, x_out, waits=(wkv_done[NT4 - 1],))
        kb.wait_all_dma()
        return kb
    toks2, psA_f2, _ = emit_ln(kb, cm, wkv_done[NT4 - 1],
                               gamma_beta=(prm(88, 8), prm(96, 8)), entry=proj_guard)
    ln2_last = toks2[-1]
    ck_bld = None
    for c in range(DC):
        for q in range(4):
            base = c * T + q * 512
            nb = 511 if q == 0 else 512
            ob = base + 1 if q == 0 else base
            oo = 1 if q == 0 else 0
            kb.emit("vector", lambda e, ob=ob, nb=nb, oo=oo: e.tensor_sub(
                ap_view(GT2, [[512, 128], [1, nb]], off=oo),
                ap_view(cm.XN, [[DC * T, 128], [1, nb]], off=ob),
                ap_view(cm.XN, [[DC * T, 128], [1, nb]], off=ob - 1)),
                waits=(ln2_last,), sig="ve")
            ck_bld = kb.emit("vector", lambda e, c=c, q=q, ob=ob, nb=nb, oo=oo:
                             e.scalar_tensor_tensor(
                ap_view(MXA, [[16384, 128], [1, nb]], off=c * 2048 + q * 512 + oo),
                ap_view(GT2, [[512, 128], [1, nb]], off=oo), prm(56 + c),
                ap_view(cm.XN, [[DC * T, 128], [1, nb]], off=ob - 1),
                ALU.mult, ALU.add), sig="ve")
        ck_bld = kb.emit("vector", lambda e, c=c: e.tensor_scalar_mul(
            mxa(c * 2048, 1),
            ap_view(cm.XN, [[DC * T, 128], [1, 1]], off=c * T),
            prm(56 + c)), sig="ve")
    sq_toks = {}
    kk_free = {}
    for kc in range(32):
        wt = kb.dma("gpsimd", v(WB[kc % 2]),
                    ap_view(Wck, [[4096, 128], [128 * 4096, DC], [1, 128]], off=kc * 128),
                    waits=(wstream[kc % 2],))
        ps = cm.psA if kc % 2 == 0 else cm.psB
        mm = None
        for q in range(4):
            for c in range(DC):
                mm = kb.emit("tensor", lambda e, kc=kc, q=q, c=c, ps=ps: e.matmul(
                    ap_view(ps, [[T, 128], [1, 512]], off=q * 512),
                    ap_view(WB[kc % 2], [[1024, 128], [1, 128]], off=c * 128),
                    ap_view(MXA, [[16384, 128], [1, 512]], off=c * 2048 + q * 512),
                    start=(c == 0), stop=(c == DC - 1)),
                    waits=(wt, ck_bld, psA_f2 if kc <= 1 else None,
                           sq_toks.get(kc - 2) if (q == 0 and c == 0) else None),
                    sig="te" if (q == 3 and c == DC - 1) else None)
        wstream[kc % 2] = mm
        bnc = (kc % 2) * 2048
        rl = kb.emit("scalar", lambda e, ps=ps, bnc=bnc: e.activation(
            tmb(bnc, 2048), ap_view(ps, [[T, 128], [1, 2048]]), AF.Relu),
            waits=(mm, kk_free.get(kc - 2)), sig="se")
        sq_toks[kc] = kb.emit("vector", lambda e, bnc=bnc: e.tensor_mul(
            tmb(bnc, 2048), tmb(bnc, 2048), tmb(bnc, 2048)), waits=(rl,), sig="ve")
        kk_free[kc] = kb.dma("sync",
                             ap_view(KKd, [[T, 128], [1, T]], off=kc * 128 * T),
                             tmb(bnc, 2048), waits=(sq_toks[kc],))
    kk_last = kk_free[31]
    cr_bld = None
    for c in range(DC):
        for q in range(4):
            base = c * T + q * 512
            nb = 511 if q == 0 else 512
            ob = base + 1 if q == 0 else base
            oo = 1 if q == 0 else 0
            kb.emit("vector", lambda e, ob=ob, nb=nb, oo=oo: e.tensor_sub(
                ap_view(GT2, [[512, 128], [1, nb]], off=oo),
                ap_view(cm.XN, [[DC * T, 128], [1, nb]], off=ob),
                ap_view(cm.XN, [[DC * T, 128], [1, nb]], off=ob - 1)),
                waits=(sq_toks[31], wstream[0], wstream[1]), sig="ve")
            cr_bld = kb.emit("vector", lambda e, c=c, q=q, ob=ob, nb=nb, oo=oo:
                             e.scalar_tensor_tensor(
                ap_view(MXA, [[16384, 128], [1, nb]], off=c * 2048 + q * 512 + oo),
                ap_view(GT2, [[512, 128], [1, nb]], off=oo), prm(64 + c),
                ap_view(cm.XN, [[DC * T, 128], [1, nb]], off=ob - 1),
                ALU.mult, ALU.add), sig="ve")
        cr_bld = kb.emit("vector", lambda e, c=c: e.tensor_scalar_mul(
            mxa(c * 2048, 1),
            ap_view(cm.XN, [[DC * T, 128], [1, 1]], off=c * T),
            prm(64 + c)), sig="ve")
    crs_toks = {}
    for oc in range(DC):
        wt = kb.dma("gpsimd", v(WB[oc % 2]),
                    ap_view(Wcr, [[1024, 128], [128 * 1024, DC], [1, 128]], off=oc * 128),
                    waits=(wstream[oc % 2],))
        ps = cm.psA if oc % 2 == 0 else cm.psB
        mm = None
        for q in range(4):
            for c in range(DC):
                mm = kb.emit("tensor", lambda e, oc=oc, q=q, c=c, ps=ps: e.matmul(
                    ap_view(ps, [[T, 128], [1, 512]], off=q * 512),
                    ap_view(WB[oc % 2], [[1024, 128], [1, 128]], off=c * 128),
                    ap_view(MXA, [[16384, 128], [1, 512]], off=c * 2048 + q * 512),
                    start=(c == 0), stop=(c == DC - 1)),
                    waits=(wt, cr_bld, kk_last if oc <= 1 else None,
                           crs_toks.get(oc - 2) if (q == 0 and c == 0) else None),
                    sig="te" if (q == 3 and c == DC - 1) else None)
        wstream[oc % 2] = mm
        crs_toks[oc] = kb.emit("scalar", lambda e, oc=oc, ps=ps: e.activation(
            tmb(oc * 2048, 2048), ap_view(ps, [[T, 128], [1, 2048]]), AF.Sigmoid),
            waits=(mm,), sig="se")
    kknt_dma = None
    xcv = None
    ev_last = {}
    for nt in range(NT4):
        kknt_dma = kb.dma("sync", v(MXA),
                          ap_view(KKd, [[T, 128], [128 * T, 32], [1, 512]], off=nt * 512),
                          waits=(crs_toks[DC - 1] if nt == 0 else None,
                                 ev_last.get(nt - 1), kk_last))
        for oc in range(DC):
            mm = None
            for kg in range(4):
                wt = kb.dma("gpsimd", v(WB[kg % 2]),
                            ap_view(Wcv, [[1024, 128], [128 * 1024, 8], [1, 128]],
                                    off=(kg * 8) * 128 * 1024 + oc * 128),
                            waits=(wstream[kg % 2],))
                for kcl in range(8):
                    kc = kg * 8 + kcl
                    mm = kb.emit("tensor", lambda e, oc=oc, kc=kc, kcl=kcl: e.matmul(
                        psl(oc),
                        ap_view(WB[(kc // 8) % 2], [[1024, 128], [1, 128]], off=kcl * 128),
                        ap_view(MXA, [[16384, 128], [1, 512]], off=kc * 512),
                        start=(kc == 0), stop=(kc == 31)),
                        waits=(wt, kknt_dma,
                               ev_last.get(nt - 1) if kc == 0 else None),
                        sig="te" if kcl == 7 else None)
                wstream[kg % 2] = mm
            kb.emit("vector", lambda e, oc=oc, nt=nt: e.tensor_mul(
                v(GT2), tmb(oc * 2048 + nt * 512), psl(oc)),
                waits=(mm, crs_toks[oc]), sig="ve")
            xcv = kb.emit("vector", lambda e, oc=oc, nt=nt: e.tensor_add(
                cm.xc(oc, nt * 512, 512), v(GT2), cm.xc(oc, nt * 512, 512)), sig="ve")
            ev_last[nt] = xcv
    xfin = None
    for c in range(DC):
        for q in range(4):
            src = (ap_view(x_in, [[T, 128], [1, 512]], off=c * 128 * T + q * 512)
                   if fz is None else
                   ap_view(XSAVE, [[DC * T, 128], [1, 512]], off=c * T + q * 512))
            ld = kb.dma("sync",
                        ap_view(SAb if q % 2 == 0 else SBb, [[513, 128], [1, 512]]),
                        src, waits=(xfin,))
            xfin = kb.emit("vector", lambda e, c=c, q=q: e.tensor_add(
                cm.xc(c, q * 512, 512),
                ap_view(SAb if q % 2 == 0 else SBb, [[513, 128], [1, 512]]),
                cm.xc(c, q * 512, 512)),
                waits=(ld, xcv), sig="ve")
    return _blk_tail(kb, cm, fz, x_out, xfin)



def build_l2f(weights, fz=None):
    kb, cm, x_in, x_out = _blk_head(fz, "l2f")
    WQT = kb.dram_in("w_wqt", np.ascontiguousarray(np.asarray(weights["L2_ffn_W_q"]).T))
    keys = np.asarray(weights["L2_ffn_keys"])
    ky = np.zeros((128, 512), np.float32)
    for h in range(4):
        for p in range(2):
            ky[(h % 2) * 64:(h % 2) * 64 + 64,
               (2 * p + h // 2) * 64:(2 * p + h // 2) * 64 + 64] = \
                keys[h, :, p, :].T
    KYd = kb.dram_in("w_ky", ky)
    DET = kb.dram_in("w_det", np.ascontiguousarray(np.asarray(weights["L2_ffn_down_embed"]).T))
    UEd = kb.dram_in("w_ue", np.asarray(weights["L2_ffn_up_embed"]))
    WUPT = kb.dram_in("w_wupt", np.ascontiguousarray(np.asarray(weights["L2_ffn_W_up"]).T))
    WDNT = kb.dram_in("w_wdnt", np.ascontiguousarray(np.asarray(weights["L2_ffn_W_down"]).T))
    ee = np.arange(4096)
    spx = (ee[None, :] // 64 == np.arange(64)[:, None]).astype(np.float32)
    spy = (ee[None, :] % 64 == np.arange(64)[:, None]).astype(np.float32)
    SPXd = kb.dram_in("c_spx", spx)
    SPYd = kb.dram_in("c_spy", spy)
    IDFd = kb.dram_in("c_idf2", np.eye(128, dtype=np.float32))
    Gd = kb.dram_tmp("g_scratch", [32 * 128, T], BF16)
    WDd = kb.dram_tmp("wd_scratch2", [32 * 128, T], BF16)
    DHd = kb.dram_tmp("dh_scratch", [32 * 128, T], BF16)
    RWd = kb.dram_tmp("rw_scratch", [8, T], F32)

    WQS = kb.sbuf("WQS", [128, DC * 512], BF16)
    Q = kb.sbuf("Q", [128, 4 * T], BF16)
    KY = kb.sbuf("KY", [128, 512], BF16)
    SIMS = kb.sbuf("SIMS", [64, 8 * T], BF16)
    GT = kb.sbuf("GT", [128, 264], F32)
    TMRC = kb.sbuf("TMRC", [128, 8], F32)
    IDF = kb.sbuf("IDF2", [128, 128], F32)
    ROWA = kb.sbuf("ROWA", [1, 512], F32)
    ROWB = kb.sbuf("ROWB", [1, 512], F32)
    WS = [kb.sbuf("WSa", [128, 1024], BF16), kb.sbuf("WSb", [128, 1024], BF16)]
    GB = kb.sbuf("GB", [128, T], BF16)
    WDC = kb.sbuf("WDC", [128, T], F32)
    EB = kb.sbuf("EB", [128, T], BF16)
    MB = kb.sbuf("MB", [128, T], BF16)
    PB = kb.sbuf("PB", [128, T], BF16)

    gt = lambda off, dims: ap_view(GT, [[264, 128]] + dims, off=off)
    psa = lambda slot, n=512: ap_view(cm.psA, [[T, 128], [1, n]], off=slot * 512)
    psb = lambda slot, n=512: ap_view(cm.psB, [[T, 128], [1, n]], off=slot * 512)
    psl_es = lambda oc, n=512: (psa(oc % 4, n) if oc < 4 else psb(oc % 4, n))

    x_tok = emit_load_x(kb, cm, x_in) if fz is None else fz["x_tok"]
    wq_t = kb.dma("gpsimd", ap_view(WQS, [[DC * 512, 128], [512, DC], [1, 512]]),
                  WQT.rearrange("(c p) e -> p c e", p=128))
    ky_t = kb.dma("gpsimd", v(KY), KYd)
    idf_t = kb.dma("sync", v(IDF), IDFd)
    ln_toks, psA_free, _ = emit_ln(kb, cm, x_tok)
    ln_last = ln_toks[-1]

    qev = {}
    for qc in range(4):
        ps = psa if qc % 2 == 0 else psb
        mm = None
        for nt in range(NT4):
            for c in range(DC):
                mm = kb.emit("tensor", lambda e, qc=qc, nt=nt, c=c, ps=ps: e.matmul(
                    ps(nt),
                    ap_view(WQS, [[DC * 512, 128], [1, 128]], off=c * 512 + qc * 128),
                    cm.xnc(c, nt * 512, 512),
                    start=(c == 0), stop=(c == DC - 1)),
                    waits=(wq_t, ln_last, psA_free if qc == 0 else None,
                           qev.get(qc - 2) if (nt == 0 and c == 0) else None),
                    sig="te" if c == DC - 1 else None)
        qev[qc] = kb.emit("vector", lambda e, qc=qc, ps=ps: e.tensor_copy(
            ap_view(Q, [[4 * T, 128], [1, T]], off=qc * T),
            ps(0, T)), waits=(mm,), sig="ve")
    sev = {}
    sprev = qev[3]
    for h in range(4):
        for p in range(2):
            qc = 2 * p + h // 2
            base = (h % 2) * 64
            mm = None
            for nt in range(NT4):
                mm = kb.emit("tensor", lambda e, qc=qc, base=base, nt=nt: e.matmul(
                    ap_view(cm.psA, [[T, 64], [1, 512]], off=nt * 512),
                    ap_view(KY, [[512, 64], [1, 64]], off=base * 512 + qc * 64),
                    ap_view(Q, [[4 * T, 64], [1, 512]], off=base * 4 * T + qc * T + nt * 512),
                    start=True, stop=True),
                    waits=(ky_t, sprev) if nt == 0 else (),
                    sig="te" if nt == NT4 - 1 else None)
            sev[(h, p)] = mm
            sprev = kb.emit("vector", lambda e, h=h, p=p: e.tensor_copy(
                ap_view(SIMS, [[8 * T, 64], [1, T]], off=(h * 2 + p) * T),
                ap_view(cm.psA, [[T, 64], [1, T]], off=0)),
                waits=(mm,), sig="ve")
    sims_done = sprev

    rw_dma = {}
    tp_prev = {}
    tpr_prev = None
    cseq = 0
    for tt in range(16):
        for h in range(4):
            for p in range(2):
                slot = cseq % 2
                tp = kb.emit("tensor", lambda e, h=h, p=p, tt=tt, slot=slot: e.matmul(
                    ap_view(cm.psB, [[T, 128], [1, 64]], off=slot * 512),
                    ap_view(SIMS, [[8 * T, 64], [1, 128]], off=(h * 2 + p) * T + tt * 128),
                    ap_view(cm.IDENT, [[128, 64], [1, 64]]),
                    start=True, stop=True),
                    waits=(sims_done, idf_t, tp_prev.get(cseq - 2)), sig="te")
                cpt = kb.emit("vector", lambda e, slot=slot: e.tensor_copy(
                    gt(0, [[1, 64]]), ap_view(cm.psB, [[T, 128], [1, 64]], off=slot * 512)),
                    waits=(tp,), sig="ve")
                tp_prev[cseq] = cpt
                kb.emit("vector", lambda e: e.tensor_copy(gt(64, [[1, 64]]), gt(0, [[1, 64]])),
                        waits=(rw_dma.get(tt - 1) if cseq % 8 == 0 else None,))
                cseq += 1
                x8off = 128 + h * 16 + p * 8
                for r in range(8):
                    kb.emit("vector", lambda e, x8off=x8off, r=r: e.tensor_reduce(
                        gt(x8off + r, [[1, 1]]), gt(64, [[1, 64]]), AX.X, ALU.max))
                    if r < 7:
                        kb.emit("vector", lambda e, x8off=x8off, r=r: e.tensor_tensor(
                            gt(0, [[1, 64]]), gt(64, [[1, 64]]),
                            gt(x8off + r, [[0, 64]]), ALU.is_ge))
                        kb.emit("vector", lambda e: e.scalar_tensor_tensor(
                            gt(64, [[1, 64]]), gt(0, [[1, 64]]), -1e9, gt(64, [[1, 64]]),
                            ALU.mult, ALU.add))
            x8off = 128 + h * 16
            kb.emit("vector", lambda e, x8off=x8off: e.tensor_tensor(
                gt(192, [[8, 8], [1, 8]]),
                gt(x8off, [[1, 8], [0, 8]]),
                gt(x8off + 8, [[0, 8], [1, 8]]), ALU.add))
            kb.emit("vector", lambda e: e.tensor_copy(gt(64, [[1, 64]]), gt(192, [[1, 64]])))
            for r in range(8):
                kb.emit("vector", lambda e, r=r: e.tensor_reduce(
                    gt(256 if r < 7 else 259, [[1, 1]]), gt(64, [[1, 64]]), AX.X, ALU.max))
                if r == 0:
                    kb.emit("vector", lambda e: e.tensor_scalar(
                        gt(257, [[1, 1]]), gt(256, [[1, 1]]), -1.0, 0.0, ALU.mult, ALU.add))
                if r < 7:
                    kb.emit("vector", lambda e: e.tensor_tensor(
                        gt(0, [[1, 64]]), gt(64, [[1, 64]]), gt(256, [[0, 64]]), ALU.is_ge))
                    kb.emit("vector", lambda e: e.scalar_tensor_tensor(
                        gt(64, [[1, 64]]), gt(0, [[1, 64]]), -1e9, gt(64, [[1, 64]]),
                        ALU.mult, ALU.add))
            vtk = kb.emit("vector", lambda e: e.engine_nop(), sig="ve")
            ectk = kb.emit("scalar", lambda e: e.activation(
                gt(64, [[1, 64]]), gt(192, [[1, 64]]), AF.Exp, bias=gt(257, [[1, 1]]),
                scale=1.0), waits=(vtk,), sig="se")
            kb.emit("vector", lambda e: e.tensor_tensor(
                gt(0, [[1, 64]]), gt(192, [[1, 64]]), gt(259, [[0, 64]]), ALU.is_ge),
                waits=(ectk,))
            kb.emit("vector", lambda e: e.tensor_mul(
                gt(0, [[1, 64]]), gt(0, [[1, 64]]), gt(64, [[1, 64]])))
            sdn = kb.emit("vector", lambda e: e.tensor_reduce(
                gt(260, [[1, 1]]), gt(0, [[1, 64]]), AX.X, ALU.add), sig="ve")
            ln = kb.emit("scalar", lambda e: e.activation(
                gt(258, [[1, 1]]), gt(260, [[1, 1]]), AF.Ln), waits=(sdn,), sig="se")
            kb.emit("vector", lambda e, h=h: e.tensor_sub(
                ap_view(TMRC, [[8, 128], [1, 1]], off=h),
                gt(257, [[1, 1]]), gt(258, [[1, 1]])), waits=(ln, tpr_prev))
            kb.emit("vector", lambda e, h=h: e.tensor_scalar(
                ap_view(TMRC, [[8, 128], [1, 1]], off=4 + h),
                gt(259, [[1, 1]]), -1.0, 0.0, ALU.mult, ALU.add))
        ttok = ("ve", kb.cnt["ve"])
        tpr = kb.emit("tensor", lambda e: e.matmul(
            ap_view(cm.psB, [[T, 8], [1, 128]], off=1024),
            v(TMRC), v(IDF), start=True, stop=True),
            waits=(ttok, idf_t, rw_dma.get(tt - 1)), sig="te")
        rev = kb.emit("vector", lambda e: e.tensor_copy(
            ap_view(GT, [[264, 8], [1, 128]], off=64),
            ap_view(cm.psB, [[T, 8], [1, 128]], off=1024)),
            waits=(tpr,), sig="ve")
        rw_dma[tt] = kb.dma("sync", ap_view(RWd, [[T, 8], [1, 128]], off=tt * 128),
                            ap_view(GT, [[264, 8], [1, 128]], off=64), waits=(rev,))
        tpr_prev = tpr
    rw_last = rw_dma[15]

    g_dma = {}
    wprev = {0: None, 1: None}
    ev_prev = None
    for ec in range(32):
        wt = kb.dma("gpsimd", v(WS[ec % 2]),
                    ap_view(DET, [[4096, 128], [128 * 4096, DC], [1, 128]], off=ec * 128),
                    waits=(wprev[ec % 2],))
        mm = None
        for nt in range(NT4):
            for c in range(DC):
                mm = kb.emit("tensor", lambda e, ec=ec, nt=nt, c=c: e.matmul(
                    psa(nt),
                    ap_view(WS[ec % 2], [[1024, 128], [1, 128]], off=c * 128),
                    cm.xnc(c, nt * 512, 512),
                    start=(c == 0), stop=(c == DC - 1)),
                    waits=(wt, sims_done if ec == 0 else None,
                           g_dma.get(ec - 1) if (nt == 0 and c == 0) else None),
                    sig="te" if (nt == NT4 - 1 and c == DC - 1) else None)
        wprev[ec % 2] = mm
        ev = kb.emit("vector", lambda e: e.tensor_copy(v(GB), psa(0, T)),
                     waits=(mm, g_dma.get(ec - 1)), sig="ve")
        g_dma[ec] = kb.dma("gpsimd", ap_view(Gd, [[T, 128], [1, T]], off=ec * 128 * T),
                           v(GB), waits=(ev,))
    g_last = g_dma[31]

    spx_t = {}
    wd_dma = {}
    gload = {}
    for ec in range(32):
        s1 = kb.dma("gpsimd", ap_view(WS[0], [[1024, 64], [1, 128]]),
                    ap_view(SPXd, [[4096, 64], [1, 128]], off=ec * 128),
                    waits=(wprev[0] if ec == 0 else spx_t.get(ec - 1),))
        s2 = kb.dma("gpsimd", ap_view(WS[1], [[1024, 64], [1, 128]]),
                    ap_view(SPYd, [[4096, 64], [1, 128]], off=ec * 128),
                    waits=(wprev[1] if ec == 0 else spx_t.get(ec - 1),))
        gl = kb.dma("sync", v(GB), ap_view(Gd, [[T, 128], [1, T]], off=ec * 128 * T),
                    waits=(g_last, gload.get(ec - 1)))
        acc_tok = None
        for h in range(4):
            mm1 = None
            mm2 = None
            for nt in range(NT4):
                ra = kb.dma("sync", v(ROWA),
                            ap_view(RWd, [[T, 1], [1, 512]], off=h * T + nt * 512),
                            waits=(rw_last, ("te", kb.cnt["te"])))
                rb = kb.dma("sync", v(ROWB),
                            ap_view(RWd, [[T, 1], [1, 512]], off=(4 + h) * T + nt * 512),
                            waits=(rw_last, ("te", kb.cnt["te"])))
                mm1 = kb.emit("tensor", lambda e, ec=ec, h=h, nt=nt: e.matmul(
                    psa(nt),
                    ap_view(WS[0], [[1024, 64], [1, 128]]),
                    ap_view(SIMS, [[8 * T, 64], [1, 512]], off=(h * 2) * T + nt * 512),
                    start=True, stop=False),
                    waits=(s1, sev[(h, 0)], acc_tok if nt == 0 else None,
                           wd_dma.get(ec - 1) if (h == 0 and nt == 0) else None))
                kb.emit("tensor", lambda e, ec=ec, h=h, nt=nt: e.matmul(
                    psa(nt),
                    ap_view(WS[1], [[1024, 64], [1, 128]]),
                    ap_view(SIMS, [[8 * T, 64], [1, 512]], off=(h * 2 + 1) * T + nt * 512),
                    start=False, stop=False), waits=(s2,))
                mm1 = kb.emit("tensor", lambda e, nt=nt: e.matmul(
                    psa(nt),
                    ap_view(cm.ONESF, [[128, 1], [1, 128]]), v(ROWA),
                    start=False, stop=True),
                    waits=(ra, cm.const_tok), sig="te")
                mm2 = kb.emit("tensor", lambda e, ec=ec, h=h, nt=nt: e.matmul(
                    psb(nt),
                    ap_view(WS[0], [[1024, 64], [1, 128]]),
                    ap_view(SIMS, [[8 * T, 64], [1, 512]], off=(h * 2) * T + nt * 512),
                    start=True, stop=False), waits=())
                kb.emit("tensor", lambda e, ec=ec, h=h, nt=nt: e.matmul(
                    psb(nt),
                    ap_view(WS[1], [[1024, 64], [1, 128]]),
                    ap_view(SIMS, [[8 * T, 64], [1, 512]], off=(h * 2 + 1) * T + nt * 512),
                    start=False, stop=False), waits=())
                mm2 = kb.emit("tensor", lambda e, nt=nt: e.matmul(
                    psb(nt),
                    ap_view(cm.ONESF, [[128, 1], [1, 128]]), v(ROWB),
                    start=False, stop=True),
                    waits=(rb,), sig="te")
            exp_t = kb.emit("scalar", lambda e: e.activation(
                v(EB), ap_view(cm.psA, [[T, 128], [1, T]]), AF.Exp),
                waits=(mm1, acc_tok), sig="se")
            msk_t = kb.emit("vector", lambda e: e.tensor_single_scalar(
                v(MB), ap_view(cm.psB, [[T, 128], [1, T]]), 0.0, ALU.is_ge),
                waits=(mm2,), sig="ve")
            kb.emit("vector", lambda e: e.tensor_mul(v(PB), v(GB), v(EB)),
                    waits=(gl, exp_t))
            sil_t = kb.emit("scalar", lambda e: e.activation(
                v(EB), v(PB), AF.Silu), waits=(("ve", kb.cnt["ve"]),), sig="se")
            kb.emit("vector", lambda e: e.tensor_mul(v(PB), v(EB), v(MB)),
                    waits=(sil_t,))
            if h == 0:
                acc_tok = kb.emit("vector", lambda e: e.tensor_copy(v(WDC), v(PB)),
                                  sig="ve")
            else:
                acc_tok = kb.emit("vector", lambda e: e.tensor_add(v(WDC), v(PB), v(WDC)),
                                  sig="ve")
        spx_t[ec] = mm2
        gload[ec] = gl
        wd_dma[ec] = kb.dma("gpsimd", ap_view(WDd, [[T, 128], [1, T]], off=ec * 128 * T),
                            v(WDC), waits=(acc_tok,))
    wd_last = wd_dma[31]

    dh_dma = {}
    for hc in range(32):
        wt = kb.dma("gpsimd", v(WS[hc % 2]),
                    ap_view(WUPT, [[4096, 128], [128 * 4096, DC], [1, 128]], off=hc * 128),
                    waits=(spx_t.get(31) if hc < 2 else None, dh_dma.get(hc - 2),))
        mm = None
        for nt in range(NT4):
            for c in range(DC):
                mm = kb.emit("tensor", lambda e, hc=hc, nt=nt, c=c: e.matmul(
                    psa(nt) if hc % 2 == 0 else psb(nt),
                    ap_view(WS[hc % 2], [[1024, 128], [1, 128]], off=c * 128),
                    cm.xnc(c, nt * 512, 512),
                    start=(c == 0), stop=(c == DC - 1)),
                    waits=(wt, wd_dma.get(31) if hc <= 1 and nt == 0 and c == 0 else None,
                           dh_dma.get(hc - 2) if (nt == 0 and c == 0) else None),
                    sig="te" if (nt == NT4 - 1 and c == DC - 1) else None)
        wprev[hc % 2] = mm
        sl = kb.emit("scalar", lambda e, hc=hc: e.activation(
            v(GB) if hc % 2 == 0 else v(PB),
            ap_view(cm.psA if hc % 2 == 0 else cm.psB, [[T, 128], [1, T]]), AF.Silu),
            waits=(mm, dh_dma.get(hc - 2)), sig="se")
        dh_dma[hc] = kb.dma("sync", ap_view(DHd, [[T, 128], [1, T]], off=hc * 128 * T),
                            v(GB) if hc % 2 == 0 else v(PB), waits=(sl,))
    dh_last = dh_dma[31]

    def wdnt_slot(kc):
        if kc < 16:
            return Q, kc * 512
        if kc < 20:
            return GB, (kc - 16) * 512
        if kc < 24:
            return PB, (kc - 20) * 512
        if kc < 28:
            return EB, (kc - 24) * 512
        return MB, (kc - 28) * 512

    for (src_d, w_d, tag) in ((WDd, UEd, "es"), (DHd, WDNT, "dn")):
        ev_last = {}
        ld_toks = {}
        for nt in range(NT4):
            lds = []
            for tn, base_kc in ((Q, 0), (GB, 16), (PB, 20), (EB, 24), (MB, 28)):
                nkc = 16 if tn is Q else 4
                lds.append(kb.dma(
                    "sync",
                    ap_view(tn, [[v(tn).ap[0][0], 128], [512, nkc], [1, 512]]),
                    ap_view(src_d, [[T, 128], [128 * T, nkc], [1, 512]],
                            off=base_kc * 128 * T + nt * 512),
                    waits=(wd_last, dh_last,
                           ev_last.get(nt - 1),
                           sl if (tag == "dn" and nt == 0) else None)))
            ld_toks[nt] = lds[-1]
            for oc in range(DC):
                mm = None
                for kg in range(4):
                    wt = kb.dma("gpsimd", v(WS[kg % 2]),
                                ap_view(w_d, [[1024, 128], [131072, 8], [1, 128]],
                                        off=(kg * 8) * 131072 + oc * 128),
                                waits=(wprev[kg % 2],))
                    for kcl in range(8):
                        kc = kg * 8 + kcl
                        tn, coff = wdnt_slot(kc)
                        mm = kb.emit("tensor", lambda e, oc=oc, kc=kc, kcl=kcl, kg=kg,
                                     tn=tn, coff=coff, nt=nt: e.matmul(
                            psl_es(oc),
                            ap_view(WS[kg % 2], [[1024, 128], [1, 128]], off=kcl * 128),
                            ap_view(tn, [[v(tn).ap[0][0], 128], [1, 512]], off=coff),
                            start=(kc == 0), stop=(kc == 31)),
                            waits=(wt, lds[0], lds[1], lds[2], lds[3], lds[4],
                                   ev_last.get(nt - 1) if kc == 0 else None),
                            sig="te" if kcl == 7 else None)
                    wprev[kg % 2] = mm
                xcv = kb.emit("vector", lambda e, oc=oc, nt=nt: e.tensor_add(
                    cm.xc(oc, nt * 512, 512), psl_es(oc), cm.xc(oc, nt * 512, 512)),
                    waits=(mm, x_tok), sig="ve")
                ev_last[nt] = xcv
        sl = xcv

    return _blk_tail(kb, cm, fz, x_out, xcv)





def build_l3f(weights, fz=None):
    kb, cm, x_in, x_out = _blk_head(fz, "l3f")
    GWd = kb.dram_in("w_gw", np.ascontiguousarray(np.asarray(weights["L3_ffn_gate_w"]).T))
    GBd = kb.dram_in("w_gb", np.asarray(weights["L3_ffn_gate_bias"]).reshape(1, 64))
    WEG = kb.dram_in("w_weg", np.asarray(weights["L3_ffn_We_gate"]).reshape(64 * 1024, 512))
    WEU = kb.dram_in("w_weu", np.asarray(weights["L3_ffn_We_up"]).reshape(64 * 1024, 512))
    WED = kb.dram_in("w_wed", np.asarray(weights["L3_ffn_We_down"]).reshape(64 * 512, 1024))
    WSG = kb.dram_in("w_wsg", np.asarray(weights["L3_ffn_Ws_gate"]))
    WSU = kb.dram_in("w_wsu", np.asarray(weights["L3_ffn_Ws_up"]))
    WSD = kb.dram_in("w_wsd", np.asarray(weights["L3_ffn_Ws_down"]))
    IDFd = kb.dram_in("c_idf", np.eye(128, dtype=np.float32))
    WDd = kb.dram_tmp("wd_scratch", [64, T], F32)

    GW = kb.sbuf("GW", [128, DC * 64], BF16)
    BR = kb.sbuf("BR", [1, 64], F32)
    BIASB = kb.sbuf("BIASB", [128, 64], F32)
    IDF = kb.sbuf("IDF", [128, 128], F32)
    SCO = kb.sbuf("SCO", [64, T], BF16)
    GT = kb.sbuf("GT", [128, 448], F32)
    WGT = [kb.sbuf("WGT0", [128, DC * 512], BF16), kb.sbuf("WGT1", [128, DC * 512], BF16)]
    WUP = [kb.sbuf("WUP0", [128, DC * 512], BF16), kb.sbuf("WUP1", [128, DC * 512], BF16)]
    WDN = [kb.sbuf("WDN0", [128, 4 * 1024], BF16), kb.sbuf("WDN1", [128, 4 * 1024], BF16)]
    BCW = [kb.sbuf("BCW0", [128, T], BF16), kb.sbuf("BCW1", [128, T], BF16)]
    S1 = kb.sbuf("S1", [128, 4 * 512], BF16)
    HM2 = kb.sbuf("HM2", [128, 4 * 512], BF16)
    WDS = kb.sbuf("WDS", [64, 256], F32)

    gt = lambda off, dims: ap_view(GT, [[448, 128]] + dims, off=off)
    psa = lambda slot, n=512: ap_view(cm.psA, [[T, 128], [1, n]], off=slot * 512)
    psb = lambda slot, n=512: ap_view(cm.psB, [[T, 128], [1, n]], off=slot * 512)

    x_tok = emit_load_x(kb, cm, x_in) if fz is None else fz["x_tok"]
    gw_t = kb.dma("gpsimd", ap_view(GW, [[DC * 64, 128], [64, DC], [1, 64]]),
                  GWd.rearrange("(c p) j -> p c j", p=128))
    br_t = kb.dma("sync", v(BR), GBd)
    idf_t = kb.dma("sync", v(IDF), IDFd)
    ln_toks, psA_free, psB_free = emit_ln(kb, cm, x_tok)
    ln_last = ln_toks[-1]

    bb_mm = kb.emit("tensor", lambda e: e.matmul(
        ap_view(cm.psB, [[T, 128], [1, 64]]),
        ap_view(cm.ONESF, [[128, 1], [1, 128]]), v(BR),
        start=True, stop=True), waits=(br_t, psB_free), sig="te")
    bb_tok = kb.emit("vector", lambda e: e.tensor_copy(
        v(BIASB), ap_view(cm.psB, [[T, 128], [1, 64]])), waits=(bb_mm,), sig="ve")

    sg_tok = {}
    for nt in range(NT4):
        mm = None
        for c in range(DC):
            mm = kb.emit("tensor", lambda e, nt=nt, c=c: e.matmul(
                ap_view(cm.psA, [[T, 64], [1, 512]], off=nt * 512),
                ap_view(GW, [[DC * 64, 128], [1, 64]], off=c * 64),
                cm.xnc(c, nt * 512, 512),
                start=(c == 0), stop=(c == DC - 1)),
                waits=(gw_t, ln_last, psA_free), sig="te" if c == DC - 1 else None)
        sg_tok[nt] = kb.emit("scalar", lambda e, nt=nt: e.activation(
            ap_view(SCO, [[T, 64], [1, 512]], off=nt * 512),
            ap_view(cm.psA, [[T, 64], [1, 512]], off=nt * 512), AF.Sigmoid),
            waits=(mm,), sig="se")

    G3 = [[8, 8], [1, 8]]
    sbcp = {}
    wdd_dma = {}
    wtr = {}
    for tt in range(16):
        slot_f = (tt % 2)
        slot_w = 2 + (tt % 2)
        tp = kb.emit("tensor", lambda e, tt=tt, slot_f=slot_f: e.matmul(
            ap_view(cm.psB, [[T, 128], [1, 64]], off=slot_f * 512),
            ap_view(SCO, [[T, 64], [1, 128]], off=tt * 128),
            ap_view(cm.IDENT, [[128, 64], [1, 64]]),
            start=True, stop=True),
            waits=(idf_t, sg_tok[tt // 4], sbcp.get(tt - 2, bb_tok)), sig="te")
        sbcp[tt] = kb.emit("vector", lambda e, slot_f=slot_f: e.tensor_copy(
            gt(0, [[1, 64]]), psb(slot_f, 64)), waits=(tp,), sig="ve")
        kb.emit("vector", lambda e: e.tensor_add(
            gt(64, [[1, 64]]), gt(0, [[1, 64]]), v(BIASB)), waits=(bb_tok,))
        kb.emit("vector", lambda e: e.tensor_reduce(
            gt(384, [[1, 8]]), gt(64, G3), AX.X, ALU.max))
        kb.emit("vector", lambda e: e.tensor_tensor(
            gt(128, G3), gt(64, G3), gt(384, [[1, 8], [0, 8]]), ALU.is_ge))
        kb.emit("vector", lambda e: e.scalar_tensor_tensor(
            gt(192, G3), gt(128, G3), -1e9, gt(64, G3), ALU.mult, ALU.add))
        kb.emit("vector", lambda e: e.tensor_reduce(
            gt(392, [[1, 8]]), gt(192, G3), AX.X, ALU.max))
        kb.emit("vector", lambda e: e.tensor_add(
            gt(400, [[1, 8]]), gt(384, [[1, 8]]), gt(392, [[1, 8]])))
        kb.emit("vector", lambda e: e.tensor_copy(gt(408, [[1, 8]]), gt(400, [[1, 8]])))
        for r in range(4):
            kb.emit("vector", lambda e, r=r: e.tensor_reduce(
                gt(425 if r == 3 else 424, [[1, 1]]), gt(408, [[1, 8]]), AX.X, ALU.max))
            if r < 3:
                kb.emit("vector", lambda e: e.tensor_tensor(
                    gt(416, [[1, 8]]), gt(408, [[1, 8]]), gt(424, [[0, 8]]), ALU.is_ge))
                kb.emit("vector", lambda e: e.scalar_tensor_tensor(
                    gt(408, [[1, 8]]), gt(416, [[1, 8]]), -1e9, gt(408, [[1, 8]]),
                    ALU.mult, ALU.add))
        kb.emit("vector", lambda e: e.tensor_tensor(
            gt(416, [[1, 8]]), gt(400, [[1, 8]]), gt(425, [[0, 8]]), ALU.is_ge))
        kb.emit("vector", lambda e: e.tensor_tensor(
            gt(192, G3), gt(64, G3), gt(416, [[1, 8], [0, 8]]), ALU.mult))
        kb.emit("vector", lambda e: e.tensor_scalar(
            gt(432, [[1, 8]]), gt(416, [[1, 8]]), 1e9, -1e9, ALU.mult, ALU.add))
        kb.emit("vector", lambda e: e.tensor_tensor(
            gt(128, G3), gt(192, G3), gt(432, [[1, 8], [0, 8]]), ALU.add))
        kb.emit("vector", lambda e: e.tensor_copy(gt(192, [[1, 64]]), gt(128, [[1, 64]])))
        for r in range(8):
            kb.emit("vector", lambda e, r=r: e.tensor_reduce(
                gt(426 if r == 7 else 424, [[1, 1]]), gt(128, [[1, 64]]), AX.X, ALU.max))
            if r < 7:
                kb.emit("vector", lambda e: e.tensor_tensor(
                    gt(256, [[1, 64]]), gt(128, [[1, 64]]), gt(424, [[0, 64]]), ALU.is_ge))
                kb.emit("vector", lambda e: e.scalar_tensor_tensor(
                    gt(128, [[1, 64]]), gt(256, [[1, 64]]), -1e9, gt(128, [[1, 64]]),
                    ALU.mult, ALU.add))
        kb.emit("vector", lambda e: e.tensor_tensor(
            gt(256, [[1, 64]]), gt(192, [[1, 64]]), gt(426, [[0, 64]]), ALU.is_ge))
        kb.emit("vector", lambda e: e.tensor_tensor(
            gt(320, [[1, 64]]), gt(256, [[1, 64]]), gt(0, [[1, 64]]), ALU.mult))
        kb.emit("vector", lambda e: e.tensor_reduce(
            gt(427, [[1, 1]]), gt(320, [[1, 64]]), AX.X, ALU.add))
        kb.emit("vector", lambda e: e.tensor_scalar(
            gt(428, [[1, 1]]), gt(427, [[1, 1]]), 1.0, 1e-20, ALU.mult, ALU.add))
        rtk = kb.emit("vector", lambda e: e.reciprocal(
            gt(429, [[1, 1]]), gt(428, [[1, 1]])), sig="ve")
        kb.emit("vector", lambda e: e.tensor_scalar(
            gt(428, [[1, 1]]), gt(429, [[1, 1]]), 2.5, 0.0, ALU.mult, ALU.add),
            waits=(rtk,))
        wfin = kb.emit("vector", lambda e: e.tensor_tensor(
            gt(256, [[1, 64]]), gt(320, [[1, 64]]), gt(428, [[0, 64]]), ALU.mult),
            sig="ve")
        tp2 = kb.emit("tensor", lambda e, slot_w=slot_w: e.matmul(
            ap_view(cm.psB, [[T, 64], [1, 128]], off=slot_w * 512),
            gt(256, [[1, 64]]), v(IDF),
            start=True, stop=True),
            waits=(wfin, wtr.get(tt - 2)), sig="te")
        wtr[tt] = kb.emit("vector", lambda e, tt=tt, slot_w=slot_w: e.tensor_copy(
            ap_view(WDS, [[256, 64], [1, 128]], off=(tt % 2) * 128),
            ap_view(cm.psB, [[T, 64], [1, 128]], off=slot_w * 512)),
            waits=(tp2, wdd_dma.get(tt - 2)), sig="ve")
        wdd_dma[tt] = kb.dma("sync",
                             ap_view(WDd, [[T, 64], [1, 128]], off=tt * 128),
                             ap_view(WDS, [[256, 64], [1, 128]], off=(tt % 2) * 128),
                             waits=(wtr[tt],))
    wdd_last = wdd_dma[15]
    gate_ve_last = sbcp[15]

    free_g = {0: None, 1: None}
    free_u = {0: None, 1: None}
    free_d = {0: None, 1: None}
    free_b = {0: None, 1: None}
    xe_last = None
    sil_last = None
    dmm_last = None
    hm_prev = None
    for e in range(65):
        buf = e % 2
        shared = (e == 64)
        if shared:
            wg_t = kb.dma("gpsimd", ap_view(WGT[buf], [[DC * 512, 128], [512, DC], [1, 512]]),
                          ap_view(WSG, [[512, 128], [65536, DC], [1, 512]]),
                          waits=(free_g[buf],))
            wu_t = kb.dma("gpsimd", ap_view(WUP[buf], [[DC * 512, 128], [512, DC], [1, 512]]),
                          ap_view(WSU, [[512, 128], [65536, DC], [1, 512]]),
                          waits=(free_u[buf],))
            wd_t = kb.dma("gpsimd", ap_view(WDN[buf], [[4 * 1024, 128], [1024, 4], [1, 1024]]),
                          ap_view(WSD, [[1024, 128], [131072, 4], [1, 1024]]),
                          waits=(free_d[buf],))
            bc_t = None
        else:
            wg_t = kb.dma("gpsimd", ap_view(WGT[buf], [[DC * 512, 128], [512, DC], [1, 512]]),
                          ap_view(WEG, [[512, 128], [65536, DC], [1, 512]], off=e * 524288),
                          waits=(free_g[buf],))
            wu_t = kb.dma("gpsimd", ap_view(WUP[buf], [[DC * 512, 128], [512, DC], [1, 512]]),
                          ap_view(WEU, [[512, 128], [65536, DC], [1, 512]], off=e * 524288),
                          waits=(free_u[buf],))
            wd_t = kb.dma("gpsimd", ap_view(WDN[buf], [[4 * 1024, 128], [1024, 4], [1, 1024]]),
                          ap_view(WED, [[1024, 128], [131072, 4], [1, 1024]], off=e * 524288),
                          waits=(free_d[buf],))
            bc_t = kb.dma("gpsimd", v(BCW[buf]),
                          bass.AP(tensor=WDd.tensor, offset=e * T, ap=[[0, 128], [1, T]]),
                          waits=(wdd_last, free_b[buf]))
        for nt in range(NT4):
            first = (e == 0 and nt == 0)
            g_mm = {}
            u_mm = {}
            for fc in range(4):
                mm = None
                for c in range(DC):
                    mm = kb.emit("tensor", lambda e_, fc=fc, c=c, nt=nt, buf=buf: e_.matmul(
                        psa(fc),
                        ap_view(WGT[buf], [[DC * 512, 128], [1, 128]], off=c * 512 + fc * 128),
                        cm.xnc(c, nt * 512, 512),
                        start=(c == 0), stop=(c == DC - 1)),
                        waits=(wg_t, ln_last, gate_ve_last if first else xe_last),
                        sig="te" if c == DC - 1 else None)
                g_mm[fc] = mm
            for fc in range(4):
                mm = None
                for c in range(DC):
                    mm = kb.emit("tensor", lambda e_, fc=fc, c=c, nt=nt, buf=buf: e_.matmul(
                        psb(fc),
                        ap_view(WUP[buf], [[DC * 512, 128], [1, 128]], off=c * 512 + fc * 128),
                        cm.xnc(c, nt * 512, 512),
                        start=(c == 0), stop=(c == DC - 1)),
                        waits=(wu_t, wdd_last if first else None,),
                        sig="te" if c == DC - 1 else None)
                u_mm[fc] = mm
            hm_toks = []
            for fc in range(4):
                st = kb.emit("scalar", lambda e_, fc=fc: e_.activation(
                    ap_view(S1, [[4 * 512, 128], [1, 512]], off=fc * 512),
                    psa(fc), AF.Silu),
                    waits=(g_mm[fc], hm_prev, dmm_last), sig="se")
                sil_last = st
                if shared:
                    hmt = kb.emit("vector", lambda e_, fc=fc: e_.tensor_tensor(
                        ap_view(HM2, [[4 * 512, 128], [1, 512]], off=fc * 512),
                        ap_view(S1, [[4 * 512, 128], [1, 512]], off=fc * 512),
                        psb(fc), ALU.mult),
                        waits=(u_mm[fc], st, dmm_last), sig="ve")
                else:
                    kb.emit("vector", lambda e_, fc=fc: e_.tensor_tensor(
                        ap_view(HM2, [[4 * 512, 128], [1, 512]], off=fc * 512),
                        ap_view(S1, [[4 * 512, 128], [1, 512]], off=fc * 512),
                        psb(fc), ALU.mult),
                        waits=(u_mm[fc], st, dmm_last), sig="ve")
                    hmt = kb.emit("vector", lambda e_, fc=fc, nt=nt, buf=buf: e_.tensor_tensor(
                        ap_view(S1, [[4 * 512, 128], [1, 512]], off=fc * 512),
                        ap_view(HM2, [[4 * 512, 128], [1, 512]], off=fc * 512),
                        ap_view(BCW[buf], [[T, 128], [1, 512]], off=nt * 512),
                        ALU.mult),
                        waits=(bc_t,), sig="ve")
                hm_toks.append(hmt)
            hm_prev = hm_toks[-1]
            if not shared:
                free_b[buf] = hm_toks[-1]
            rhs_hm = HM2 if shared else S1
            d_mm = {}
            for oc in range(DC):
                mm = None
                for fc in range(4):
                    mm = kb.emit("tensor", lambda e_, oc=oc, fc=fc, buf=buf,
                                 rhs_hm=rhs_hm: e_.matmul(
                        psa(oc) if oc < 4 else psb(oc - 4),
                        ap_view(WDN[buf], [[4 * 1024, 128], [1, 128]], off=fc * 1024 + oc * 128),
                        ap_view(rhs_hm, [[4 * 512, 128], [1, 512]], off=fc * 512),
                        start=(fc == 0), stop=(fc == 3)),
                        waits=(wd_t, hm_toks[-1], sil_last),
                        sig="te" if fc == 3 else None)
                d_mm[oc] = mm
            dmm_last = d_mm[DC - 1]
            for oc in range(DC):
                xe_last = kb.emit("vector", lambda e_, oc=oc, nt=nt: e_.tensor_add(
                    cm.xc(oc, nt * 512, 512),
                    psa(oc) if oc < 4 else psb(oc - 4),
                    cm.xc(oc, nt * 512, 512)),
                    waits=(d_mm[oc],), sig="ve")
            if nt == NT4 - 1:
                free_g[buf] = g_mm[3]
                free_u[buf] = u_mm[3]
                free_d[buf] = d_mm[DC - 1]

    return _blk_tail(kb, cm, fz, x_out, xe_last)


BLOCK_BUILDERS = (build_l0a, build_l0f, build_l1a, build_l1f,
                  build_l2a, build_l2f, build_l3a, build_l3f)


def build_mega(weights):
    kb = KB("mega")
    base = Common(kb, psum_plan=None)
    x_in = kb.nc.dram_tensor("x_in", [D, T], F32, kind="ExternalInput").ap()
    x_out = kb.dram_out("x_out", [D, T], F32)
    tok = emit_load_x(kb, base, x_in)
    for build in BLOCK_BUILDERS:
        tok = build(weights, {"kb": kb, "base": base, "x_tok": tok})
    kb.tag = ""
    emit_store_x(kb, base, x_out, waits=(tok,))
    kb.wait_all_dma()
    return kb


def kernel(**inputs):
    x = np.asarray(inputs["L0_attn_inputs"], dtype=np.float32)
    xs = [np.ascontiguousarray(x[b].T) for b in range(NCORE)]
    kb = build_mega(inputs)
    res = kb.run([{"x_in": xs[b]} for b in range(NCORE)])
    return np.stack([np.ascontiguousarray(res.results[b]["x_out"]).T
                     for b in range(NCORE)], axis=0).astype(np.float32)
